# version 05
# baseline (speedup 1.0000x reference)
import sys
sys.path.insert(0, "/opt/trn_rl_repo")
from collections import defaultdict
from contextlib import ExitStack

import numpy as np
import ml_dtypes

import concourse.bass as bass
import concourse.mybir as mybir
from concourse.bass_utils import run_bass_kernel_spmd

F32 = mybir.dt.float32
BF16 = mybir.dt.bfloat16
AF = mybir.ActivationFunctionType
ALU = mybir.AluOpType
AX = mybir.AxisListType

P = 128
T = 2048
DC = 16
NCORES = 8
EPS = 1e-5

INTER = 4096
CONV_DIM = INTER + 256
PROJ = INTER + CONV_DIM + 64
HEADS = 64
HD = 64
STATE = 128
NCH = 16
AH = 16
AKV = 4
AHD = 128
E = 8
I_MOE = 1408
I_SH = 5632
NEG_BIG = -1e30
DETECT_RACES = True
P3_CHUNKS = NCH
FFN_HALVES = 2
FFN_NEXP = 12
HAMMER = False
P3_ABLATE = ()
HAMMER_ENGINES = ("sync", "tensor", "scalar", "vector", "gpsimd")


def _hi(x):
    x = np.ascontiguousarray(np.asarray(x, dtype=np.float32))
    return np.ascontiguousarray(x.view(np.uint16)[..., 1::2]).view(ml_dtypes.bfloat16)


def _kt(w):
    K, O = w.shape
    assert K % P == 0
    return np.ascontiguousarray(w.reshape(K // P, P, O))


def bankw_list(st, b):
    g = st["bank"][b]
    return [g] if g is not None else []


class Sem:
    __slots__ = ("h", "v")

    def __init__(self, ctx, nc, name):
        self.h = ctx.enter_context(nc.semaphore(name))
        self.v = 0


class Phase:

    ENGINES = ("sync", "tensor", "scalar", "vector", "gpsimd")

    def __init__(self, b, name):
        self.b = b
        self.name = name
        self.steps = defaultdict(list)

    def step(self, engine, waits=(), inc=None, inck=1):
        if HAMMER and engine in HAMMER_ENGINES:
            b = self.b
            waits = list(waits) + [(s, s.v) for s in
                                   (b.sLD, b.sST, b.sPE, b.sDV, b.sAC, b.sGP)]
        seen = self.b.waited[engine]
        fw = []
        for s, v in waits:
            if seen[s] < v:
                fw.append((s, v))
                seen[s] = v
        waits = fw
        tgt = None
        if inc is not None:
            inc.v += inck
            tgt = inc.v

        def deco(fn):
            def wrapped(eng, fn=fn, waits=tuple(waits), inc=inc, inck=inck):
                for s, v in waits:
                    eng.wait_ge(s.h, v)
                last = fn(eng)
                if inc is not None:
                    assert last is not None, f"step in {self.name} returned no inst"
                    last.then_inc(inc.h, inck)
            self.steps[engine].append(wrapped)
            return tgt
        return deco

    def flush(self):
        with self.b.nc.named_scope(self.name):
            with self.b.nc.Block(self.name) as blk:
                for en in self.ENGINES:
                    fns = self.steps[en]
                    if fns:
                        getattr(blk, en)(lambda eng, fns=fns: [f(eng) for f in fns])
        self.steps = defaultdict(list)



def host_constants():
    c = {}
    c["ident_f32"] = np.eye(P, dtype=np.float32)
    c["ident_bf"] = np.eye(P, dtype=np.float32).astype(ml_dtypes.bfloat16)
    c["ones_row"] = np.ones((1, P), dtype=np.float32)
    c["ones_col_bf"] = np.ones((P, 1), dtype=ml_dtypes.bfloat16)
    c["ones_row_bf"] = np.ones((1, P), dtype=ml_dtypes.bfloat16)
    c["zeros64"] = np.zeros((HEADS, P), dtype=np.float32)
    kk = np.arange(P)[:, None]
    qq = np.arange(P)[None, :]
    c["tri_mask"] = np.where(kk > qq, NEG_BIG, 0.0).astype(np.float32)
    c["lt_incl"] = np.triu(np.ones((P, P), dtype=np.float32))
    slopes = 2.0 ** (-8.0 * np.arange(1, AH + 1) / AH)
    bias = np.zeros((P, AH * 16), dtype=np.float32)
    for h in range(AH):
        for jc in range(16):
            bias[:, h * 16 + jc] = slopes[h] * (np.arange(P) + 128.0 * (jc - 12))
    c["alibi"] = bias
    bf = ml_dtypes.bfloat16
    qn = np.zeros((AH, 512), dtype=np.float32)
    for h in range(AH):
        qn[h] = -slopes[h] * np.arange(512)
    c["qneg"] = qn.astype(bf)
    half = AHD // 2
    theta = 1.0 / (10000.0 ** (np.arange(half, dtype=np.float64) / half))
    ang = np.outer(theta, np.arange(T, dtype=np.float64))
    cos2 = np.concatenate([np.cos(ang), np.cos(ang)], 0)
    sin2 = np.concatenate([-np.sin(ang), np.sin(ang)], 0)
    sc = 1.0 / np.sqrt(AHD)
    c["cos_k"] = cos2.astype(np.float32).astype(bf)
    c["sin_k"] = sin2.astype(np.float32).astype(bf)
    c["cos_q"] = (cos2 * sc).astype(np.float32).astype(bf)
    c["sin_q"] = (sin2 * sc).astype(np.float32).astype(bf)
    rsw = np.zeros((P, P), dtype=np.float32)
    rsw[(np.arange(P) + 64) % P, np.arange(P)] = 1.0
    c["rsw"] = rsw.astype(bf)
    return c


def prep_core_inputs(core, inputs, consts):
    m = {}
    x = np.asarray(inputs["L0_attn_a_hidden_states"][core], dtype=np.float32)
    m["x0"] = np.ascontiguousarray(x.T).reshape(DC, P, T)

    m["w_in"] = _hi(_kt(np.ascontiguousarray(np.asarray(inputs["L0_attn_a_in_proj_w"], np.float32).T)))
    m["w_out"] = _hi(_kt(np.ascontiguousarray(np.asarray(inputs["L0_attn_a_out_proj_w"], np.float32).T)))
    m["norm_w"] = np.ascontiguousarray(
        np.asarray(inputs["L0_attn_a_norm_weight"], np.float32).reshape(DC, P).T)
    m["gnw"] = np.ascontiguousarray(
        np.asarray(inputs["L0_attn_a_gate_norm_weight"], np.float32).reshape(32, P).T)
    m["conv_w"] = np.ascontiguousarray(
        np.asarray(inputs["L0_attn_a_conv_w"], np.float32).reshape(34, P, 4)
        .transpose(1, 0, 2).reshape(P, 136))
    m["conv_b"] = np.ascontiguousarray(
        np.asarray(inputs["L0_attn_a_conv_b"], np.float32).reshape(34, P).T)
    m["dt_bias"] = np.ascontiguousarray(np.asarray(inputs["L0_attn_a_dt_bias"], np.float32).reshape(HEADS, 1))
    m["a_log"] = np.ascontiguousarray(np.asarray(inputs["L0_attn_a_A_log"], np.float32).reshape(HEADS, 1))
    m["d_vec"] = np.ascontiguousarray(np.asarray(inputs["L0_attn_a_D"], np.float32).reshape(HEADS, 1))

    m["wq"] = _hi(_kt(np.asarray(inputs["L0_attn_b_Wq"], np.float32)))
    m["wk"] = _hi(_kt(np.asarray(inputs["L0_attn_b_Wk"], np.float32)))
    m["wv"] = _hi(_kt(np.asarray(inputs["L0_attn_b_Wv"], np.float32)))
    m["wo"] = _hi(_kt(np.asarray(inputs["L0_attn_b_Wo"], np.float32)))

    gu = np.asarray(inputs["L0_ffn_gate_up_proj"], np.float32)
    dn = np.asarray(inputs["L0_ffn_down_proj"], np.float32)
    m["w_gu"] = np.ascontiguousarray(np.stack([_hi(_kt(np.ascontiguousarray(gu[e].T))) for e in range(E)]))
    m["w_dn"] = np.ascontiguousarray(np.stack([_hi(_kt(np.ascontiguousarray(dn[e].T))) for e in range(E)]))
    m["w_shg"] = _hi(_kt(np.ascontiguousarray(np.asarray(inputs["L0_ffn_sh_gate_w"], np.float32).T)))
    m["w_shu"] = _hi(_kt(np.ascontiguousarray(np.asarray(inputs["L0_ffn_sh_up_w"], np.float32).T)))
    m["w_shd"] = _hi(_kt(np.ascontiguousarray(np.asarray(inputs["L0_ffn_sh_down_w"], np.float32).T)))
    m["w_rt"] = _hi(_kt(np.ascontiguousarray(np.asarray(inputs["L0_ffn_router_weight"], np.float32).T)))
    m["w_sg"] = _hi(np.ascontiguousarray(
        np.asarray(inputs["L0_ffn_shared_gate_w"], np.float32).reshape(DC, P).T))
    m.update(consts)
    return m



class Builder:
    def __init__(self, dbg=()):
        self.dbg = set(dbg)
        nc = bass.Bass("TRN2", target_bir_lowering=False,
                       detect_race_conditions=DETECT_RACES)
        self.nc = nc
        g = ExitStack()
        self.g = g

        di = lambda n, shp, dt=F32: nc.dram_tensor(n, shp, dt, kind="ExternalInput")
        dint = lambda n, shp, dt=F32: nc.dram_tensor(n, shp, dt)
        self.X0 = di("x0", [DC, P, T])
        self.W_IN = di("w_in", [DC, P, PROJ], BF16)
        self.W_OUT = di("w_out", [32, P, 2048], BF16)
        self.NORM_W = di("norm_w", [P, DC])
        self.GNW = di("gnw", [P, 32])
        self.CONV_W = di("conv_w", [P, 136])
        self.CONV_B = di("conv_b", [P, 34])
        self.DT_BIAS = di("dt_bias", [HEADS, 1])
        self.A_LOG = di("a_log", [HEADS, 1])
        self.D_VEC = di("d_vec", [HEADS, 1])
        self.WQ = di("wq", [DC, P, 2048], BF16)
        self.WK = di("wk", [DC, P, 512], BF16)
        self.WV = di("wv", [DC, P, 512], BF16)
        self.WO = di("wo", [DC, P, 2048], BF16)
        self.W_GU = di("w_gu", [E, DC, P, 2816], BF16)
        self.W_DN = di("w_dn", [E, 11, P, 2048], BF16)
        self.W_SHG = di("w_shg", [DC, P, I_SH], BF16)
        self.W_SHU = di("w_shu", [DC, P, I_SH], BF16)
        self.W_SHD = di("w_shd", [44, P, 2048], BF16)
        self.W_RT = di("w_rt", [DC, P, E], BF16)
        self.W_SG = di("w_sg", [P, DC], BF16)
        self.IDENT = di("ident_f32", [P, P])
        self.IDENT_BF = di("ident_bf", [P, P], BF16)
        self.ONES_ROW = di("ones_row", [1, P])
        self.ONES_COL_BF = di("ones_col_bf", [P, 1], BF16)
        self.ONES_ROW_BF = di("ones_row_bf", [1, P], BF16)
        self.ZEROS64 = di("zeros64", [HEADS, P])
        self.TRI_MASK = di("tri_mask", [P, P])
        self.LT_INCL = di("lt_incl", [P, P])
        self.ALIBI = di("alibi", [P, AH * 16])
        self.QNEG = di("qneg", [AH, 512], BF16)
        self.COS_K = di("cos_k", [P, T], BF16)
        self.SIN_K = di("sin_k", [P, T], BF16)
        self.COS_Q = di("cos_q", [P, T], BF16)
        self.SIN_Q = di("sin_q", [P, T], BF16)
        self.RSW = di("rsw", [P, P], BF16)

        self.XR = dint("xr", [DC, P, T])
        self.X1 = dint("x1", [DC, P, T])
        self.X2 = dint("x2", [DC, P, T])
        self.X3 = dint("x3", [DC, P, T])
        self.PROJD = dint("projd", [66, P, T], BF16)
        self.YNORM = dint("ynormd", [32, P, T], BF16)
        self.OUT = nc.dram_tensor("out", [DC, P, T], F32, kind="ExternalOutput")

        sb = lambda n, shp, dt=F32: g.enter_context(nc.sbuf_tensor(n, shp, dt))
        self.c_ident = sb("c_ident", [P, P])
        self.c_ident_bf = sb("c_ident_bf", [P, P], BF16)
        self.c_ones_row = sb("c_ones_row", [1, P])
        self.c_ones_col_bf = sb("c_ones_col_bf", [P, 1], BF16)
        self.c_ones_row_bf = sb("c_ones_row_bf", [1, P], BF16)
        self.c_zeros64 = sb("c_zeros64", [HEADS, P])
        self.c_tri = sb("c_tri", [P, P])
        self.c_lt = sb("c_lt", [P, P])
        self.c_alibi = sb("c_alibi", [P, AH * 16])
        self.c_norm_w = sb("c_norm_w", [P, DC])
        self.c_gnw = sb("c_gnw", [P, 32])
        self.c_conv_w = sb("c_conv_w", [P, 136])
        self.c_conv_b = sb("c_conv_b", [P, 34])
        self.c_dt_bias = sb("c_dt_bias", [HEADS, 1])
        self.c_acol = sb("c_acol", [HEADS, 1])
        self.c_dext = sb("c_dext", [P, 32])
        self.c_abc = sb("c_abc", [P, HEADS])
        self.c_eps = sb("c_eps", [P, 1])
        self.c_one = sb("c_one", [P, 1])

        self.waited = defaultdict(lambda: defaultdict(int))
        self.sLD = Sem(g, nc, "sLD")
        self.sST = Sem(g, nc, "sST")
        self.sPE = Sem(g, nc, "sPE")
        self.sDV = Sem(g, nc, "sDV")
        self.sAC = Sem(g, nc, "sAC")
        self.sGP = Sem(g, nc, "sGP")

        self.final_store_tgt = None

    def phase(self, name):
        return Phase(self, name)

    def drain(self, ph):
        for en, sem in (("sync", self.sLD), ("scalar", self.sST), ("gpsimd", self.sGP)):
            v = sem.v
            ph.steps[en].append(lambda eng, s=sem, v=v: eng.wait_ge(s.h, v))

    def p0_consts(self):
        nc = self.nc
        ph = self.phase("p0")
        tmp_a = self.g.enter_context(nc.sbuf_tensor("tmp_a", [HEADS, 1], F32))
        tmp_d = self.g.enter_context(nc.sbuf_tensor("tmp_d", [HEADS, 1], F32))
        tmp_drow = self.g.enter_context(nc.sbuf_tensor("tmp_drow", [1, HEADS], F32))
        tmp_dbc = self.g.enter_context(nc.sbuf_tensor("tmp_dbc", [P, HEADS], F32))
        p0ctx = ExitStack()
        ps = p0ctx.enter_context(nc.psum_tensor("p0ps", [P, 512], F32))
        loads = [
            (self.c_ident, self.IDENT), (self.c_ident_bf, self.IDENT_BF),
            (self.c_ones_row, self.ONES_ROW), (self.c_ones_col_bf, self.ONES_COL_BF),
            (self.c_ones_row_bf, self.ONES_ROW_BF),
            (self.c_zeros64, self.ZEROS64), (self.c_tri, self.TRI_MASK),
            (self.c_alibi, self.ALIBI), (self.c_norm_w, self.NORM_W),
            (self.c_lt, self.LT_INCL),
            (self.c_gnw, self.GNW), (self.c_conv_w, self.CONV_W),
            (self.c_conv_b, self.CONV_B), (self.c_dt_bias, self.DT_BIAS),
            (tmp_a, self.A_LOG), (tmp_d, self.D_VEC),
        ]
        def do_loads(eng):
            for dst, src in loads:
                eng.dma_start(dst[:], src[:]).then_inc(self.sLD.h, 16)
        ph.steps["sync"].append(do_loads)
        ph.steps["vector"].append(lambda eng: (eng.memset(self.c_eps[:, :], EPS),
                                               eng.memset(self.c_one[:, :], 1.0))[-1])
        self.sLD.v += 16 * len(loads)
        t_ld = self.sLD.v

        t_exp = ph.step("scalar", waits=[(self.sLD, t_ld)], inc=self.sAC)(
            lambda eng: eng.activation(self.c_acol[:], tmp_a[:], AF.Exp))
        t_neg = ph.step("vector", waits=[(self.sAC, t_exp)], inc=self.sDV)(
            lambda eng: eng.tensor_scalar_mul(self.c_acol[:], self.c_acol[:], -1.0))

        t_tp = ph.step("tensor", waits=[(self.sLD, t_ld)], inc=self.sPE)(
            lambda eng: eng.transpose(ps[0:1, 0:HEADS], tmp_d[:],
                                      self.c_ident[0:HEADS, 0:HEADS]))
        t_ev = ph.step("vector", waits=[(self.sPE, t_tp)], inc=self.sDV)(
            lambda eng: eng.tensor_copy(tmp_drow[0:1, 0:HEADS], ps[0:1, 0:HEADS]))
        t_bc = ph.step("tensor", waits=[(self.sDV, t_ev)], inc=self.sPE)(
            lambda eng: eng.matmul(ps[:, 64:64 + HEADS], self.c_ones_row[:],
                                   tmp_drow[0:1, 0:HEADS], start=True, stop=True))

        t_dx0 = ph.step("vector", waits=[(self.sPE, t_bc)], inc=self.sDV)(
            lambda eng: eng.tensor_copy(tmp_dbc[:, :], ps[:, 64:64 + HEADS]))
        t_dx1 = ph.step("scalar", waits=[(self.sDV, t_dx0)], inc=self.sAC)(
            lambda eng: eng.engine_nop() if False else eng.copy(tmp_drow[0:1, 0:HEADS],
                                                               tmp_dbc[0:1, 0:HEADS]))

        def f_dext(eng):
            eng.tensor_copy(self.c_dext[0:64, :], tmp_dbc[0:64, 0:HEADS:2])
            return eng.tensor_copy(self.c_dext[64:128, :], tmp_dbc[64:128, 1:HEADS:2])
        t_dx = ph.step("vector", waits=[(self.sAC, t_dx1)], inc=self.sDV)(f_dext)

        t_at = ph.step("tensor", waits=[(self.sDV, t_neg), (self.sDV, t_dx)],
                       inc=self.sPE)(
            lambda eng: eng.transpose(ps[0:1, 192:192 + HEADS], self.c_acol[:, :],
                                      self.c_ident[0:HEADS, 0:HEADS]))
        t_ar = ph.step("vector", waits=[(self.sPE, t_at)], inc=self.sDV)(
            lambda eng: eng.tensor_copy(tmp_drow[0:1, 0:HEADS], ps[0:1, 192:192 + HEADS]))
        t_ab = ph.step("tensor", waits=[(self.sDV, t_ar)], inc=self.sPE)(
            lambda eng: eng.matmul(ps[:, 256:256 + HEADS], self.c_ones_row[:],
                                   tmp_drow[0:1, 0:HEADS], start=True, stop=True))
        ph.step("vector", waits=[(self.sPE, t_ab)], inc=self.sDV)(
            lambda eng: eng.tensor_copy(self.c_abc[:, :], ps[:, 256:256 + HEADS]))
        ph.flush()
        p0ctx.close()

    def p1_mamba_ln(self, xlnm, outer_ctx):
        nc = self.nc
        ctx = ExitStack()
        sb = lambda n, shp, dt=F32: ctx.enter_context(nc.sbuf_tensor(n, shp, dt))
        pst = lambda n: ctx.enter_context(nc.psum_tensor(n, [P, 512], F32))
        xt = sb("p1_xt", [P, DC * 512])
        xbf = sb("p1_xbf", [P, DC * 512], BF16)
        sqbf = sb("p1_sq", [P, DC * 512], BF16)
        t2 = sb("p1_t2", [P, DC * 512])
        mbc = sb("p1_mbc", [P, 3 * 512])
        rows = sb("p1_rows", [1, 6 * 512])
        ps_sx, ps_sxx, ps_m, ps_r, ps_v2, ps_r2 = (
            pst("p1_sx"), pst("p1_sxx"), pst("p1_m"), pst("p1_r"),
            pst("p1_v2"), pst("p1_r2"))
        m_row = rows[0:1, 0:512]
        v_row = rows[0:1, 512:1024]
        lv_row = rows[0:1, 1024:1536]
        r_row = rows[0:1, 1536:2048]
        r2s = rows[0:1, 2048:2560]

        prev = {}
        ph = self.phase("p1")
        for ti in range(4):
            t0 = ti * 512

            t_ld = ph.step("sync",
                           waits=([(self.sST, prev["xr"])] if "xr" in prev else []),
                           inc=self.sLD, inck=16)(
                lambda eng, t0=t0: eng.dma_start(
                    xt[:, :].rearrange("p (c j) -> p c j", c=DC),
                    self.X0[:, :, t0:t0 + 512].rearrange("c p j -> p c j")))

            def f_cast(eng):
                for c in range(DC):
                    sl = slice(c * 512, (c + 1) * 512)
                    eng.tensor_copy(xbf[:, sl], xt[:, sl])
                for c in range(DC):
                    sl = slice(c * 512, (c + 1) * 512)
                    ins = eng.tensor_mul(sqbf[:, sl], xbf[:, sl], xbf[:, sl])
                return ins
            t_cast = ph.step("vector", waits=[(self.sLD, t_ld)] +
                             ([(self.sPE, prev["sxx"])] if "sxx" in prev else []),
                             inc=self.sDV)(f_cast)

            def f_stats(eng):
                for c in range(DC):
                    sl = slice(c * 512, (c + 1) * 512)
                    eng.matmul(ps_sx[0:1, :], self.c_ones_col_bf[:], xbf[:, sl],
                               start=(c == 0), stop=(c == DC - 1))
                for c in range(DC):
                    sl = slice(c * 512, (c + 1) * 512)
                    ins = eng.matmul(ps_sxx[0:1, :], self.c_ones_col_bf[:], sqbf[:, sl],
                                     start=(c == 0), stop=(c == DC - 1))
                return ins
            t_sxx = ph.step("tensor", waits=[(self.sDV, t_cast)] +
                            ([(self.sDV, prev["rows"])] if "rows" in prev else []),
                            inc=self.sPE)(f_stats)
            prev["sxx"] = t_sxx

            def f_rows(eng):
                eng.tensor_scalar_mul(m_row, ps_sx[0:1, :], 1.0 / 2048)
                eng.tensor_mul(v_row, m_row, m_row)
                return eng.scalar_tensor_tensor(v_row, ps_sxx[0:1, :], 1.0 / 2048,
                                                v_row, ALU.mult, ALU.subtract)
            t_rows = ph.step("vector", waits=[(self.sPE, t_sxx)], inc=self.sDV)(f_rows)
            prev["rows"] = t_rows

            def f_rstd(eng):
                eng.activation(lv_row, v_row, AF.Ln, bias=self.c_eps[0:1, 0:1])
                return eng.activation(r_row, lv_row, AF.Exp, scale=-0.5)
            t_rstd = ph.step("scalar", waits=[(self.sDV, t_rows)], inc=self.sAC)(f_rstd)

            def f_bc(eng):
                eng.matmul(ps_m[:, :], self.c_ones_row[:], m_row, start=True, stop=True)
                return eng.matmul(ps_r[:, :], self.c_ones_row[:], r_row, start=True, stop=True)
            t_bc = ph.step("tensor", waits=[(self.sAC, t_rstd)] +
                           ([(self.sDV, prev["mc"])] if "mc" in prev else []),
                           inc=self.sPE)(f_bc)

            def f_mc(eng):
                eng.tensor_copy(mbc[:, 0:512], ps_m[:, :])
                return eng.tensor_copy(mbc[:, 512:1024], ps_r[:, :])
            t_mc = ph.step("vector", waits=[(self.sPE, t_bc)], inc=self.sDV)(f_mc)
            prev["mc"] = t_mc

            def f_norm(eng):
                for c in range(DC):
                    sl = slice(c * 512, (c + 1) * 512)
                    eng.tensor_sub(t2[:, sl], xt[:, sl], mbc[:, 0:512])
                    eng.tensor_mul(t2[:, sl], t2[:, sl], mbc[:, 512:1024])
                    eng.tensor_add(xt[:, sl], t2[:, sl], xt[:, sl])
                for c in range(DC):
                    sl = slice(c * 512, (c + 1) * 512)
                    ins = eng.tensor_mul(sqbf[:, sl], t2[:, sl], t2[:, sl])
                return ins
            t_norm = ph.step("vector", waits=[], inc=self.sDV)(f_norm)
            prev["norm"] = t_norm

            t_xr = ph.step("scalar", waits=[(self.sDV, t_norm)], inc=self.sST, inck=16)(
                lambda eng, t0=t0: eng.dma_start(
                    self.XR[:, :, t0:t0 + 512].rearrange("c p j -> p c j"),
                    xt[:, :].rearrange("p (c j) -> p c j", c=DC)))
            prev["xr"] = t_xr

            def f_v2(eng):
                for c in range(DC):
                    sl = slice(c * 512, (c + 1) * 512)
                    ins = eng.matmul(ps_v2[0:1, :], self.c_ones_col_bf[:], sqbf[:, sl],
                                     start=(c == 0), stop=(c == DC - 1))
                return ins
            t_v2 = ph.step("tensor", waits=[(self.sDV, t_norm)] +
                           ([(self.sAC, prev["r2"])] if "r2" in prev else []),
                           inc=self.sPE)(f_v2)

            def f_r2(eng):
                eng.activation(lv_row, ps_v2[0:1, :], AF.Ln,
                               bias=self.c_eps[0:1, 0:1], scale=1.0 / 2048)
                return eng.activation(r2s, lv_row, AF.Exp, scale=-0.5)
            t_r2 = ph.step("scalar", waits=[(self.sPE, t_v2)], inc=self.sAC)(f_r2)
            prev["r2"] = t_r2

            t_bc2 = ph.step("tensor", waits=[(self.sAC, t_r2)] +
                            ([(self.sDV, prev["r2c"])] if "r2c" in prev else []),
                            inc=self.sPE)(
                lambda eng: eng.matmul(ps_r2[:, :], self.c_ones_row[:], r2s,
                                       start=True, stop=True))

            t_r2c = ph.step("vector", waits=[(self.sPE, t_bc2)], inc=self.sDV)(
                lambda eng: eng.tensor_copy(mbc[:, 1024:1536], ps_r2[:, :]))
            prev["r2c"] = t_r2c

            def f_fin(eng, t0=t0):
                for c in range(DC):
                    sl = slice(c * 512, (c + 1) * 512)
                    ins = eng.scalar_tensor_tensor(
                        xlnm[:, c * T + t0: c * T + t0 + 512], t2[:, sl],
                        self.c_norm_w[:, c:c + 1], mbc[:, 1024:1536],
                        ALU.mult, ALU.mult)
                return ins
            t_fin = ph.step("vector", waits=[], inc=self.sDV)(f_fin)
            prev["fin"] = t_fin
        self.drain(ph)
        ph.flush()
        ctx.close()
        return prev

    def p2_in_proj(self, xlnm, dt_all, outer_ctx):
        nc = self.nc
        ctx = ExitStack()
        sb = lambda n, shp, dt=F32: ctx.enter_context(nc.sbuf_tensor(n, shp, dt))
        wb = [sb(f"p2_w{i}", [P, DC * 512], BF16) for i in range(2)]
        stg = [sb(f"p2_s{i}", [P, T], BF16) for i in range(2)]
        pss = [ctx.enter_context(nc.psum_tensor(f"p2_ps{i}", [P, 512], F32))
               for i in range(8)]

        ph = self.phase("p2")
        ogs = [(i * 512, 512) for i in range(16)] + [(8192, 320)]
        t_wl = {}
        t_mm_last = {}
        t_ev = {}
        t_st = {}
        oc_global = 0
        for ogi, (o0, ow) in enumerate(ogs):
            waits = []
            if ogi >= 2:
                waits.append((self.sPE, t_mm_last[ogi - 2]))
            t_wl[ogi] = ph.step("sync", waits=waits, inc=self.sLD, inck=16)(
                lambda eng, o0=o0, ow=ow, ogi=ogi: eng.dma_start(
                    wb[ogi % 2][:, 0:DC * ow].rearrange("p (c j) -> p c j", c=DC),
                    self.W_IN[:, :, o0:o0 + ow].rearrange("c p j -> p c j")))

            nfull = ow // 128
            for oci in range(nfull + (1 if ow % 128 else 0)):
                mrows = 64 if (ogi == 16 and oci == 2) else 128
                bank0 = (oc_global % 2) * 4
                mywaits = [(self.sLD, t_wl[ogi])]
                if oc_global >= 2:
                    mywaits.append((self.sDV, t_ev[oc_global - 2]))

                def f_mm2(eng, ogi=ogi, oci=oci, ow=ow, mrows=mrows, bank0=bank0):
                    ins = None
                    for kc in range(DC):
                        base = kc * ow + oci * 128
                        lhsT = wb[ogi % 2][:, base:base + mrows]
                        for tt in range(4):
                            rhs = xlnm[:, kc * T + tt * 512: kc * T + tt * 512 + 512]
                            ins = eng.matmul(pss[bank0 + tt][0:mrows, :], lhsT, rhs,
                                             start=(kc == 0), stop=(kc == DC - 1))
                    return ins
                t_mm = ph.step("tensor", waits=mywaits, inc=self.sPE)(f_mm2)
                t_mm_last[ogi] = t_mm

                evwaits = [(self.sPE, t_mm)]
                if not (ogi == 16 and oci == 2):
                    if oc_global >= 2:
                        evwaits.append((self.sST, t_st[oc_global - 2]))

                    def f_ev(eng, oc_global=oc_global, bank0=bank0):
                        s = stg[oc_global % 2]
                        ins = None
                        for tt in range(4):
                            ins = eng.tensor_copy(s[:, tt * 512:(tt + 1) * 512],
                                                  pss[bank0 + tt][:, :])
                        return ins
                    t_ev[oc_global] = ph.step("vector", waits=evwaits, inc=self.sDV)(f_ev)
                    t_st[oc_global] = ph.step(
                        "scalar", waits=[(self.sDV, t_ev[oc_global])],
                        inc=self.sST, inck=16)(
                        lambda eng, oc=oc_global: eng.dma_start(
                            self.PROJD[oc, :, :], stg[oc % 2][:, :]))
                else:
                    def f_evdt(eng, bank0=bank0):
                        ins = None
                        for tt in range(4):
                            ins = eng.tensor_copy(dt_all[:, tt * 512:(tt + 1) * 512],
                                                  pss[bank0 + tt][0:64, :])
                        return ins
                    t_ev[oc_global] = ph.step("vector", waits=evwaits, inc=self.sDV)(f_evdt)
                oc_global += 1
        self.drain(ph)
        ph.flush()
        ctx.close()
        return {"done_dv": self.sDV.v, "done_st": self.sST.v}

    def p3_ssd(self, dt_all):
        nc = self.nc
        ctx = ExitStack()
        sb = lambda n, shp, dt=F32: ctx.enter_context(nc.sbuf_tensor(n, shp, dt))
        pst = lambda n, dt=F32, w=512: ctx.enter_context(nc.psum_tensor(n, [P, w], dt))

        xbc_in = [sb(f"p3_xin{i}", [P, 34 * 131], BF16) for i in range(2)]
        gate_in = [sb(f"p3_gin{i}", [P, 32 * 128], BF16) for i in range(2)]
        conv_raw = sb("p3_craw", [P, 34 * 128])
        xbc_act = sb("p3_act", [P, 34 * 128])
        sg = sb("p3_sg", [P, 32 * 128], BF16)
        xs_bf = sb("p3_xsbf", [P, 32 * 128], BF16)
        dtxT = sb("p3_dtxT", [P, 32 * 128], BF16)
        y2bf = sb("p3_y2", [P, 32 * 128], BF16)
        S_sb = sb("p3_S", [P, 32 * 128], BF16)
        tp_scr = sb("p3_tps", [P, 128])
        sq_full = sb("p3_sqf", [P, 32 * 128], BF16)
        G_sl = [sb(f"p3_G{i}", [P, 128], BF16) for i in range(4)]
        M_sl = [sb(f"p3_M{i}", [P, 128], BF16) for i in range(4)]
        Bd_sl = [sb(f"p3_Bd{i}", [P, 128], BF16) for i in range(4)]
        Ce_bf = sb("p3_Ce", [P, 512], BF16)
        B_bf = sb("p3_Bbf", [P, 128], BF16)
        C_bf = sb("p3_Cbf", [P, 128], BF16)
        BT_bf = sb("p3_BT", [P, 128], BF16)
        CB_bf = sb("p3_CB", [P, 128], BF16)
        C_dup = sb("p3_Cdup", [P, 512])
        dtw = sb("p3_dtw", [HEADS, 3 * 128])
        cum_hs = sb("p3_cumhs", [HEADS, 128])
        exp_hs = sb("p3_exphs", [HEADS, 128])
        dtT_sb = sb("p3_dtT", [P, 64])
        ldA_T = sb("p3_ldAT", [P, HEADS])
        cum_T = sb("p3_cumT", [P, HEADS])
        ncum_T = sb("p3_ncumT", [P, HEADS])
        dtmp64 = sb("p3_dtmp64", [P, HEADS])
        dcol = sb("p3_dcol", [P, 64])
        e_cols = sb("p3_ecols", [P, 64])
        cl_row = sb("p3_clrow", [1, 64])
        rows = sb("p3_rows", [1, 256])
        cum_rows = sb("p3_cumrows", [1, HEADS * 128])
        ec_rows = sb("p3_ecrows", [1, HEADS * 128], BF16)
        D_sl = [sb(f"p3_D{i}", [P, 128]) for i in range(4)]

        b01 = [pst("p3_b0"), pst("p3_b1")]
        b23 = [pst("p3_b2"), pst("p3_b3")]
        b45 = [pst("p3_b4"), pst("p3_b5")]
        b6 = pst("p3_b6", BF16, 1024)
        b7 = pst("p3_b7")

        z_sl = dtw[:, 0:128]
        sf_sl = dtw[:, 128:256]
        l1_sl = dtw[:, 256:384]
        rstd_row = rows[0:1, 0:128]
        lnv_row = rows[0:1, 128:256]

        ph = self.phase("p3")
        prev = {}

        t_ms = ph.step("vector", inc=self.sDV)(
            lambda eng: (eng.memset(xbc_in[0][:, :], 0.0),
                         eng.memset(xbc_in[1][:, :], 0.0))[-1])

        def emit_loads(ci, extra_waits):
            t0 = ci * 128
            self.sLD.v += 32
            tgt = self.sLD.v

            def f_ld(eng, ci=ci, t0=t0, w=tuple(extra_waits)):
                for s, v in w:
                    eng.wait_ge(s.h, v)
                bi = ci % 2
                if ci == 0:
                    dst = xbc_in[0][:, :].rearrange("p (c j) -> p c j", c=34)[:, :, 3:131]
                    srcx = self.PROJD[32:66, :, 0:128]
                else:
                    dst = xbc_in[bi][:, :].rearrange("p (c j) -> p c j", c=34)
                    srcx = self.PROJD[32:66, :, t0 - 3:t0 + 128]
                eng.dma_start(dst, srcx.rearrange("c p j -> p c j")).then_inc(self.sLD.h, 16)
                eng.dma_start(
                    gate_in[bi][:, :].rearrange("p (c j) -> p c j", c=32),
                    self.PROJD[0:32, :, t0:t0 + 128].rearrange("c p j -> p c j"),
                ).then_inc(self.sLD.h, 16)
            ph.steps["sync"].append(f_ld)
            return tgt

        t_ld = {0: emit_loads(0, [(self.sST, self.sST.v), (self.sDV, t_ms)])}

        for ci in range(P3_CHUNKS):
            t0 = ci * 128
            bi = ci % 2

            def f_conv(eng, bi=bi):
                ins = None
                for j in range(4):
                    for cc in range(34):
                        o = cc * 131
                        oo = cc * 128
                        if j == 0:
                            ins = eng.tensor_scalar(
                                conv_raw[:, oo:oo + 128], xbc_in[bi][:, o:o + 128],
                                self.c_conv_w[:, cc * 4:cc * 4 + 1], None, ALU.mult)
                        else:
                            ins = eng.scalar_tensor_tensor(
                                conv_raw[:, oo:oo + 128],
                                xbc_in[bi][:, o + j:o + j + 128],
                                self.c_conv_w[:, cc * 4 + j:cc * 4 + j + 1],
                                conv_raw[:, oo:oo + 128], ALU.mult, ALU.add)
                return ins
            cw = [(self.sLD, t_ld[ci])]
            if "act_done" in prev:
                cw.append((self.sAC, prev["act_done"]))
            t_conv = ph.step("vector", waits=cw, inc=self.sDV)(f_conv)

            def f_silu(eng, bi=bi):
                ins = None
                for cc in range(34):
                    oo = cc * 128
                    dst = xs_bf if cc < 32 else xbc_act
                    eng.activation(dst[:, oo:oo + 128], conv_raw[:, oo:oo + 128],
                                   AF.Silu, bias=self.c_conv_b[:, cc:cc + 1])
                for cc in range(32):
                    oo = cc * 128
                    ins = eng.activation(sg[:, oo:oo + 128], gate_in[bi][:, oo:oo + 128],
                                         AF.Silu)
                return ins
            sw = [(self.sDV, t_conv)]
            if "ynorm_dv" in prev:
                sw.append((self.sDV, prev["ynorm_dv"]))
            t_silu = ph.step("scalar", waits=sw, inc=self.sAC)(f_silu)
            prev["act_done"] = t_silu

            if ci + 1 < P3_CHUNKS:
                t_ld[ci + 1] = emit_loads(ci + 1, [(self.sDV, t_conv)])

            def f_dt1(eng, t0=t0):
                return eng.tensor_scalar(z_sl, dt_all[:, t0:t0 + 128],
                                         self.c_dt_bias[:, 0:1], None, ALU.add)
            t_dt1 = ph.step("vector", waits=[], inc=self.sDV)(f_dt1)

            def f_dt2(eng):
                eng.activation(sf_sl, z_sl, AF.Abs)
                eng.activation(l1_sl, sf_sl, AF.Exp, scale=-1.0)
                return eng.activation(l1_sl, l1_sl, AF.Ln,
                                      bias=self.c_one[0:HEADS, 0:1])
            t_dt2 = ph.step("scalar", waits=[(self.sDV, t_dt1)],
                            inc=self.sAC)(f_dt2)

            def f_casts(eng):
                o = 32 * 128
                eng.tensor_scalar(sf_sl, z_sl, 0.0, None, ALU.max)
                eng.tensor_copy(B_bf[:, :], xbc_act[:, o:o + 128])
                eng.tensor_copy(C_bf[:, :], xbc_act[:, o + 128:o + 256])
                eng.tensor_add(sf_sl, sf_sl, l1_sl)
                ins = None
                for r in range(4):
                    ins = eng.tensor_copy(C_dup[:, r * 128:(r + 1) * 128],
                                          xbc_act[:, o + 128:o + 256])
                return ins
            t_dt3 = ph.step("vector", waits=[(self.sAC, t_silu), (self.sAC, t_dt2)],
                            inc=self.sDV)(f_casts)
            t_scan = t_dt3

            bw = [(self.sDV, t_scan), (self.sDV, t_dt3),
                  (self.sAC, prev["act_done"])]
            if "rms_act" in prev:
                bw.append((self.sAC, prev["rms_act"]))
            if "ynorm_dv" in prev:
                bw.append((self.sDV, prev["ynorm_dv"]))

            def f_tp1(eng):
                eng.transpose(b7[0:128, 0:64], sf_sl, self.c_ident[0:HEADS, 0:HEADS])
                eng.matmul(b7[:, 256:384], B_bf[:, :], C_bf[:, :],
                           start=True, stop=True)
                ins = None
                for j in range(8):
                    ins = eng.transpose(b6[:, j * 128:(j + 1) * 128],
                                        xs_bf[:, j * 128:(j + 1) * 128],
                                        self.c_ident_bf[:, :])
                return ins
            t_tp1 = ph.step("tensor", waits=bw, inc=self.sPE)(f_tp1)

            def dtx_evacs(eng, r):
                ins = None
                for j in range(8 * (r - 1), 8 * r):
                    for hf in range(2):
                        h = 2 * j + hf
                        ins = eng.tensor_scalar(
                            dtxT[:, j * 128 + hf * 64:j * 128 + hf * 64 + 64],
                            b6[:, (j % 8) * 128 + hf * 64:(j % 8) * 128 + hf * 64 + 64],
                            dtT_sb[:, h:h + 1], None, ALU.mult)
                return ins

            def f_ev1(eng):
                eng.tensor_copy(dtT_sb[:, :], b7[0:128, 0:64])
                eng.tensor_mul(CB_bf[:, :], b7[:, 256:384], self.c_lt[:, :])
                eng.tensor_mul(ldA_T[:, :], dtT_sb[:, :], self.c_abc[:, :])
                return dtx_evacs(eng, 1)
            t_ev1 = ph.step("vector", waits=[(self.sPE, t_tp1)], inc=self.sDV)(f_ev1)

            def f_tp2(eng):
                eng.matmul(b7[:, 64:128], self.c_lt[:, :], ldA_T[:, :],
                           start=True, stop=True)
                ins = None
                for j in range(8, 16):
                    ins = eng.transpose(b6[:, (j % 8) * 128:(j % 8) * 128 + 128],
                                        xs_bf[:, j * 128:(j + 1) * 128],
                                        self.c_ident_bf[:, :])
                return ins
            t_tp2 = ph.step("tensor", waits=[(self.sDV, t_ev1)], inc=self.sPE)(f_tp2)

            def f_ev2(eng):
                eng.tensor_copy(cum_T[:, :], b7[:, 64:128])
                ins = dtx_evacs(eng, 2)
                eng.tensor_scalar_mul(ncum_T[:, :], cum_T[:, :], -1.0)
                return eng.tensor_copy(tp_scr[0:1, 0:1], ncum_T[0:1, 0:1])
            t_ev2 = ph.step("vector", waits=[(self.sPE, t_tp2)], inc=self.sDV)(f_ev2)

            def f_tp3(eng):
                eng.transpose(b7[0:64, 128:256], cum_T[:, :], self.c_ident[:, :])
                ins = None
                for j in range(16, 24):
                    ins = eng.transpose(b6[:, (j % 8) * 128:(j % 8) * 128 + 128],
                                        xs_bf[:, j * 128:(j + 1) * 128],
                                        self.c_ident_bf[:, :])
                return ins
            t_tp3 = ph.step("tensor", waits=[(self.sDV, t_ev2)], inc=self.sPE)(f_tp3)

            evw3 = [(self.sPE, t_tp3)]
            if "rows_gp" in prev:
                evw3.append((self.sGP, prev["rows_gp"]))
            def f_ev3(eng):
                eng.tensor_copy(cum_hs[:, :], b7[0:64, 128:256])
                return dtx_evacs(eng, 3)
            t_ev3 = ph.step("vector", waits=evw3, inc=self.sDV)(f_ev3)

            t_a1 = ph.step("scalar", waits=[(self.sDV, t_ev3)], inc=self.sAC)(
                lambda eng: eng.activation(exp_hs[:, :], cum_hs[:, :], AF.Exp))

            self.sGP.v += 32
            t_rows = self.sGP.v

            def f_rows_w(eng, t_a1=t_a1):
                eng.wait_ge(self.sAC.h, t_a1)
                eng.dma_start(cum_rows[0:1, :], cum_hs[:, :]).then_inc(self.sGP.h, 16)
                eng.dma_start(ec_rows[0:1, :], exp_hs[:, :]).then_inc(self.sGP.h, 16)
            ph.steps["gpsimd"].append(f_rows_w)
            prev["rows_gp"] = t_rows

            def f_tp4(eng):
                eng.transpose(b7[0:1, 64:128], cum_hs[:, 127:128],
                              self.c_ident[0:HEADS, 0:HEADS])
                ins = None
                for j in range(24, 32):
                    ins = eng.transpose(b6[:, (j % 8) * 128:(j % 8) * 128 + 128],
                                        xs_bf[:, j * 128:(j + 1) * 128],
                                        self.c_ident_bf[:, :])
                return ins
            t_tp4 = ph.step("tensor", waits=[(self.sDV, t_ev3)], inc=self.sPE)(f_tp4)

            def f_ev4(eng):
                eng.tensor_copy(cl_row[0:1, :], b7[0:1, 64:64 + HEADS])
                return dtx_evacs(eng, 4)
            t_ev4 = ph.step("vector", waits=[(self.sPE, t_tp4)], inc=self.sDV)(f_ev4)

            tpw = [(self.sDV, t_ev4)]
            if "last_d2" in prev:
                tpw.append((self.sDV, prev["last_d2"]))

            def f_tp5(eng):
                eng.transpose(b6[:, 0:128], B_bf[:, :], self.c_ident_bf[:, :])
                return eng.matmul(b23[0][:, 0:64], self.c_ones_row[:],
                                  cl_row[0:1, :], start=True, stop=True)
            t_tp5 = ph.step("tensor", waits=tpw, inc=self.sPE)(f_tp5)

            def f_ev5(eng):
                eng.tensor_copy(BT_bf[:, :], b6[:, 0:128])
                return eng.tensor_sub(dtmp64[:, :], b23[0][:, 0:64], cum_T[:, :])
            t_ev5 = ph.step("vector", waits=[(self.sPE, t_tp5)], inc=self.sDV)(f_ev5)

            def f_a2(eng):
                eng.activation(dcol[:, :], dtmp64[:, :], AF.Exp)
                return eng.activation(e_cols[:, :], b23[0][:, 0:64], AF.Exp)
            t_a2 = ph.step("scalar", waits=[(self.sDV, t_ev5)], inc=self.sAC)(f_a2)

            t_scan_q = {}
            t_yevac = {}
            t_smm = {}

            def emit_bc(k):
                wbc = [(self.sGP, t_rows), (self.sDV, t_ev5)]
                if k >= 2:
                    wbc.append((self.sDV, t_scan_q[k - 2]))
                elif "last_scan" in prev:
                    wbc.append((self.sDV, prev["last_scan"]))
                if k == 0:
                    wbc.append((self.sAC, t_a2))
                    if "ynorm_dv" in prev:
                        wbc.append((self.sDV, prev["ynorm_dv"]))

                def f_bc(eng, q=k):
                    eng.matmul(b01[q % 2][:, :], self.c_ones_row[:],
                               cum_rows[0:1, 512 * q:512 * q + 512],
                               start=True, stop=True)
                    return eng.matmul(b23[q % 2][:, :], self.c_ones_row_bf[:],
                                      ec_rows[0:1, 512 * q:512 * q + 512],
                                      start=True, stop=True)
                return ph.step("tensor", waits=wbc, inc=self.sPE)(f_bc)

            t_bcd = {0: emit_bc(0)}
            for q in range(16):
                if q + 1 < 16:
                    t_bcd[q + 1] = emit_bc(q + 1)
                t_bc = t_bcd[q]

                def f_d2(eng, q=q):
                    for k in range(4):
                        h = 4 * q + k
                        eng.tensor_scalar(D_sl[k][:, :],
                                          b01[q % 2][:, k * 128:(k + 1) * 128],
                                          cum_T[:, h:h + 1], None, ALU.min)
                        eng.tensor_scalar(Bd_sl[k][:, :], BT_bf[:, :],
                                          dcol[:, h:h + 1], None, ALU.mult)
                    return eng.tensor_mul(Ce_bf[:, :], C_dup[:, :], b23[q % 2][:, :])
                t_d2 = ph.step("vector", waits=[(self.sPE, t_bc)], inc=self.sDV)(f_d2)
                t_scan_q[q] = t_d2

                def f_g(eng, q=q):
                    ins = None
                    for k in range(4):
                        h = 4 * q + k
                        ins = eng.activation(G_sl[k][:, :], D_sl[k][:, :], AF.Exp,
                                             bias=ncum_T[:, h:h + 1])
                    return ins
                t_g = ph.step("scalar", waits=[(self.sDV, t_d2)], inc=self.sAC)(f_g)

                def f_m(eng, q=q):
                    ins = None
                    for k in range(4):
                        ins = eng.tensor_mul(M_sl[k][:, :], CB_bf[:, :], G_sl[k][:, :])
                    return ins
                t_dveq = ph.step("vector", waits=[(self.sAC, t_g)], inc=self.sDV)(f_m)

                for r in range(2):
                    p = 2 * q + r
                    wmm = [(self.sDV, t_dveq)]
                    if p >= 2:
                        wmm.append((self.sDV, t_smm[p - 2]))
                    elif "last_smm" in prev:
                        wmm.append((self.sDV, prev["last_smm"]))

                    def f_ymm(eng, q=q, r=r, p=p, ci=ci):
                        ins = None
                        if "duck" in P3_ABLATE:
                            eng.matmul(b45[p % 2][0:64, 384:385],
                                       dtxT[:, p * 128:p * 128 + 64],
                                       M_sl[2 * r][:, 0:1], start=True, stop=True)
                        for hf in range(2):
                            k = 2 * r + hf
                            dsl = slice(p * 128 + hf * 64, p * 128 + hf * 64 + 64)
                            ysl = b45[p % 2][0:64, hf * 128:hf * 128 + 128]
                            ins = eng.matmul(ysl, dtxT[:, dsl], M_sl[k][:, :],
                                             start=True, stop=(ci == 0))
                            if ci > 0:
                                ins = eng.matmul(ysl, S_sb[:, dsl],
                                                 Ce_bf[:, k * 128:(k + 1) * 128],
                                                 start=False, stop=True)
                            if "contrib" not in P3_ABLATE:
                                ins = eng.matmul(
                                    b45[p % 2][:, 256 + hf * 64:256 + hf * 64 + 64],
                                    Bd_sl[k][:, :], dtxT[:, dsl],
                                    start=True, stop=True)
                        return ins
                    t_ymmp = ph.step("tensor", waits=wmm, inc=self.sPE)(f_ymm)

                    wye = [(self.sPE, t_ymmp)]
                    if p == 0 and "last_rmm" in prev:
                        wye.append((self.sPE, prev["last_rmm"]))
                    if p == 0 and "yn_store" in prev:
                        wye.append((self.sST, prev["yn_store"]))
                    if "act_evac" in P3_ABLATE:
                        wae = list(wye)
                        if p >= 1:
                            wae.append((self.sDV, t_yevac[p - 1]))

                        def f_aev(eng, p=p):
                            ins = None
                            for hf in range(2):
                                rsl = slice(hf * 64, hf * 64 + 64)
                                ins = eng.copy(tp_scr[rsl, :],
                                               b45[p % 2][0:64, hf * 128:hf * 128 + 128])
                            return ins
                        t_aev = ph.step("scalar", waits=wae, inc=self.sAC)(f_aev)
                        wye = wye + [(self.sAC, t_aev)]

                    def f_yev(eng, p=p):
                        o = slice(p * 128, (p + 1) * 128)
                        if "act_evac" in P3_ABLATE:
                            for hf in range(2):
                                rsl = slice(hf * 64, hf * 64 + 64)
                                eng.scalar_tensor_tensor(
                                    tp_scr[rsl, :], xs_bf[rsl, o],
                                    self.c_dext[rsl, p:p + 1], tp_scr[rsl, :],
                                    ALU.mult, ALU.add)
                        elif "stt_evac" in P3_ABLATE:
                            for hf in range(2):
                                rsl = slice(hf * 64, hf * 64 + 64)
                                eng.tensor_copy(tp_scr[rsl, :],
                                                b45[p % 2][0:64, hf * 128:hf * 128 + 128])
                            for hf in range(2):
                                rsl = slice(hf * 64, hf * 64 + 64)
                                eng.scalar_tensor_tensor(
                                    tp_scr[rsl, :], xs_bf[rsl, o],
                                    self.c_dext[rsl, p:p + 1], tp_scr[rsl, :],
                                    ALU.mult, ALU.add)
                        else:
                            for hf in range(2):
                                rsl = slice(hf * 64, hf * 64 + 64)
                                eng.scalar_tensor_tensor(
                                    tp_scr[rsl, :],
                                    xs_bf[rsl, o],
                                    self.c_dext[rsl, p:p + 1],
                                    b45[p % 2][0:64, hf * 128:hf * 128 + 128],
                                    ALU.mult, ALU.add)
                        for hf in range(2):
                            rsl = slice(hf * 64, hf * 64 + 64)
                            eng.tensor_mul(y2bf[rsl, o], tp_scr[rsl, :], sg[rsl, o])
                        for hf in range(2):
                            rsl = slice(hf * 64, hf * 64 + 64)
                            ins = eng.tensor_mul(sq_full[rsl, o], y2bf[rsl, o],
                                                 y2bf[rsl, o])
                        return ins
                    t_yevac[p] = ph.step("vector", waits=wye, inc=self.sDV)(f_yev)

                    def f_sup(eng, q=q, r=r, p=p, ci=ci):
                        if "contrib" in P3_ABLATE:
                            return eng.engine_nop()
                        ins = None
                        for hf in range(2):
                            h = 4 * q + 2 * r + hf
                            ssl = slice(p * 128 + hf * 64, p * 128 + hf * 64 + 64)
                            csl = b45[p % 2][:, 256 + hf * 64:256 + hf * 64 + 64]
                            if ci == 0:
                                ins = eng.tensor_copy(S_sb[:, ssl], csl)
                            else:
                                ins = eng.scalar_tensor_tensor(
                                    S_sb[:, ssl], S_sb[:, ssl],
                                    e_cols[:, h:h + 1], csl, ALU.mult, ALU.add)
                        return ins
                    t_smm[p] = ph.step("vector", waits=[], inc=self.sDV)(f_sup)

            def f_rmm(eng):
                ins = None
                for p in range(32):
                    ins = eng.matmul(b7[0:1, 384:512], self.c_ones_col_bf[:, :],
                                     sq_full[:, p * 128:(p + 1) * 128],
                                     start=(p == 0), stop=(p == 31))
                return ins
            t_rmm = ph.step("tensor", waits=[(self.sDV, t_yevac[31])],
                            inc=self.sPE)(f_rmm)
            prev["last_rmm"] = t_rmm
            if "qdbg" in self.dbg and ci == 0:
                for nm, ap, shp, dt in [
                    ("dbg_G0", G_sl[0][:, :], [P, 128], BF16),
                    ("dbg_y2bf", y2bf[:, :], [P, 32 * 128], BF16),
                    ("dbg_dtxT", dtxT[:, :], [P, 32 * 128], BF16),
                ]:
                    out_t = self.nc.dram_tensor(nm, shp, dt, kind="ExternalOutput")
                    self.sGP.v += 16
                    tgtv = self.sGP.v

                    def f_dbg(eng, out_t=out_t, ap=ap, tgtv=tgtv, w=t_smm[31]):
                        eng.wait_ge(self.sDV.h, w)
                        eng.dma_start(out_t[:, :], ap).then_inc(self.sGP.h, 16)
                        eng.wait_ge(self.sGP.h, tgtv)
                    ph.steps["gpsimd"].append(f_dbg)

            prev["last_scan"] = t_scan_q[15]
            prev["last_d2"] = t_scan_q[14]
            prev["last_smm"] = t_smm[31]

            if "p3dbg" in self.dbg and ci == 0:
                for nm, ap, shp, dt in [
                    ("dbg_M0", M_sl[0][:, :], [P, 128], BF16),
                    ("dbg_M1", M_sl[1][:, :], [P, 128], BF16),
                    ("dbg_dtxT", dtxT[:, :], [P, 32 * 128], BF16),
                    ("dbg_dArows", dA_rows[0:1, :], [1, HEADS * 128], F32),
                    ("dbg_Yf", Yf[:, :], [HEADS, 128], F32),
                    ("dbg_CB", CB_bf[:, :], [P, 128], BF16),
                    ("dbg_y2bf", y2bf[:, :], [P, 32 * 128], BF16),
                    ("dbg_dcol", dcol[:, :], [P, 64], F32),
                ]:
                    out_t = self.nc.dram_tensor(nm, shp, dt, kind="ExternalOutput")
                    self.sGP.v += 16
                    tgtv = self.sGP.v

                    def f_dbg(eng, out_t=out_t, ap=ap, tgtv=tgtv, w=t_smm[31]):
                        eng.wait_ge(self.sDV.h, w)
                        eng.dma_start(out_t[:, :], ap).then_inc(self.sGP.h, 16)
                        eng.wait_ge(self.sGP.h, tgtv)
                    ph.steps["gpsimd"].append(f_dbg)

            def f_rms(eng):
                eng.activation(lnv_row, b7[0:1, 384:512], AF.Ln,
                               bias=self.c_eps[0:1, 0:1], scale=1.0 / 4096)
                return eng.activation(rstd_row, lnv_row, AF.Exp, scale=-0.5)
            t_rms = ph.step("scalar", waits=[(self.sPE, t_rmm)], inc=self.sAC)(f_rms)
            prev["rms_act"] = t_rms

            if "rmsdbg" in self.dbg and ci == 0:
                sumrow = ctx.enter_context(nc.sbuf_tensor("p3_sumdbg", [1, 128], F32))
                t_sd = ph.step("vector", waits=[(self.sPE, t_rmm), (self.sAC, t_rms)],
                               inc=self.sDV)(
                    lambda eng: eng.tensor_copy(sumrow[0:1, :], b7[0:1, 384:512]))
                out_t = self.nc.dram_tensor("dbg_sumrow", [1, 128], F32, kind="ExternalOutput")
                out_t2 = self.nc.dram_tensor("dbg_rstd", [1, 256], F32, kind="ExternalOutput")
                self.sGP.v += 32
                tgtv = self.sGP.v

                def f_sd(eng, tgtv=tgtv, t_sd=t_sd, t_rms=t_rms):
                    eng.wait_ge(self.sDV.h, t_sd)
                    eng.wait_ge(self.sAC.h, t_rms)
                    eng.dma_start(out_t[:, :], sumrow[0:1, :]).then_inc(self.sGP.h, 16)
                    eng.dma_start(out_t2[:, :], rows[0:1, :]).then_inc(self.sGP.h, 16)
                    eng.wait_ge(self.sGP.h, tgtv)
                ph.steps["gpsimd"].append(f_sd)

            t_rbc = ph.step("tensor", waits=[(self.sAC, t_rms),
                                             (self.sDV, t_scan_q[14])],
                            inc=self.sPE)(
                lambda eng: eng.matmul(b01[0][:, 0:128], self.c_ones_row[:],
                                       rstd_row, start=True, stop=True))

            wyn = [(self.sPE, t_rbc)]
            if "yn_store" in prev:
                wyn.append((self.sST, prev["yn_store"]))

            def f_yn(eng):
                if "skipnorm" in P3_ABLATE:
                    return eng.engine_nop()
                ins = None
                for j in range(32):
                    ins = eng.scalar_tensor_tensor(
                        y2bf[:, j * 128:(j + 1) * 128],
                        y2bf[:, j * 128:(j + 1) * 128],
                        self.c_gnw[:, j:j + 1], b01[0][:, 0:128],
                        ALU.mult, ALU.mult)
                return ins
            t_yn = ph.step("vector", waits=wyn, inc=self.sDV)(f_yn)
            prev["ynorm_dv"] = t_yn

            t_st = ph.step("scalar", waits=[(self.sDV, t_yn)],
                           inc=self.sST, inck=16)(
                lambda eng, t0=t0: eng.dma_start(
                    self.YNORM[:, :, t0:t0 + 128].rearrange("j p t -> p j t"),
                    y2bf[:, :].rearrange("p (j t) -> p j t", j=32)))
            prev["yn_store"] = t_st

        self.drain(ph)
        ph.flush()
        ctx.close()

    def p4_out_proj(self):
        nc = self.nc
        ctx = ExitStack()
        sb = lambda n, shp, dt=F32: ctx.enter_context(nc.sbuf_tensor(n, shp, dt))
        yn = sb("p4_ynf", [P, 32 * 2048], BF16)
        wb = [sb(f"p4_w{i}", [P, 32 * 128], BF16) for i in range(2)]
        xrb = [sb(f"p4_x{i}", [P, T]) for i in range(2)]
        stg = [sb(f"p4_s{i}", [P, T]) for i in range(2)]
        pss = [ctx.enter_context(nc.psum_tensor(f"p4_ps{i}", [P, 512], F32))
               for i in range(8)]
        ph = self.phase("p4")

        self.sLD.v += 16
        t_yn = self.sLD.v
        stv = self.sST.v

        def f_ldyn(eng):
            eng.wait_ge(self.sST.h, stv)
            eng.dma_start(yn[:, :].rearrange("p (c j) -> p c j", c=32),
                          self.YNORM[:, :, :].rearrange("c p j -> p c j")
                          ).then_inc(self.sLD.h, 16)
        ph.steps["sync"].append(f_ldyn)

        t_wl = {}
        t_mm = {}
        t_ev = {}
        t_st = {}
        t_xr = {}
        for oc in range(16):
            o0 = oc * 128
            bank0 = (oc % 2) * 4
            waits = []
            if oc >= 2:
                waits.append((self.sPE, t_mm[oc - 2]))
            t_wl[oc] = ph.step("sync", waits=waits, inc=self.sLD, inck=16)(
                lambda eng, o0=o0, oc=oc: eng.dma_start(
                    wb[oc % 2][:, :].rearrange("p (c j) -> p c j", c=32),
                    self.W_OUT[:, :, o0:o0 + 128].rearrange("c p j -> p c j")))
            if True:
                xw = []
                if oc >= 2:
                    xw.append((self.sDV, t_ev[oc - 2]))
                t_xr[oc] = ph.step("sync", waits=xw, inc=self.sLD, inck=16)(
                    lambda eng, oc=oc: eng.dma_start(xrb[oc % 2][:, :],
                                                     self.XR[oc, :, :]))

                mywaits = [(self.sLD, t_wl[oc]), (self.sLD, t_yn)]
                if oc >= 2:
                    mywaits.append((self.sDV, t_ev[oc - 2]))

                def f_mm(eng, oc=oc, bank0=bank0):
                    ins = None
                    for kc in range(32):
                        lhsT = wb[oc % 2][:, kc * 128:(kc + 1) * 128]
                        for tt in range(4):
                            ins = eng.matmul(pss[bank0 + tt][:, :], lhsT,
                                             yn[:, kc * 2048 + tt * 512:kc * 2048 + tt * 512 + 512],
                                             start=(kc == 0), stop=(kc == 31))
                    return ins
                t_mm[oc] = ph.step("tensor", waits=mywaits, inc=self.sPE)(f_mm)

                evw = [(self.sPE, t_mm[oc]), (self.sLD, t_xr[oc])]
                if oc >= 2:
                    evw.append((self.sST, t_st[oc - 2]))

                def f_ev(eng, oc=oc, bank0=bank0):
                    ins = None
                    for tt in range(4):
                        ins = eng.tensor_add(stg[oc % 2][:, tt * 512:(tt + 1) * 512],
                                             pss[bank0 + tt][:, :],
                                             xrb[oc % 2][:, tt * 512:(tt + 1) * 512])
                    return ins
                t_ev[oc] = ph.step("vector", waits=evw, inc=self.sDV)(f_ev)
                t_st[oc] = ph.step("scalar", waits=[(self.sDV, t_ev[oc])],
                                   inc=self.sST, inck=16)(
                    lambda eng, oc=oc: eng.dma_start(self.X1[oc, :, :],
                                                     stg[oc % 2][:, :]))
        self.drain(ph)
        ph.flush()
        ctx.close()

    def p_ffn(self, SRC, DST, tag):
        nc = self.nc
        ctx = ExitStack()
        sb = lambda n, shp, dt=F32: ctx.enter_context(
            nc.sbuf_tensor(f"{tag}{n}", shp, dt))
        TH = 1024
        acc = sb("acc", [P, DC * TH])
        xln = sb("xln", [P, DC * TH], BF16)
        mbc = sb("mbc", [P, 1024])
        sqs = sb("sqs", [P, 2 * 512], BF16)
        gh = sb("gh", [P, 11 * TH], BF16)
        ftmp = sb("ftmp", [P, 512])
        cbs = sb("cbs", [P, TH])
        wgu = [sb(f"wg{i}", [P, DC * 512], BF16) for i in range(2)]
        wdn = [sb(f"wd{i}", [P, 11 * 512], BF16) for i in range(2)]
        rt_sb = sb("rt", [P, DC * E], BF16)
        sgw_sb = sb("sgw", [P, DC], BF16)
        ctm = sb("ctm", [P, 8 * E])
        crow = sb("crow", [1, TH])
        sigr = sb("sigr", [1, TH])
        rows = sb("rows", [1, 3 * 512])
        rwork = sb("rwork", [P, 336])
        pb = [ctx.enter_context(nc.psum_tensor(f"{tag}b{i}", [P, 512], F32))
              for i in range(8)]
        ph = self.phase(f"p_{tag}")
        st = {"gu_ld": 0, "dn_ld": 0, "mm_i": 0,
              "gu_slot_pe": [0, 0], "dn_slot_pe": [0, 0],
              "bank": [None] * 8, "acc_st": 0, "xln_pe": 0, "ct_dv": 0,
              "mrow_pe": 0}
        m_row, v_row, l_row = (rows[0:1, 0:512], rows[0:1, 512:1024],
                               rows[0:1, 1024:1536])

        def bankw(b):
            g = st["bank"][b]
            return [g] if g is not None else []

        t_rt = ph.step("sync", waits=[], inc=self.sLD, inck=16)(
            lambda eng: eng.dma_start(
                rt_sb[:, :].rearrange("p (c j) -> p c j", c=DC),
                self.W_RT[:, :, :].rearrange("c p j -> p c j")))
        t_rt = ph.step("sync", waits=[], inc=self.sLD, inck=16)(
            lambda eng: eng.dma_start(sgw_sb[:, :], self.W_SG[:, :]))

        for half in range(FFN_HALVES):
            t0h = half * TH

            lw = [(self.sST, st["acc_st"])] if st["acc_st"] else []
            t_ldx = ph.step("sync", waits=lw, inc=self.sLD, inck=16)(
                lambda eng, t0h=t0h: eng.dma_start(
                    acc[:, :].rearrange("p (c j) -> p c j", c=DC),
                    SRC[:, :, t0h:t0h + TH].rearrange("c p j -> p c j")))

            cw = [(self.sLD, t_ldx)]
            if st["xln_pe"]:
                cw.append((self.sPE, st["xln_pe"]))

            def f_cast(eng):
                ins = None
                for c in range(DC):
                    ins = eng.tensor_copy(xln[:, c * TH:(c + 1) * TH],
                                          acc[:, c * TH:(c + 1) * TH])
                return ins
            t_cast = ph.step("vector", waits=cw, inc=self.sDV)(f_cast)

            for ti in range(2):
                ts0 = ti * 512

                def f_sx(eng, ts0=ts0):
                    ins = None
                    for c in range(DC):
                        ins = eng.matmul(pb[0][0:1, :], self.c_ones_col_bf[:, :],
                                         xln[:, c * TH + ts0:c * TH + ts0 + 512],
                                         start=(c == 0), stop=(c == DC - 1))
                    return ins
                t_sx = ph.step("tensor", waits=[(self.sDV, t_cast)] + bankw(0),
                               inc=self.sPE)(f_sx)

                t_sqc, t_mmq = {}, {}
                for c in range(DC):
                    qw = [(self.sDV, t_cast)]
                    if c >= 2:
                        qw.append((self.sPE, t_mmq[c - 2]))

                    def f_sq(eng, c=c, ts0=ts0):
                        sl = slice((c % 2) * 512, (c % 2) * 512 + 512)
                        return eng.tensor_mul(
                            sqs[:, sl], xln[:, c * TH + ts0:c * TH + ts0 + 512],
                            xln[:, c * TH + ts0:c * TH + ts0 + 512])
                    t_sqc[c] = ph.step("vector", waits=qw, inc=self.sDV)(f_sq)

                    def f_mq(eng, c=c):
                        sl = slice((c % 2) * 512, (c % 2) * 512 + 512)
                        return eng.matmul(pb[1][0:1, :], self.c_ones_col_bf[:, :],
                                          sqs[:, sl], start=(c == 0),
                                          stop=(c == DC - 1))
                    t_mmq[c] = ph.step(
                        "tensor",
                        waits=[(self.sDV, t_sqc[c])] + (bankw(1) if c == 0 else []),
                        inc=self.sPE)(f_mq)

                rww = [(self.sPE, t_mmq[DC - 1]), (self.sPE, t_sx)]
                if st["mrow_pe"]:
                    rww.append((self.sPE, st["mrow_pe"]))

                def f_rows(eng):
                    eng.tensor_scalar(m_row, pb[0][0:1, :], 1.0 / 2048, None,
                                      ALU.mult)
                    eng.tensor_mul(v_row, m_row, m_row)
                    return eng.scalar_tensor_tensor(
                        v_row, pb[1][0:1, :], 1.0 / 2048, v_row,
                        ALU.mult, ALU.subtract)
                t_rows = ph.step("vector", waits=rww, inc=self.sDV)(f_rows)
                st["bank"][0] = st["bank"][1] = (self.sDV, t_rows)

                def f_rstd(eng):
                    eng.activation(l_row, v_row, AF.Ln, bias=self.c_eps[0:1, 0:1])
                    return eng.activation(l_row, l_row, AF.Exp, scale=-0.5)
                t_rstd = ph.step("scalar", waits=[(self.sDV, t_rows)],
                                 inc=self.sAC)(f_rstd)

                def f_bc(eng):
                    eng.matmul(pb[4][:, :], self.c_ones_row[:], m_row,
                               start=True, stop=True)
                    return eng.matmul(pb[5][:, :], self.c_ones_row[:], l_row,
                                      start=True, stop=True)
                t_bc = ph.step("tensor",
                               waits=[(self.sAC, t_rstd)] + bankw(4) + bankw(5),
                               inc=self.sPE)(f_bc)
                st["mrow_pe"] = t_bc

                def f_mc(eng):
                    eng.tensor_copy(mbc[:, 0:512], pb[4][:, :])
                    return eng.tensor_copy(mbc[:, 512:1024], pb[5][:, :])
                t_mc = ph.step("vector", waits=[(self.sPE, t_bc)],
                               inc=self.sDV)(f_mc)
                st["bank"][4] = st["bank"][5] = (self.sDV, t_mc)

                def f_norm(eng, ts0=ts0):
                    for c in range(DC):
                        sl = slice(c * TH + ts0, c * TH + ts0 + 512)
                        eng.tensor_sub(xln[:, sl], xln[:, sl], mbc[:, 0:512])
                    ins = None
                    for c in range(DC):
                        sl = slice(c * TH + ts0, c * TH + ts0 + 512)
                        ins = eng.tensor_mul(xln[:, sl], xln[:, sl],
                                             mbc[:, 512:1024])
                    return ins
                t_norm = ph.step("vector", waits=[], inc=self.sDV)(f_norm)
            t_xln = t_norm

            t_lgdv = None
            for tt8 in range(8):
                q0 = tt8 * 128
                b = 6 + tt8 % 2

                def f_lg(eng, q0=q0, b=b):
                    ins = None
                    for c in range(DC):
                        ins = eng.matmul(pb[b][0:128, 0:E],
                                         xln[:, c * TH + q0:c * TH + q0 + 128],
                                         rt_sb[:, c * E:(c + 1) * E],
                                         start=(c == 0), stop=(c == DC - 1))
                    return ins
                t_lg = ph.step(
                    "tensor",
                    waits=[(self.sDV, t_xln), (self.sLD, t_rt)] + bankw(b),
                    inc=self.sPE)(f_lg)

                def f_ex(eng, b=b, tt8=tt8):
                    lg = pb[b][0:128, 0:E]
                    eng.reduce_max(rwork[:, 192 + tt8:193 + tt8], lg, AX.X,
                                   negate=True)
                    return eng.tensor_copy(rwork[:, tt8 * E:tt8 * E + E], lg)
                t_lgdv = ph.step("vector", waits=[(self.sPE, t_lg)],
                                 inc=self.sDV)(f_ex)
                st["bank"][b] = (self.sDV, t_lgdv)

            def f_exp(eng):
                ins = None
                for t8 in range(8):
                    ins = eng.activation(rwork[:, 64 + t8 * E:64 + t8 * E + E],
                                         rwork[:, t8 * E:t8 * E + E], AF.Exp,
                                         bias=rwork[:, 192 + t8:193 + t8])
                return ins
            t_exp = ph.step("scalar", waits=[(self.sDV, t_lgdv)],
                            inc=self.sAC)(f_exp)

            def f_t1(eng):
                ins = None
                for t8 in range(8):
                    ins = eng.reduce_max(rwork[:, 128 + t8:129 + t8],
                                         rwork[:, 64 + t8 * E:64 + t8 * E + E],
                                         AX.X)
                return ins
            t_t1 = ph.step("vector", waits=[(self.sAC, t_exp)],
                           inc=self.sDV)(f_t1)
            t_t1b = ph.step("scalar", waits=[(self.sDV, t_t1)], inc=self.sAC)(
                lambda eng: eng.copy(rwork[:, 136:144], rwork[:, 128:136]))

            def f_mask(eng):
                ins = None
                for t8 in range(8):
                    ins = eng.tensor_scalar(
                        rwork[:, 208 + t8 * E:208 + t8 * E + E],
                        rwork[:, 64 + t8 * E:64 + t8 * E + E],
                        rwork[:, 136 + t8:137 + t8], None, ALU.is_ge)
                return ins
            t_mask = ph.step("vector", waits=[(self.sAC, t_t1b)],
                             inc=self.sDV)(f_mask)
            t_maskb = ph.step("scalar", waits=[(self.sDV, t_mask)],
                              inc=self.sAC)(
                lambda eng: eng.copy(rwork[:, 272:336], rwork[:, 208:272]))

            def f_excl(eng):
                ins = None
                for t8 in range(8):
                    ins = eng.scalar_tensor_tensor(
                        rwork[:, 208 + t8 * E:208 + t8 * E + E],
                        rwork[:, 272 + t8 * E:272 + t8 * E + E], -1e30,
                        rwork[:, 64 + t8 * E:64 + t8 * E + E],
                        ALU.mult, ALU.add)
                return ins
            t_excl = ph.step("vector", waits=[(self.sAC, t_maskb)],
                             inc=self.sDV)(f_excl)
            t_exclb = ph.step("scalar", waits=[(self.sDV, t_excl)],
                              inc=self.sAC)(
                lambda eng: eng.copy(rwork[:, 272:336], rwork[:, 208:272]))

            def f_sec(eng):
                ins = None
                for t8 in range(8):
                    ins = eng.reduce_max(rwork[:, 144 + t8:145 + t8],
                                         rwork[:, 272 + t8 * E:272 + t8 * E + E],
                                         AX.X)
                return ins
            t_sec = ph.step("vector", waits=[(self.sAC, t_exclb)],
                            inc=self.sDV)(f_sec)
            t_secb = ph.step("scalar", waits=[(self.sDV, t_sec)], inc=self.sAC)(
                lambda eng: eng.copy(rwork[:, 152:160], rwork[:, 144:152]))

            def f_sum(eng):
                eng.tensor_add(rwork[:, 160:168], rwork[:, 152:160],
                               rwork[:, 128:136])
                ins = None
                for t8 in range(8):
                    ins = eng.tensor_scalar(
                        rwork[:, 208 + t8 * E:208 + t8 * E + E],
                        rwork[:, 64 + t8 * E:64 + t8 * E + E],
                        rwork[:, 152 + t8:153 + t8], None, ALU.is_ge)
                return ins
            t_sum = ph.step("vector", waits=[(self.sAC, t_secb)],
                            inc=self.sDV)(f_sum)

            def f_sumb(eng):
                eng.copy(rwork[:, 200:208], rwork[:, 160:168])
                return eng.copy(rwork[:, 272:336], rwork[:, 208:272])
            t_sumb = ph.step("scalar", waits=[(self.sDV, t_sum)],
                             inc=self.sAC)(f_sumb)

            t_rec = ph.step("vector", waits=[(self.sAC, t_sumb)],
                            inc=self.sDV)(
                lambda eng: eng.reciprocal(rwork[:, 168:176],
                                           rwork[:, 200:208]))
            t_recb = ph.step("scalar", waits=[(self.sDV, t_rec)], inc=self.sAC)(
                lambda eng: eng.copy(rwork[:, 176:184], rwork[:, 168:176]))

            def f_ct(eng):
                ins = None
                for t8 in range(8):
                    ins = eng.scalar_tensor_tensor(
                        ctm[:, t8 * E:(t8 + 1) * E],
                        rwork[:, 64 + t8 * E:64 + t8 * E + E],
                        rwork[:, 176 + t8:177 + t8],
                        rwork[:, 272 + t8 * E:272 + t8 * E + E],
                        ALU.mult, ALU.mult)
                return ins
            st["ct_dv"] = ph.step("vector", waits=[(self.sAC, t_recb)],
                                  inc=self.sDV)(f_ct)

            for ti in range(2):
                b = 6 + ti

                def f_sg(eng, ti=ti, b=b):
                    ins = None
                    for c in range(DC):
                        ins = eng.matmul(
                            pb[b][0:1, :], sgw_sb[:, c:c + 1],
                            xln[:, c * TH + ti * 512:c * TH + ti * 512 + 512],
                            start=(c == 0), stop=(c == DC - 1))
                    return ins
                t_s1 = ph.step(
                    "tensor",
                    waits=[(self.sDV, t_xln), (self.sLD, t_rt)] + bankw(b),
                    inc=self.sPE)(f_sg)
                t_s2 = ph.step("scalar", waits=[(self.sPE, t_s1)], inc=self.sAC)(
                    lambda eng, ti=ti, b=b: eng.activation(
                        sigr[0:1, ti * 512:ti * 512 + 512], pb[b][0:1, :],
                        AF.Tanh, scale=0.5))
                st["bank"][b] = (self.sAC, t_s2)
            t_sg3 = ph.step("vector", waits=[(self.sAC, t_s2)], inc=self.sDV)(
                lambda eng: eng.tensor_scalar(sigr[0:1, :], sigr[0:1, :],
                                              0.5, 0.5, ALU.mult, ALU.add))

            pe_list = ([("r", e) for e in range(E)] +
                       [("s", q) for q in range(4)])[:FFN_NEXP]
            for pidx, (kind, e) in enumerate(pe_list):
                self._ffn_expert(ph, st, kind, e, xln, gh, wgu, wdn, ctm, crow,
                                 sigr, cbs, ftmp, acc, pb, t_sg3)
            st["xln_pe"] = self.sPE.v

            dvv = self.sDV.v
            st["acc_st"] = ph.step("scalar", waits=[(self.sDV, dvv)],
                                   inc=self.sST, inck=16)(
                lambda eng, t0h=t0h: eng.dma_start(
                    DST[:, :, t0h:t0h + TH].rearrange("c p j -> p c j"),
                    acc[:, :].rearrange("p (c j) -> p c j", c=DC)))

        self.drain(ph)
        ph.flush()
        if "fdbg" in self.dbg and tag == "f1":
            self.dump_sbuf("dbg_facc", acc[:, :], [P, DC * TH], F32)
            self.dump_sbuf("dbg_fxln", xln[:, :], [P, DC * TH], BF16)
            self.dump_sbuf("dbg_fctm", ctm[:, :], [P, 8 * E], F32)
            self.dump_sbuf("dbg_fsig", sigr[0:1, :], [1, TH], F32)
            self.dump_sbuf("dbg_fgh", gh[:, :], [P, 11 * TH], BF16)
            self.dump_sbuf("dbg_fcbs", cbs[:, :], [P, TH], F32)
            self.dump_sbuf("dbg_fcrow", crow[0:1, :], [1, TH], F32)
        ctx.close()

    def _ffn_expert(self, ph, st, kind, e, xln, gh, wgu, wdn, ctm, crow, sigr,
                    cbs, ftmp, acc, pb, t_sg3):
        nc = self.nc
        TH = 1024
        GRPS = [(0, 512), (512, 512), (1024, 384)]

        if kind == "r":
            tps = {}
            for rnd in range(2):
                if rnd == 0:
                    tw = [(self.sDV, st["ct_dv"])] + bankw_list(st, 6)
                else:
                    tw = [(self.sDV, tps[0])]

                def f_tp(eng, rnd=rnd, e=e):
                    ins = None
                    for j in range(4):
                        tt8 = rnd * 4 + j
                        ins = eng.transpose(
                            pb[6][0:1, j * 128:j * 128 + 128],
                            ctm[:, tt8 * E + e:tt8 * E + e + 1],
                            self.c_ident[:, :])
                    return ins
                t_tp = ph.step("tensor", waits=tw, inc=self.sPE)(f_tp)

                def f_cv(eng, rnd=rnd):
                    return eng.tensor_copy(
                        crow[0:1, rnd * 512:rnd * 512 + 512], pb[6][0:1, :])
                tps[rnd] = ph.step("vector", waits=[(self.sPE, t_tp)],
                                   inc=self.sDV)(f_cv)
            st["bank"][6] = (self.sDV, tps[1])
            srow, row_dep = crow, (self.sDV, tps[1])
        else:
            srow, row_dep = sigr, (self.sDV, t_sg3)

        def f_k1(eng, srow=srow):
            eng.matmul(pb[4][:, :], self.c_ones_row[:], srow[0:1, 0:512],
                       start=True, stop=True)
            return eng.matmul(pb[5][:, :], self.c_ones_row[:],
                              srow[0:1, 512:1024], start=True, stop=True)
        t_k1 = ph.step("tensor",
                       waits=[row_dep] + bankw_list(st, 4) + bankw_list(st, 5),
                       inc=self.sPE)(f_k1)

        def f_cbs(eng):
            eng.tensor_copy(cbs[:, 0:512], pb[4][:, :])
            return eng.tensor_copy(cbs[:, 512:1024], pb[5][:, :])
        t_cbs = ph.step("vector", waits=[(self.sPE, t_k1)], inc=self.sDV)(f_cbs)
        st["bank"][4] = st["bank"][5] = (self.sDV, t_cbs)

        t_gate_last = None
        t_up_last = None
        for up_flag in range(2):
            for gi, (io0, gw) in enumerate(GRPS):
                slot = st["gu_ld"] % 2
                st["gu_ld"] += 1
                lw = ([(self.sPE, st["gu_slot_pe"][slot])]
                      if st["gu_slot_pe"][slot] else [])
                if kind == "r":
                    wsrc = self.W_GU[e, :, :, up_flag * I_MOE + io0:
                                     up_flag * I_MOE + io0 + gw]
                else:
                    W = self.W_SHU if up_flag else self.W_SHG
                    wsrc = W[:, :, e * I_MOE + io0:e * I_MOE + io0 + gw]
                t_wl = ph.step("sync", waits=lw, inc=self.sLD, inck=16)(
                    lambda eng, slot=slot, gw=gw, wsrc=wsrc: eng.dma_start(
                        wgu[slot][:, 0:DC * gw].rearrange("p (c j) -> p c j",
                                                          c=DC),
                        wsrc.rearrange("c p j -> p c j")))
                for icl in range(gw // 128):
                    ic = io0 // 128 + icl
                    for tt in range(2):
                        b = st["mm_i"] % 4
                        st["mm_i"] += 1
                        pw = [(self.sLD, t_wl)] + bankw_list(st, b)

                        def f_mm(eng, slot=slot, gw=gw, icl=icl, tt=tt, b=b):
                            ins = None
                            for kc in range(DC):
                                lhsT = wgu[slot][:, kc * gw + icl * 128:
                                                 kc * gw + icl * 128 + 128]
                                rhs = xln[:, kc * TH + tt * 512:
                                          kc * TH + tt * 512 + 512]
                                ins = eng.matmul(pb[b][:, :], lhsT, rhs,
                                                 start=(kc == 0),
                                                 stop=(kc == DC - 1))
                            return ins
                        t_mm = ph.step("tensor", waits=pw, inc=self.sPE)(f_mm)
                        st["gu_slot_pe"][slot] = t_mm
                        gsl = slice(ic * TH + tt * 512, ic * TH + tt * 512 + 512)
                        if not up_flag:
                            t_gate_last = ph.step(
                                "scalar", waits=[(self.sPE, t_mm)],
                                inc=self.sAC)(
                                lambda eng, gsl=gsl, b=b: eng.activation(
                                    gh[:, gsl], pb[b][:, :], AF.Silu))
                            st["bank"][b] = (self.sAC, t_gate_last)
                        else:
                            uw = [(self.sPE, t_mm), (self.sAC, t_gate_last)]

                            def f_up(eng, gsl=gsl, b=b, tt=tt):
                                eng.tensor_mul(ftmp[:, :], pb[b][:, :],
                                               cbs[:, tt * 512:tt * 512 + 512])
                                return eng.tensor_mul(gh[:, gsl], ftmp[:, :],
                                                      gh[:, gsl])
                            t_up_last = ph.step("vector", waits=uw,
                                                inc=self.sDV)(f_up)
                            st["bank"][b] = (self.sDV, t_up_last)

        for og in range(4):
            slot = st["dn_ld"] % 2
            st["dn_ld"] += 1
            lw = ([(self.sPE, st["dn_slot_pe"][slot])]
                  if st["dn_slot_pe"][slot] else [])
            if kind == "r":
                wsrc = self.W_DN[e, :, :, og * 512:og * 512 + 512]
            else:
                wsrc = self.W_SHD[11 * e:11 * e + 11, :, og * 512:og * 512 + 512]
            t_wl = ph.step("sync", waits=lw, inc=self.sLD, inck=16)(
                lambda eng, slot=slot, wsrc=wsrc: eng.dma_start(
                    wdn[slot][:, :].rearrange("p (c j) -> p c j", c=11),
                    wsrc.rearrange("c p j -> p c j")))
            for ocl in range(4):
                co = og * 4 + ocl
                for tt in range(2):
                    b = st["mm_i"] % 4
                    st["mm_i"] += 1
                    pw = [(self.sLD, t_wl), (self.sDV, t_up_last)] + \
                        bankw_list(st, b)

                    def f_dmm(eng, slot=slot, ocl=ocl, tt=tt, b=b):
                        ins = None
                        for ic in range(11):
                            lhsT = wdn[slot][:, ic * 512 + ocl * 128:
                                             ic * 512 + ocl * 128 + 128]
                            rhs = gh[:, ic * TH + tt * 512:
                                     ic * TH + tt * 512 + 512]
                            ins = eng.matmul(pb[b][:, :], lhsT, rhs,
                                             start=(ic == 0), stop=(ic == 10))
                        return ins
                    t_mm = ph.step("tensor", waits=pw, inc=self.sPE)(f_dmm)
                    st["dn_slot_pe"][slot] = t_mm

                    def f_dev(eng, co=co, tt=tt, b=b):
                        sl = slice(co * TH + tt * 512, co * TH + tt * 512 + 512)
                        return eng.tensor_add(acc[:, sl], acc[:, sl],
                                              pb[b][:, :])
                    t_dev = ph.step("vector", waits=[(self.sPE, t_mm)],
                                    inc=self.sDV)(f_dev)
                    st["bank"][b] = (self.sDV, t_dev)

    def p6_attn(self, SRC, DST):
        nc = self.nc
        ctx = ExitStack()
        sb = lambda n, shp, dt=F32: ctx.enter_context(nc.sbuf_tensor(n, shp, dt))
        TH2 = 1024
        union = sb("a_un", [P, AH * T], BF16)
        kfm = sb("a_kf", [P, 4 * T], BF16)
        vtm = sb("a_vt", [P, 16 * 512], BF16)
        tck = sb("a_ck", [P, T], BF16)
        tsk = sb("a_sk", [P, T], BF16)
        rsw = sb("a_rw", [P, P], BF16)
        wqb = [sb(f"a_wq{i}", [P, DC * 128], BF16) for i in range(2)]
        wkv = [sb(f"a_wk{i}", [P, DC * 256], BF16) for i in range(2)]
        t1 = sb("a_t1", [P, 512], BF16)
        rows = sb("a_ro", [1, 3 * 512])
        mbc = sb("a_mbc", [P, 1024])
        sqs = sb("a_sqs", [P, 2 * 512], BF16)
        pb = [ctx.enter_context(nc.psum_tensor(f"a_b{i}", [P, 512], F32))
              for i in range(8)]
        ctxB = ExitStack()
        xln = ctxB.enter_context(nc.sbuf_tensor("a_xln", [P, DC * TH2], BF16))
        xt = ctxB.enter_context(nc.sbuf_tensor("a_xt", [P, DC * 256], F32))
        ph = self.phase("p6")
        st = {"bank": [None] * 8, "mrow_pe": 0, "xln_pe": 0, "cast_dv": 0,
              "wq_pe": [0, 0], "wkv_pe": [0, 0], "xt_dv": 0}
        m_row, v_row, l_row = (rows[0:1, 0:512], rows[0:1, 512:1024],
                               rows[0:1, 1024:1536])

        def bankw(b):
            g = st["bank"][b]
            return [g] if g is not None else []

        for dst, dsrc in [(tck, self.COS_K), (tsk, self.SIN_K),
                          (rsw, self.RSW)]:
            ph.step("sync", waits=[], inc=self.sLD, inck=16)(
                lambda eng, dst=dst, dsrc=dsrc: eng.dma_start(dst[:, :],
                                                              dsrc[:, :]))
        t_tbl = self.sLD.v

        q_ld_i = 0
        kv_ld_i = 0
        pp_i = 0
        sw_i = 0
        for th in range(2):
            t0h = th * TH2

            for ti in range(2):
                ts_g = t0h + ti * 512
                for hf in range(2):
                    xw = [(self.sDV, st["xt_dv"])] if st["xt_dv"] else []
                    t_ldx = ph.step("sync", waits=xw, inc=self.sLD, inck=16)(
                        lambda eng, ts_g=ts_g, hf=hf: eng.dma_start(
                            xt[:, :].rearrange("p (c j) -> p c j", c=DC),
                            SRC[:, :, ts_g + hf * 256:ts_g + hf * 256 + 256]
                            .rearrange("c p j -> p c j")))
                    cw = [(self.sLD, t_ldx)]
                    if st["xln_pe"]:
                        cw.append((self.sPE, st["xln_pe"]))

                    def f_cast(eng, ti=ti, hf=hf):
                        ins = None
                        for c in range(DC):
                            o = c * TH2 + ti * 512 + hf * 256
                            ins = eng.tensor_copy(xln[:, o:o + 256],
                                                  xt[:, c * 256:(c + 1) * 256])
                        return ins
                    t_cast = ph.step("vector", waits=cw, inc=self.sDV)(f_cast)
                    st["xt_dv"] = t_cast

                def f_sx(eng, ti=ti):
                    ins = None
                    for c in range(DC):
                        ins = eng.matmul(
                            pb[0][0:1, :], self.c_ones_col_bf[:, :],
                            xln[:, c * TH2 + ti * 512:c * TH2 + ti * 512 + 512],
                            start=(c == 0), stop=(c == DC - 1))
                    return ins
                t_sx = ph.step("tensor", waits=[(self.sDV, t_cast)] + bankw(0),
                               inc=self.sPE)(f_sx)

                t_sqc, t_mmq = {}, {}
                for c in range(DC):
                    qw = [(self.sDV, t_cast)]
                    if c >= 2:
                        qw.append((self.sPE, t_mmq[c - 2]))

                    def f_sq(eng, c=c, ti=ti):
                        sl = slice((c % 2) * 512, (c % 2) * 512 + 512)
                        xs = xln[:, c * TH2 + ti * 512:c * TH2 + ti * 512 + 512]
                        return eng.tensor_mul(sqs[:, sl], xs, xs)
                    t_sqc[c] = ph.step("vector", waits=qw, inc=self.sDV)(f_sq)

                    def f_mq(eng, c=c):
                        sl = slice((c % 2) * 512, (c % 2) * 512 + 512)
                        return eng.matmul(pb[1][0:1, :], self.c_ones_col_bf[:, :],
                                          sqs[:, sl], start=(c == 0),
                                          stop=(c == DC - 1))
                    t_mmq[c] = ph.step(
                        "tensor",
                        waits=[(self.sDV, t_sqc[c])] + (bankw(1) if c == 0 else []),
                        inc=self.sPE)(f_mq)

                rww = [(self.sPE, t_mmq[DC - 1]), (self.sPE, t_sx)]
                if st["mrow_pe"]:
                    rww.append((self.sPE, st["mrow_pe"]))

                def f_rows(eng):
                    eng.tensor_scalar(m_row, pb[0][0:1, :], 1.0 / 2048, None,
                                      ALU.mult)
                    eng.tensor_mul(v_row, m_row, m_row)
                    return eng.scalar_tensor_tensor(
                        v_row, pb[1][0:1, :], 1.0 / 2048, v_row,
                        ALU.mult, ALU.subtract)
                t_rows = ph.step("vector", waits=rww, inc=self.sDV)(f_rows)
                st["bank"][0] = st["bank"][1] = (self.sDV, t_rows)

                def f_rstd(eng):
                    eng.activation(l_row, v_row, AF.Ln, bias=self.c_eps[0:1, 0:1])
                    return eng.activation(l_row, l_row, AF.Exp, scale=-0.5)
                t_rstd = ph.step("scalar", waits=[(self.sDV, t_rows)],
                                 inc=self.sAC)(f_rstd)

                def f_bc(eng):
                    eng.matmul(pb[2][:, :], self.c_ones_row[:], m_row,
                               start=True, stop=True)
                    return eng.matmul(pb[3][:, :], self.c_ones_row[:], l_row,
                                      start=True, stop=True)
                t_bc = ph.step("tensor",
                               waits=[(self.sAC, t_rstd)] + bankw(2) + bankw(3),
                               inc=self.sPE)(f_bc)
                st["mrow_pe"] = t_bc

                def f_mc(eng):
                    eng.tensor_copy(mbc[:, 0:512], pb[2][:, :])
                    return eng.tensor_copy(mbc[:, 512:1024], pb[3][:, :])
                t_mc = ph.step("vector", waits=[(self.sPE, t_bc)],
                               inc=self.sDV)(f_mc)
                st["bank"][2] = st["bank"][3] = (self.sDV, t_mc)

                def f_norm(eng, ti=ti):
                    for c in range(DC):
                        sl = slice(c * TH2 + ti * 512, c * TH2 + ti * 512 + 512)
                        eng.tensor_sub(xln[:, sl], xln[:, sl], mbc[:, 0:512])
                    ins = None
                    for c in range(DC):
                        sl = slice(c * TH2 + ti * 512, c * TH2 + ti * 512 + 512)
                        ins = eng.tensor_mul(xln[:, sl], xln[:, sl],
                                             mbc[:, 512:1024])
                    return ins
                t_norm = ph.step("vector", waits=[], inc=self.sDV)(f_norm)
            t_xln = t_norm

            def rope_tile(dst_sl, tc, ts, ts_g, t_evac, swb):
                t_sw = ph.step("tensor",
                               waits=[(self.sDV, t_evac), (self.sLD, t_tbl)] +
                               bankw(swb), inc=self.sPE)(
                    lambda eng: eng.matmul(pb[swb][:, :], rsw[:, :],
                                           dst_sl, start=True, stop=True))

                def f_rope(eng):
                    eng.tensor_mul(t1[:, :], pb[swb][:, :],
                                   ts[:, ts_g:ts_g + 512])
                    eng.tensor_mul(dst_sl, dst_sl, tc[:, ts_g:ts_g + 512])
                    return eng.tensor_add(dst_sl, dst_sl, t1[:, :])
                t_r = ph.step("vector", waits=[(self.sPE, t_sw)],
                              inc=self.sDV)(f_rope)
                st["bank"][swb] = (self.sDV, t_r)
                return t_r

            for grp in range(2):
                slot = kv_ld_i % 2
                kv_ld_i += 1
                lw = ([(self.sPE, st["wkv_pe"][slot])]
                      if st["wkv_pe"][slot] else [])
                t_wl = ph.step("sync", waits=lw, inc=self.sLD, inck=16)(
                    lambda eng, slot=slot, grp=grp: eng.dma_start(
                        wkv[slot][:, :].rearrange("p (c j) -> p c j", c=DC),
                        self.WK[:, :, grp * 256:grp * 256 + 256]
                        .rearrange("c p j -> p c j")))
                for kvl in range(2):
                    kv = grp * 2 + kvl
                    for tt in range(2):
                        ts_g = t0h + tt * 512
                        ppx = pp_i % 3
                        pp_i += 1

                        def f_kmm(eng, slot=slot, kvl=kvl, tt=tt, ppx=ppx):
                            ins = None
                            for kc in range(DC):
                                ins = eng.matmul(
                                    pb[ppx][:, :],
                                    wkv[slot][:, kc * 256 + kvl * 128:
                                              kc * 256 + kvl * 128 + 128],
                                    xln[:, kc * TH2 + tt * 512:
                                        kc * TH2 + tt * 512 + 512],
                                    start=(kc == 0), stop=(kc == DC - 1))
                            return ins
                        t_mm = ph.step(
                            "tensor",
                            waits=[(self.sLD, t_wl), (self.sDV, t_xln)] +
                            bankw(ppx), inc=self.sPE)(f_kmm)
                        st["wkv_pe"][slot] = t_mm
                        dst_sl = kfm[:, kv * T + ts_g:kv * T + ts_g + 512]
                        t_ev = ph.step("vector", waits=[(self.sPE, t_mm)],
                                       inc=self.sDV)(
                            lambda eng, dst_sl=dst_sl, ppx=ppx:
                            eng.tensor_copy(dst_sl, pb[ppx][:, :]))
                        st["bank"][ppx] = (self.sDV, t_ev)
                        rope_tile(dst_sl, tck, tsk, ts_g, t_ev, 4 + sw_i % 2)
                        sw_i += 1

            for grp in range(2):
                slot = kv_ld_i % 2
                kv_ld_i += 1
                lw = ([(self.sPE, st["wkv_pe"][slot])]
                      if st["wkv_pe"][slot] else [])
                t_wl = ph.step("sync", waits=lw, inc=self.sLD, inck=16)(
                    lambda eng, slot=slot, grp=grp: eng.dma_start(
                        wkv[slot][:, :].rearrange("p (c j) -> p c j", c=DC),
                        self.WV[:, :, grp * 256:grp * 256 + 256]
                        .rearrange("c p j -> p c j")))
                for tl in range(8):
                    ppx = pp_i % 3
                    pp_i += 1

                    def f_vmm(eng, slot=slot, tl=tl, ppx=ppx):
                        ins = None
                        for kc in range(DC):
                            ins = eng.matmul(
                                pb[ppx][0:128, 0:256],
                                xln[:, kc * TH2 + tl * 128:
                                    kc * TH2 + tl * 128 + 128],
                                wkv[slot][:, kc * 256:kc * 256 + 256],
                                start=(kc == 0), stop=(kc == DC - 1))
                        return ins
                    t_mm = ph.step(
                        "tensor",
                        waits=[(self.sLD, t_wl), (self.sDV, t_xln)] + bankw(ppx),
                        inc=self.sPE)(f_vmm)
                    st["wkv_pe"][slot] = t_mm
                    vsl = vtm[:, (8 * th + tl) * 512 + grp * 256:
                              (8 * th + tl) * 512 + grp * 256 + 256]
                    t_ev = ph.step("vector", waits=[(self.sPE, t_mm)],
                                   inc=self.sDV)(
                        lambda eng, vsl=vsl, ppx=ppx:
                        eng.tensor_copy(vsl, pb[ppx][0:128, 0:256]))
                    st["bank"][ppx] = (self.sDV, t_ev)

            for h in range(AH):
                slot = q_ld_i % 2
                q_ld_i += 1
                lw = ([(self.sPE, st["wq_pe"][slot])]
                      if st["wq_pe"][slot] else [])
                t_wl = ph.step("sync", waits=lw, inc=self.sLD, inck=16)(
                    lambda eng, slot=slot, h=h: eng.dma_start(
                        wqb[slot][:, :].rearrange("p (c j) -> p c j", c=DC),
                        self.WQ[:, :, h * 128:h * 128 + 128]
                        .rearrange("c p j -> p c j")))
                for tt in range(2):
                    ts_g = t0h + tt * 512
                    ppx = pp_i % 3
                    pp_i += 1

                    def f_qmm(eng, slot=slot, tt=tt, ppx=ppx):
                        ins = None
                        for kc in range(DC):
                            ins = eng.matmul(
                                pb[ppx][:, :],
                                wqb[slot][:, kc * 128:kc * 128 + 128],
                                xln[:, kc * TH2 + tt * 512:
                                    kc * TH2 + tt * 512 + 512],
                                start=(kc == 0), stop=(kc == DC - 1))
                        return ins
                    t_mm = ph.step(
                        "tensor",
                        waits=[(self.sLD, t_wl), (self.sDV, t_xln)] + bankw(ppx),
                        inc=self.sPE)(f_qmm)
                    st["wq_pe"][slot] = t_mm
                    dst_sl = union[:, h * T + ts_g:h * T + ts_g + 512]
                    t_ev = ph.step("vector", waits=[(self.sPE, t_mm)],
                                   inc=self.sDV)(
                        lambda eng, dst_sl=dst_sl, ppx=ppx:
                        eng.tensor_scalar_mul(dst_sl, pb[ppx][:, :],
                                              1.0 / float(np.sqrt(AHD))))
                    st["bank"][ppx] = (self.sDV, t_ev)
                    rope_tile(dst_sl, tck, tsk, ts_g, t_ev, 4 + sw_i % 2)
                    sw_i += 1
            st["xln_pe"] = self.sPE.v

        ctxB.close()
        xrb = [sb(f"a_xr{i}", [P, T]) for i in range(2)]
        stg = [sb(f"a_sg{i}", [P, T]) for i in range(2)]
        probs = sb("a_pb", [P, 4 * 512], BF16)
        rbs = sb("a_rb", [P, 512])
        qrow = [sb(f"a_qr{i}", [1, 512], BF16) for i in range(2)]
        t_bdone = self.sDV.v

        qn_pe = [0, 0]
        for h in range(AH):
            kv = h // 4
            qw_ = [(self.sPE, qn_pe[h % 2])] if qn_pe[h % 2] else []
            t_qn = ph.step("sync", waits=qw_, inc=self.sLD, inck=16)(
                lambda eng, h=h: eng.dma_start(qrow[h % 2][:, :],
                                               self.QNEG[h:h + 1, :]))
            for qt in range(4):
                q0 = qt * 512
                nkt = 4 * qt + 4
                t_exp, t_spv = {}, {}
                for kt in range(nkt):
                    jc = kt - 4 * qt + 12
                    A = pb[kt % 4]
                    scw = [(self.sDV, t_bdone)] if (h == 0 and qt == 0) else []
                    scw += ([(self.sAC, t_exp[kt - 4])] if kt >= 4
                            else bankw(kt % 4))

                    def f_sc(eng, h=h, kv=kv, kt=kt, q0=q0, A=A):
                        eng.matmul(A[:, :],
                                   kfm[:, kv * T + kt * 128:kv * T + kt * 128 + 128],
                                   union[:, h * T + q0:h * T + q0 + 512],
                                   start=True, stop=False)
                        return eng.matmul(A[:, :], self.c_ones_row_bf[:, :],
                                          qrow[h % 2][0:1, :], start=False,
                                          stop=True)
                    if kt == 0 and qt == 0:
                        scw.append((self.sLD, t_qn))
                    t_sc = ph.step("tensor", waits=scw, inc=self.sPE)(f_sc)

                    t_msk = None
                    if kt >= 4 * qt:
                        j0 = kt - 4 * qt

                        def f_msk(eng, j0=j0, A=A):
                            ins = None
                            if j0 > 0:
                                ins = eng.memset(A[:, 0:128 * j0], NEG_BIG)
                            sl = slice(j0 * 128, j0 * 128 + 128)
                            return eng.tensor_add(A[:, sl], A[:, sl],
                                                  self.c_tri[:, :])
                        t_msk = ph.step("vector", waits=[(self.sPE, t_sc)],
                                        inc=self.sDV)(f_msk)

                    ew = [(self.sPE, t_sc)]
                    if t_msk is not None:
                        ew.append((self.sDV, t_msk))
                    if kt >= 4:
                        ew.append((self.sPE, t_spv[kt - 4]))
                    elif st.get("spv_prev"):
                        ew.append((self.sPE, st["spv_prev"]))

                    def f_exp(eng, kt=kt, h=h, jc=jc, A=A):
                        return eng.activation(
                            probs[:, (kt % 4) * 512:(kt % 4) * 512 + 512],
                            A[:, :], AF.Exp,
                            bias=self.c_alibi[:, h * 16 + jc:h * 16 + jc + 1])
                    t_exp[kt] = ph.step("scalar", waits=ew, inc=self.sAC)(f_exp)

                    def emit_spv(kt, kv=kv, nkt=nkt):
                        sw = [(self.sAC, t_exp[kt])]
                        if kt == 0:
                            sw += bankw(6) + bankw(7)

                        def f_spv(eng, kt=kt, kv=kv, nkt=nkt):
                            psl = probs[:, (kt % 4) * 512:(kt % 4) * 512 + 512]
                            eng.matmul(pb[6][0:1, :], self.c_ones_col_bf[:, :],
                                       psl, start=(kt == 0), stop=(kt == nkt - 1))
                            return eng.matmul(
                                pb[7][:, :],
                                vtm[:, kt * 512 + kv * 128:kt * 512 + kv * 128 + 128],
                                psl, start=(kt == 0), stop=(kt == nkt - 1))
                        t_spv[kt] = ph.step("tensor", waits=sw, inc=self.sPE)(f_spv)
                    if kt >= 1:
                        emit_spv(kt - 1)
                    if kt == nkt - 1:
                        emit_spv(kt)

                def f_r(eng):
                    eng.activation(l_row, pb[6][0:1, :], AF.Ln)
                    return eng.activation(l_row, l_row, AF.Exp, scale=-1.0)
                t_r = ph.step("scalar", waits=[(self.sPE, t_spv[nkt - 1])],
                              inc=self.sAC)(f_r)
                st["bank"][6] = (self.sAC, t_r)
                for kb in range(min(4, nkt)):
                    st["bank"][(nkt - 1 - kb) % 4] = (
                        self.sAC, t_exp[nkt - 1 - kb])
                st["spv_prev"] = t_spv[nkt - 1]

                t_rbc = ph.step("tensor", waits=[(self.sAC, t_r)] + bankw(5),
                                inc=self.sPE)(
                    lambda eng: eng.matmul(pb[5][:, :], self.c_ones_row[:],
                                           l_row, start=True, stop=True))

                def f_fin(eng, h=h, q0=q0):
                    eng.tensor_copy(rbs[:, :], pb[5][:, :])
                    return eng.tensor_mul(union[:, h * T + q0:h * T + q0 + 512],
                                          rbs[:, :], pb[7][:, :])
                t_fin = ph.step("vector", waits=[(self.sPE, t_rbc)],
                                inc=self.sDV)(f_fin)
                st["bank"][5] = st["bank"][7] = (self.sDV, t_fin)
            qn_pe[h % 2] = t_spv[nkt - 1]
        t_cdone = self.sDV.v

        t_wl, t_mm, t_ev, t_st, t_xr = {}, {}, {}, {}, {}
        for oc in range(16):
            bank0 = (oc % 2) * 4
            slot = q_ld_i % 2
            q_ld_i += 1
            lw = [(self.sPE, st["wq_pe"][slot])] if st["wq_pe"][slot] else []
            t_wl[oc] = ph.step("sync", waits=lw, inc=self.sLD, inck=16)(
                lambda eng, oc=oc, slot=slot: eng.dma_start(
                    wqb[slot][:, :].rearrange("p (c j) -> p c j", c=DC),
                    self.WO[:, :, oc * 128:oc * 128 + 128]
                    .rearrange("c p j -> p c j")))
            xw = [(self.sDV, t_ev[oc - 2])] if oc >= 2 else []
            t_xr[oc] = ph.step("sync", waits=xw, inc=self.sLD, inck=16)(
                lambda eng, oc=oc: eng.dma_start(xrb[oc % 2][:, :],
                                                 SRC[oc, :, :]))
            mw = [(self.sLD, t_wl[oc])]
            if oc == 0:
                mw.append((self.sDV, t_cdone))
            if oc >= 2:
                mw.append((self.sDV, t_ev[oc - 2]))
            else:
                for bx in range(bank0, bank0 + 4):
                    mw += bankw(bx)

            def f_mm(eng, oc=oc, bank0=bank0, slot=slot):
                ins = None
                for kc in range(DC):
                    lhsT = wqb[slot][:, kc * 128:(kc + 1) * 128]
                    for tt in range(4):
                        ins = eng.matmul(
                            pb[bank0 + tt][:, :], lhsT,
                            union[:, kc * T + tt * 512:kc * T + tt * 512 + 512],
                            start=(kc == 0), stop=(kc == DC - 1))
                return ins
            t_mm[oc] = ph.step("tensor", waits=mw, inc=self.sPE)(f_mm)
            st["wq_pe"][slot] = t_mm[oc]

            evw = [(self.sPE, t_mm[oc]), (self.sLD, t_xr[oc])]
            if oc >= 2:
                evw.append((self.sST, t_st[oc - 2]))

            def f_ev(eng, oc=oc, bank0=bank0):
                ins = None
                for tt in range(4):
                    ins = eng.tensor_add(stg[oc % 2][:, tt * 512:(tt + 1) * 512],
                                         pb[bank0 + tt][:, :],
                                         xrb[oc % 2][:, tt * 512:(tt + 1) * 512])
                return ins
            t_ev[oc] = ph.step("vector", waits=evw, inc=self.sDV)(f_ev)
            t_st[oc] = ph.step("scalar", waits=[(self.sDV, t_ev[oc])],
                               inc=self.sST, inck=16)(
                lambda eng, oc=oc: eng.dma_start(DST[oc, :, :],
                                                 stg[oc % 2][:, :]))

        self.drain(ph)
        ph.flush()
        if "adbg" in self.dbg:
            self.dump_sbuf("dbg_aun", union[:, :], [P, AH * T], BF16)
            self.dump_sbuf("dbg_akf", kfm[:, :], [P, 4 * T], BF16)
            self.dump_sbuf("dbg_avt", vtm[:, :], [P, 16 * 512], BF16)
        ctx.close()

    def dump_sbuf(self, name, ap, shp, dt=F32):
        out = self.nc.dram_tensor(name, list(shp), dt, kind="ExternalOutput")
        ph = self.phase(f"dbg_{name}")
        self.sGP.v += 16
        tgt = self.sGP.v

        def f(eng):
            eng.dma_start(out[:, :], ap).then_inc(self.sGP.h, 16)
            eng.wait_ge(self.sGP.h, tgt)
        ph.steps["gpsimd"].append(f)
        ph.flush()

    def dump_dram(self, name, src, shp, dt=F32):
        nc = self.nc
        out = nc.dram_tensor(name, list(shp), dt, kind="ExternalOutput")
        n0 = int(np.prod(shp[:-1]))
        with nc.sbuf_tensor(f"dbg_{name}", [P, shp[-1]], dt) as tile:
            ph = self.phase(f"dbgd_{name}")

            def f(eng):
                v0 = self.sGP.v
                k = 0
                if len(shp) == 3:
                    src_f = src[:, :, :].rearrange("a p t -> (a p) t")
                    out_f = out[:, :, :].rearrange("a p t -> (a p) t")
                else:
                    src_f = src[:, :]
                    out_f = out[:, :]
                for r0 in range(0, n0, P):
                    rr = min(P, n0 - r0)
                    eng.dma_start(tile[0:rr, :], src_f[r0:r0 + rr, :]).then_inc(self.sGP.h, 16)
                    eng.wait_ge(self.sGP.h, v0 + 32 * k + 16)
                    eng.dma_start(out_f[r0:r0 + rr, :], tile[0:rr, :]).then_inc(self.sGP.h, 16)
                    eng.wait_ge(self.sGP.h, v0 + 32 * k + 32)
                    k += 1
                self.sGP.v = v0 + 32 * k
            ph.steps["gpsimd"].append(f)
            ph.flush()

    def finalize_noop_out(self):
        nc = self.nc
        with nc.sbuf_tensor("fin_tile", [P, T], F32) as tile:
            ph = self.phase("fin")

            def f(eng):
                v0 = self.sGP.v
                eng.memset(tile[:, :], 0.0)
                for c in range(DC):
                    eng.dma_start(self.OUT[c, :, :], tile[:, :]).then_inc(self.sGP.h, 16)
                eng.wait_ge(self.sGP.h, v0 + 16 * DC)
                self.sGP.v = v0 + 16 * DC
            ph.steps["gpsimd"].append(f)
            ph.flush()


def build_p12(dbg=(), nphase=2):
    b = Builder(dbg)
    b.p0_consts()
    with ExitStack() as mctx:
        dt_all = mctx.enter_context(b.nc.sbuf_tensor("dt_all", [HEADS, T], F32))
        with ExitStack() as ictx:
            xlnm = ictx.enter_context(b.nc.sbuf_tensor("xlnm", [P, DC * T], BF16))
            b.p1_mamba_ln(xlnm, ictx)
            if "xlnm" in dbg:
                b.dump_sbuf("dbg_xlnm", xlnm[:, :], [P, DC * T], BF16)
            b.p2_in_proj(xlnm, dt_all, ictx)
            if "dt" in dbg:
                b.dump_sbuf("dbg_dt", dt_all[:, :], [HEADS, T], F32)
        if nphase >= 3:
            b.p3_ssd(dt_all)
        if nphase >= 4:
            b.p4_out_proj()
    if nphase >= 5:
        b.p_ffn(b.X1, b.X2, "f1")
    if nphase >= 6:
        b.p6_attn(b.X2, b.X3)
    if nphase >= 7:
        b.p_ffn(b.X3, b.OUT, "f2")
    if "xr" in dbg:
        b.dump_dram("dbg_xr", b.XR, [DC, P, T])
    if "proj" in dbg:
        b.dump_dram("dbg_proj", b.PROJD, [66, P, T], BF16)
    if "ynorm" in dbg:
        b.dump_dram("dbg_ynorm", b.YNORM, [32, P, T], BF16)
    if "x1" in dbg:
        b.dump_dram("dbg_x1", b.X1, [DC, P, T])
    if "x2" in dbg:
        b.dump_dram("dbg_x2", b.X2, [DC, P, T])
    if "x3" in dbg:
        b.dump_dram("dbg_x3", b.X3, [DC, P, T])
    if nphase >= 7:
        b.g.close()
        return b
    b.finalize_noop_out()
    b.g.close()
    return b


def kernel(**inputs):
    consts = host_constants()
    b = build_p12(nphase=7)
    in_maps = [prep_core_inputs(c, inputs, consts) for c in range(NCORES)]
    res = run_bass_kernel_spmd(b.nc, in_maps, core_ids=list(range(NCORES)))
    outs = []
    for c in range(NCORES):
        o = np.asarray(res.results[c]["out"], np.float32).reshape(T, T)
        outs.append(np.ascontiguousarray(o.T))
    return np.stack(outs)


# version 06
# speedup vs baseline: 1.0744x; 1.0744x over previous
import sys
sys.path.insert(0, "/opt/trn_rl_repo")
from collections import defaultdict
from contextlib import ExitStack

import numpy as np
import ml_dtypes

import concourse.bass as bass
import concourse.mybir as mybir
from concourse.bass_utils import run_bass_kernel_spmd

F32 = mybir.dt.float32
BF16 = mybir.dt.bfloat16
AF = mybir.ActivationFunctionType
ALU = mybir.AluOpType
AX = mybir.AxisListType

P = 128
T = 2048
DC = 16
NCORES = 8
EPS = 1e-5

INTER = 4096
CONV_DIM = INTER + 256
PROJ = INTER + CONV_DIM + 64
HEADS = 64
HD = 64
STATE = 128
NCH = 16
AH = 16
AKV = 4
AHD = 128
E = 8
I_MOE = 1408
I_SH = 5632
NEG_BIG = -1e30
DETECT_RACES = True
P3_CHUNKS = NCH
FFN_HALVES = 2
FFN_NEXP = 12
HAMMER = False
P3_ABLATE = ()
HAMMER_ENGINES = ("sync", "tensor", "scalar", "vector", "gpsimd")


def _hi(x):
    x = np.ascontiguousarray(np.asarray(x, dtype=np.float32))
    return np.ascontiguousarray(x.view(np.uint16)[..., 1::2]).view(ml_dtypes.bfloat16)


def _kt(w):
    K, O = w.shape
    assert K % P == 0
    return np.ascontiguousarray(w.reshape(K // P, P, O))


def bankw_list(st, b):
    g = st["bank"][b]
    return [g] if g is not None else []


class Sem:
    __slots__ = ("h", "v")

    def __init__(self, ctx, nc, name):
        self.h = ctx.enter_context(nc.semaphore(name))
        self.v = 0


class Phase:

    ENGINES = ("sync", "tensor", "scalar", "vector", "gpsimd")

    def __init__(self, b, name):
        self.b = b
        self.name = name
        self.steps = defaultdict(list)

    def step(self, engine, waits=(), inc=None, inck=1):
        if HAMMER and engine in HAMMER_ENGINES:
            b = self.b
            waits = list(waits) + [(s, s.v) for s in
                                   (b.sLD, b.sST, b.sPE, b.sDV, b.sAC, b.sGP)]
        seen = self.b.waited[engine]
        fw = []
        for s, v in waits:
            if seen[s] < v:
                fw.append((s, v))
                seen[s] = v
        waits = fw
        tgt = None
        if inc is not None:
            inc.v += inck
            tgt = inc.v

        def deco(fn):
            def wrapped(eng, fn=fn, waits=tuple(waits), inc=inc, inck=inck):
                for s, v in waits:
                    eng.wait_ge(s.h, v)
                last = fn(eng)
                if inc is not None:
                    assert last is not None, f"step in {self.name} returned no inst"
                    last.then_inc(inc.h, inck)
            self.steps[engine].append(wrapped)
            return tgt
        return deco

    def flush(self):
        with self.b.nc.named_scope(self.name):
            with self.b.nc.Block(self.name) as blk:
                for en in self.ENGINES:
                    fns = self.steps[en]
                    if fns:
                        getattr(blk, en)(lambda eng, fns=fns: [f(eng) for f in fns])
        self.steps = defaultdict(list)



def host_constants():
    c = {}
    c["ident_f32"] = np.eye(P, dtype=np.float32)
    c["ident_bf"] = np.eye(P, dtype=np.float32).astype(ml_dtypes.bfloat16)
    c["ones_row"] = np.ones((1, P), dtype=np.float32)
    c["ones_col_bf"] = np.ones((P, 1), dtype=ml_dtypes.bfloat16)
    c["ones_row_bf"] = np.ones((1, P), dtype=ml_dtypes.bfloat16)
    c["zeros64"] = np.zeros((HEADS, P), dtype=np.float32)
    kk = np.arange(P)[:, None]
    qq = np.arange(P)[None, :]
    c["tri_mask"] = np.where(kk > qq, NEG_BIG, 0.0).astype(np.float32)
    c["lt_incl"] = np.triu(np.ones((P, P), dtype=np.float32))
    slopes = 2.0 ** (-8.0 * np.arange(1, AH + 1) / AH)
    bias = np.zeros((P, AH * 16), dtype=np.float32)
    for h in range(AH):
        for jc in range(16):
            bias[:, h * 16 + jc] = slopes[h] * (np.arange(P) + 128.0 * (jc - 12))
    c["alibi"] = bias
    bf = ml_dtypes.bfloat16
    qn = np.zeros((AH, 512), dtype=np.float32)
    for h in range(AH):
        qn[h] = -slopes[h] * np.arange(512)
    c["qneg"] = qn.astype(bf)
    half = AHD // 2
    theta = 1.0 / (10000.0 ** (np.arange(half, dtype=np.float64) / half))
    ang = np.outer(theta, np.arange(T, dtype=np.float64))
    cos2 = np.concatenate([np.cos(ang), np.cos(ang)], 0)
    sin2 = np.concatenate([-np.sin(ang), np.sin(ang)], 0)
    sc = 1.0 / np.sqrt(AHD)
    c["cos_k"] = cos2.astype(np.float32).astype(bf)
    c["sin_k"] = sin2.astype(np.float32).astype(bf)
    c["cos_q"] = (cos2 * sc).astype(np.float32).astype(bf)
    c["sin_q"] = (sin2 * sc).astype(np.float32).astype(bf)
    rsw = np.zeros((P, P), dtype=np.float32)
    rsw[(np.arange(P) + 64) % P, np.arange(P)] = 1.0
    c["rsw"] = rsw.astype(bf)
    return c


def prep_core_inputs(core, inputs, consts):
    m = {}
    x = np.asarray(inputs["L0_attn_a_hidden_states"][core], dtype=np.float32)
    m["x0"] = np.ascontiguousarray(x.T).reshape(DC, P, T)

    m["w_in"] = _hi(_kt(np.ascontiguousarray(np.asarray(inputs["L0_attn_a_in_proj_w"], np.float32).T)))
    m["w_out"] = _hi(_kt(np.ascontiguousarray(np.asarray(inputs["L0_attn_a_out_proj_w"], np.float32).T)))
    m["norm_w"] = np.ascontiguousarray(
        np.asarray(inputs["L0_attn_a_norm_weight"], np.float32).reshape(DC, P).T)
    m["gnw"] = np.ascontiguousarray(
        np.asarray(inputs["L0_attn_a_gate_norm_weight"], np.float32).reshape(32, P).T)
    m["conv_w"] = np.ascontiguousarray(
        np.asarray(inputs["L0_attn_a_conv_w"], np.float32).reshape(34, P, 4)
        .transpose(1, 0, 2).reshape(P, 136))
    m["conv_b"] = np.ascontiguousarray(
        np.asarray(inputs["L0_attn_a_conv_b"], np.float32).reshape(34, P).T)
    m["dt_bias"] = np.ascontiguousarray(np.asarray(inputs["L0_attn_a_dt_bias"], np.float32).reshape(HEADS, 1))
    m["a_log"] = np.ascontiguousarray(np.asarray(inputs["L0_attn_a_A_log"], np.float32).reshape(HEADS, 1))
    m["d_vec"] = np.ascontiguousarray(np.asarray(inputs["L0_attn_a_D"], np.float32).reshape(HEADS, 1))

    m["wq"] = _hi(_kt(np.asarray(inputs["L0_attn_b_Wq"], np.float32)))
    m["wk"] = _hi(_kt(np.asarray(inputs["L0_attn_b_Wk"], np.float32)))
    m["wv"] = _hi(_kt(np.asarray(inputs["L0_attn_b_Wv"], np.float32)))
    m["wo"] = _hi(_kt(np.asarray(inputs["L0_attn_b_Wo"], np.float32)))

    gu = np.asarray(inputs["L0_ffn_gate_up_proj"], np.float32)
    dn = np.asarray(inputs["L0_ffn_down_proj"], np.float32)
    m["w_gu"] = np.ascontiguousarray(np.stack([_hi(_kt(np.ascontiguousarray(gu[e].T))) for e in range(E)]))
    m["w_dn"] = np.ascontiguousarray(np.stack([_hi(_kt(np.ascontiguousarray(dn[e].T))) for e in range(E)]))
    m["w_shg"] = _hi(_kt(np.ascontiguousarray(np.asarray(inputs["L0_ffn_sh_gate_w"], np.float32).T)))
    m["w_shu"] = _hi(_kt(np.ascontiguousarray(np.asarray(inputs["L0_ffn_sh_up_w"], np.float32).T)))
    m["w_shd"] = _hi(_kt(np.ascontiguousarray(np.asarray(inputs["L0_ffn_sh_down_w"], np.float32).T)))
    m["w_rt"] = _hi(_kt(np.ascontiguousarray(np.asarray(inputs["L0_ffn_router_weight"], np.float32).T)))
    m["w_sg"] = _hi(np.ascontiguousarray(
        np.asarray(inputs["L0_ffn_shared_gate_w"], np.float32).reshape(DC, P).T))
    m.update(consts)
    return m



class Builder:
    def __init__(self, dbg=()):
        self.dbg = set(dbg)
        nc = bass.Bass("TRN2", target_bir_lowering=False,
                       detect_race_conditions=DETECT_RACES)
        self.nc = nc
        g = ExitStack()
        self.g = g

        di = lambda n, shp, dt=F32: nc.dram_tensor(n, shp, dt, kind="ExternalInput")
        dint = lambda n, shp, dt=F32: nc.dram_tensor(n, shp, dt)
        self.X0 = di("x0", [DC, P, T])
        self.W_IN = di("w_in", [DC, P, PROJ], BF16)
        self.W_OUT = di("w_out", [32, P, 2048], BF16)
        self.NORM_W = di("norm_w", [P, DC])
        self.GNW = di("gnw", [P, 32])
        self.CONV_W = di("conv_w", [P, 136])
        self.CONV_B = di("conv_b", [P, 34])
        self.DT_BIAS = di("dt_bias", [HEADS, 1])
        self.A_LOG = di("a_log", [HEADS, 1])
        self.D_VEC = di("d_vec", [HEADS, 1])
        self.WQ = di("wq", [DC, P, 2048], BF16)
        self.WK = di("wk", [DC, P, 512], BF16)
        self.WV = di("wv", [DC, P, 512], BF16)
        self.WO = di("wo", [DC, P, 2048], BF16)
        self.W_GU = di("w_gu", [E, DC, P, 2816], BF16)
        self.W_DN = di("w_dn", [E, 11, P, 2048], BF16)
        self.W_SHG = di("w_shg", [DC, P, I_SH], BF16)
        self.W_SHU = di("w_shu", [DC, P, I_SH], BF16)
        self.W_SHD = di("w_shd", [44, P, 2048], BF16)
        self.W_RT = di("w_rt", [DC, P, E], BF16)
        self.W_SG = di("w_sg", [P, DC], BF16)
        self.IDENT = di("ident_f32", [P, P])
        self.IDENT_BF = di("ident_bf", [P, P], BF16)
        self.ONES_ROW = di("ones_row", [1, P])
        self.ONES_COL_BF = di("ones_col_bf", [P, 1], BF16)
        self.ONES_ROW_BF = di("ones_row_bf", [1, P], BF16)
        self.ZEROS64 = di("zeros64", [HEADS, P])
        self.TRI_MASK = di("tri_mask", [P, P])
        self.LT_INCL = di("lt_incl", [P, P])
        self.ALIBI = di("alibi", [P, AH * 16])
        self.QNEG = di("qneg", [AH, 512], BF16)
        self.COS_K = di("cos_k", [P, T], BF16)
        self.SIN_K = di("sin_k", [P, T], BF16)
        self.COS_Q = di("cos_q", [P, T], BF16)
        self.SIN_Q = di("sin_q", [P, T], BF16)
        self.RSW = di("rsw", [P, P], BF16)

        self.XR = dint("xr", [DC, P, T])
        self.X1 = dint("x1", [DC, P, T])
        self.X2 = dint("x2", [DC, P, T])
        self.X3 = dint("x3", [DC, P, T])
        self.PROJD = dint("projd", [66, P, T], BF16)
        self.YNORM = dint("ynormd", [32, P, T], BF16)
        self.OUT = nc.dram_tensor("out", [DC, P, T], F32, kind="ExternalOutput")

        sb = lambda n, shp, dt=F32: g.enter_context(nc.sbuf_tensor(n, shp, dt))
        self.c_ident = sb("c_ident", [P, P])
        self.c_ident_bf = sb("c_ident_bf", [P, P], BF16)
        self.c_ones_row = sb("c_ones_row", [1, P])
        self.c_ones_col_bf = sb("c_ones_col_bf", [P, 1], BF16)
        self.c_ones_row_bf = sb("c_ones_row_bf", [1, P], BF16)
        self.c_zeros64 = sb("c_zeros64", [HEADS, P])
        self.c_tri = sb("c_tri", [P, P])
        self.c_lt = sb("c_lt", [P, P])
        self.c_alibi = sb("c_alibi", [P, AH * 16])
        self.c_norm_w = sb("c_norm_w", [P, DC])
        self.c_gnw = sb("c_gnw", [P, 32])
        self.c_conv_w = sb("c_conv_w", [P, 136])
        self.c_conv_b = sb("c_conv_b", [P, 34])
        self.c_dt_bias = sb("c_dt_bias", [HEADS, 1])
        self.c_acol = sb("c_acol", [HEADS, 1])
        self.c_dext = sb("c_dext", [P, 32])
        self.c_abc = sb("c_abc", [P, HEADS])
        self.c_eps = sb("c_eps", [P, 1])
        self.c_one = sb("c_one", [P, 1])

        self.waited = defaultdict(lambda: defaultdict(int))
        self.sLD = Sem(g, nc, "sLD")
        self.sST = Sem(g, nc, "sST")
        self.sPE = Sem(g, nc, "sPE")
        self.sDV = Sem(g, nc, "sDV")
        self.sAC = Sem(g, nc, "sAC")
        self.sGP = Sem(g, nc, "sGP")

        self.final_store_tgt = None

    def phase(self, name):
        return Phase(self, name)

    def drain(self, ph):
        for en, sem in (("sync", self.sLD), ("scalar", self.sST), ("gpsimd", self.sGP)):
            v = sem.v
            ph.steps[en].append(lambda eng, s=sem, v=v: eng.wait_ge(s.h, v))

    def p0_consts(self):
        nc = self.nc
        ph = self.phase("p0")
        tmp_a = self.g.enter_context(nc.sbuf_tensor("tmp_a", [HEADS, 1], F32))
        tmp_d = self.g.enter_context(nc.sbuf_tensor("tmp_d", [HEADS, 1], F32))
        tmp_drow = self.g.enter_context(nc.sbuf_tensor("tmp_drow", [1, HEADS], F32))
        tmp_dbc = self.g.enter_context(nc.sbuf_tensor("tmp_dbc", [P, HEADS], F32))
        p0ctx = ExitStack()
        ps = p0ctx.enter_context(nc.psum_tensor("p0ps", [P, 512], F32))
        loads = [
            (self.c_ident, self.IDENT), (self.c_ident_bf, self.IDENT_BF),
            (self.c_ones_row, self.ONES_ROW), (self.c_ones_col_bf, self.ONES_COL_BF),
            (self.c_ones_row_bf, self.ONES_ROW_BF),
            (self.c_zeros64, self.ZEROS64), (self.c_tri, self.TRI_MASK),
            (self.c_alibi, self.ALIBI), (self.c_norm_w, self.NORM_W),
            (self.c_lt, self.LT_INCL),
            (self.c_gnw, self.GNW), (self.c_conv_w, self.CONV_W),
            (self.c_conv_b, self.CONV_B), (self.c_dt_bias, self.DT_BIAS),
            (tmp_a, self.A_LOG), (tmp_d, self.D_VEC),
        ]
        def do_loads(eng):
            for dst, src in loads:
                eng.dma_start(dst[:], src[:]).then_inc(self.sLD.h, 16)
        ph.steps["sync"].append(do_loads)
        ph.steps["vector"].append(lambda eng: (eng.memset(self.c_eps[:, :], EPS),
                                               eng.memset(self.c_one[:, :], 1.0))[-1])
        self.sLD.v += 16 * len(loads)
        t_ld = self.sLD.v

        t_exp = ph.step("scalar", waits=[(self.sLD, t_ld)], inc=self.sAC)(
            lambda eng: eng.activation(self.c_acol[:], tmp_a[:], AF.Exp))
        t_neg = ph.step("vector", waits=[(self.sAC, t_exp)], inc=self.sDV)(
            lambda eng: eng.tensor_scalar_mul(self.c_acol[:], self.c_acol[:], -1.0))

        t_tp = ph.step("tensor", waits=[(self.sLD, t_ld)], inc=self.sPE)(
            lambda eng: eng.transpose(ps[0:1, 0:HEADS], tmp_d[:],
                                      self.c_ident[0:HEADS, 0:HEADS]))
        t_ev = ph.step("vector", waits=[(self.sPE, t_tp)], inc=self.sDV)(
            lambda eng: eng.tensor_copy(tmp_drow[0:1, 0:HEADS], ps[0:1, 0:HEADS]))
        t_bc = ph.step("tensor", waits=[(self.sDV, t_ev)], inc=self.sPE)(
            lambda eng: eng.matmul(ps[:, 64:64 + HEADS], self.c_ones_row[:],
                                   tmp_drow[0:1, 0:HEADS], start=True, stop=True))

        t_dx0 = ph.step("vector", waits=[(self.sPE, t_bc)], inc=self.sDV)(
            lambda eng: eng.tensor_copy(tmp_dbc[:, :], ps[:, 64:64 + HEADS]))
        t_dx1 = ph.step("scalar", waits=[(self.sDV, t_dx0)], inc=self.sAC)(
            lambda eng: eng.engine_nop() if False else eng.copy(tmp_drow[0:1, 0:HEADS],
                                                               tmp_dbc[0:1, 0:HEADS]))

        def f_dext(eng):
            eng.tensor_copy(self.c_dext[0:64, :], tmp_dbc[0:64, 0:HEADS:2])
            return eng.tensor_copy(self.c_dext[64:128, :], tmp_dbc[64:128, 1:HEADS:2])
        t_dx = ph.step("vector", waits=[(self.sAC, t_dx1)], inc=self.sDV)(f_dext)

        t_at = ph.step("tensor", waits=[(self.sDV, t_neg), (self.sDV, t_dx)],
                       inc=self.sPE)(
            lambda eng: eng.transpose(ps[0:1, 192:192 + HEADS], self.c_acol[:, :],
                                      self.c_ident[0:HEADS, 0:HEADS]))
        t_ar = ph.step("vector", waits=[(self.sPE, t_at)], inc=self.sDV)(
            lambda eng: eng.tensor_copy(tmp_drow[0:1, 0:HEADS], ps[0:1, 192:192 + HEADS]))
        t_ab = ph.step("tensor", waits=[(self.sDV, t_ar)], inc=self.sPE)(
            lambda eng: eng.matmul(ps[:, 256:256 + HEADS], self.c_ones_row[:],
                                   tmp_drow[0:1, 0:HEADS], start=True, stop=True))
        ph.step("vector", waits=[(self.sPE, t_ab)], inc=self.sDV)(
            lambda eng: eng.tensor_copy(self.c_abc[:, :], ps[:, 256:256 + HEADS]))
        ph.flush()
        p0ctx.close()

    def p1_mamba_ln(self, xlnm, outer_ctx):
        nc = self.nc
        ctx = ExitStack()
        sb = lambda n, shp, dt=F32: ctx.enter_context(nc.sbuf_tensor(n, shp, dt))
        pst = lambda n: ctx.enter_context(nc.psum_tensor(n, [P, 512], F32))
        xt = sb("p1_xt", [P, DC * 512])
        xbf = sb("p1_xbf", [P, DC * 512], BF16)
        sqbf = sb("p1_sq", [P, DC * 512], BF16)
        t2 = sb("p1_t2", [P, DC * 512])
        mbc = sb("p1_mbc", [P, 3 * 512])
        rows = sb("p1_rows", [1, 6 * 512])
        ps_sx, ps_sxx, ps_m, ps_r, ps_v2, ps_r2 = (
            pst("p1_sx"), pst("p1_sxx"), pst("p1_m"), pst("p1_r"),
            pst("p1_v2"), pst("p1_r2"))
        m_row = rows[0:1, 0:512]
        v_row = rows[0:1, 512:1024]
        lv_row = rows[0:1, 1024:1536]
        r_row = rows[0:1, 1536:2048]
        r2s = rows[0:1, 2048:2560]

        prev = {}
        ph = self.phase("p1")
        for ti in range(4):
            t0 = ti * 512

            t_ld = ph.step("sync",
                           waits=([(self.sST, prev["xr"])] if "xr" in prev else []),
                           inc=self.sLD, inck=16)(
                lambda eng, t0=t0: eng.dma_start(
                    xt[:, :].rearrange("p (c j) -> p c j", c=DC),
                    self.X0[:, :, t0:t0 + 512].rearrange("c p j -> p c j")))

            def f_cast(eng):
                for c in range(DC):
                    sl = slice(c * 512, (c + 1) * 512)
                    eng.tensor_copy(xbf[:, sl], xt[:, sl])
                for c in range(DC):
                    sl = slice(c * 512, (c + 1) * 512)
                    ins = eng.tensor_mul(sqbf[:, sl], xbf[:, sl], xbf[:, sl])
                return ins
            t_cast = ph.step("vector", waits=[(self.sLD, t_ld)] +
                             ([(self.sPE, prev["sxx"])] if "sxx" in prev else []),
                             inc=self.sDV)(f_cast)

            def f_stats(eng):
                for c in range(DC):
                    sl = slice(c * 512, (c + 1) * 512)
                    eng.matmul(ps_sx[0:1, :], self.c_ones_col_bf[:], xbf[:, sl],
                               start=(c == 0), stop=(c == DC - 1))
                for c in range(DC):
                    sl = slice(c * 512, (c + 1) * 512)
                    ins = eng.matmul(ps_sxx[0:1, :], self.c_ones_col_bf[:], sqbf[:, sl],
                                     start=(c == 0), stop=(c == DC - 1))
                return ins
            t_sxx = ph.step("tensor", waits=[(self.sDV, t_cast)] +
                            ([(self.sDV, prev["rows"])] if "rows" in prev else []),
                            inc=self.sPE)(f_stats)
            prev["sxx"] = t_sxx

            def f_rows(eng):
                eng.tensor_scalar_mul(m_row, ps_sx[0:1, :], 1.0 / 2048)
                eng.tensor_mul(v_row, m_row, m_row)
                return eng.scalar_tensor_tensor(v_row, ps_sxx[0:1, :], 1.0 / 2048,
                                                v_row, ALU.mult, ALU.subtract)
            t_rows = ph.step("vector", waits=[(self.sPE, t_sxx)], inc=self.sDV)(f_rows)
            prev["rows"] = t_rows

            def f_rstd(eng):
                eng.activation(lv_row, v_row, AF.Ln, bias=self.c_eps[0:1, 0:1])
                return eng.activation(r_row, lv_row, AF.Exp, scale=-0.5)
            t_rstd = ph.step("scalar", waits=[(self.sDV, t_rows)], inc=self.sAC)(f_rstd)

            def f_bc(eng):
                eng.matmul(ps_m[:, :], self.c_ones_row[:], m_row, start=True, stop=True)
                return eng.matmul(ps_r[:, :], self.c_ones_row[:], r_row, start=True, stop=True)
            t_bc = ph.step("tensor", waits=[(self.sAC, t_rstd)] +
                           ([(self.sDV, prev["mc"])] if "mc" in prev else []),
                           inc=self.sPE)(f_bc)

            def f_mc(eng):
                eng.tensor_copy(mbc[:, 0:512], ps_m[:, :])
                return eng.tensor_copy(mbc[:, 512:1024], ps_r[:, :])
            t_mc = ph.step("vector", waits=[(self.sPE, t_bc)], inc=self.sDV)(f_mc)
            prev["mc"] = t_mc

            def f_norm(eng):
                for c in range(DC):
                    sl = slice(c * 512, (c + 1) * 512)
                    eng.tensor_sub(t2[:, sl], xt[:, sl], mbc[:, 0:512])
                    eng.tensor_mul(t2[:, sl], t2[:, sl], mbc[:, 512:1024])
                    eng.tensor_add(xt[:, sl], t2[:, sl], xt[:, sl])
                for c in range(DC):
                    sl = slice(c * 512, (c + 1) * 512)
                    ins = eng.tensor_mul(sqbf[:, sl], t2[:, sl], t2[:, sl])
                return ins
            t_norm = ph.step("vector", waits=[], inc=self.sDV)(f_norm)
            prev["norm"] = t_norm

            t_xr = ph.step("scalar", waits=[(self.sDV, t_norm)], inc=self.sST, inck=16)(
                lambda eng, t0=t0: eng.dma_start(
                    self.XR[:, :, t0:t0 + 512].rearrange("c p j -> p c j"),
                    xt[:, :].rearrange("p (c j) -> p c j", c=DC)))
            prev["xr"] = t_xr

            def f_v2(eng):
                for c in range(DC):
                    sl = slice(c * 512, (c + 1) * 512)
                    ins = eng.matmul(ps_v2[0:1, :], self.c_ones_col_bf[:], sqbf[:, sl],
                                     start=(c == 0), stop=(c == DC - 1))
                return ins
            t_v2 = ph.step("tensor", waits=[(self.sDV, t_norm)] +
                           ([(self.sAC, prev["r2"])] if "r2" in prev else []),
                           inc=self.sPE)(f_v2)

            def f_r2(eng):
                eng.activation(lv_row, ps_v2[0:1, :], AF.Ln,
                               bias=self.c_eps[0:1, 0:1], scale=1.0 / 2048)
                return eng.activation(r2s, lv_row, AF.Exp, scale=-0.5)
            t_r2 = ph.step("scalar", waits=[(self.sPE, t_v2)], inc=self.sAC)(f_r2)
            prev["r2"] = t_r2

            t_bc2 = ph.step("tensor", waits=[(self.sAC, t_r2)] +
                            ([(self.sDV, prev["r2c"])] if "r2c" in prev else []),
                            inc=self.sPE)(
                lambda eng: eng.matmul(ps_r2[:, :], self.c_ones_row[:], r2s,
                                       start=True, stop=True))

            t_r2c = ph.step("vector", waits=[(self.sPE, t_bc2)], inc=self.sDV)(
                lambda eng: eng.tensor_copy(mbc[:, 1024:1536], ps_r2[:, :]))
            prev["r2c"] = t_r2c

            def f_fin(eng, t0=t0):
                for c in range(DC):
                    sl = slice(c * 512, (c + 1) * 512)
                    ins = eng.scalar_tensor_tensor(
                        xlnm[:, c * T + t0: c * T + t0 + 512], t2[:, sl],
                        self.c_norm_w[:, c:c + 1], mbc[:, 1024:1536],
                        ALU.mult, ALU.mult)
                return ins
            t_fin = ph.step("vector", waits=[], inc=self.sDV)(f_fin)
            prev["fin"] = t_fin
        self.drain(ph)
        ph.flush()
        ctx.close()
        return prev

    def p2_in_proj(self, xlnm, dt_all, outer_ctx):
        nc = self.nc
        ctx = ExitStack()
        sb = lambda n, shp, dt=F32: ctx.enter_context(nc.sbuf_tensor(n, shp, dt))
        wb = [sb(f"p2_w{i}", [P, DC * 512], BF16) for i in range(2)]
        stg = [sb(f"p2_s{i}", [P, T], BF16) for i in range(2)]
        pss = [ctx.enter_context(nc.psum_tensor(f"p2_ps{i}", [P, 512], F32))
               for i in range(8)]

        ph = self.phase("p2")
        ogs = [(i * 512, 512) for i in range(16)] + [(8192, 320)]
        t_wl = {}
        t_mm_last = {}
        t_ev = {}
        t_st = {}
        oc_global = 0
        for ogi, (o0, ow) in enumerate(ogs):
            waits = []
            if ogi >= 2:
                waits.append((self.sPE, t_mm_last[ogi - 2]))
            t_wl[ogi] = ph.step("sync", waits=waits, inc=self.sLD, inck=16)(
                lambda eng, o0=o0, ow=ow, ogi=ogi: eng.dma_start(
                    wb[ogi % 2][:, 0:DC * ow].rearrange("p (c j) -> p c j", c=DC),
                    self.W_IN[:, :, o0:o0 + ow].rearrange("c p j -> p c j")))

            nfull = ow // 128
            for oci in range(nfull + (1 if ow % 128 else 0)):
                mrows = 64 if (ogi == 16 and oci == 2) else 128
                bank0 = (oc_global % 2) * 4
                mywaits = [(self.sLD, t_wl[ogi])]
                if oc_global >= 2:
                    mywaits.append((self.sDV, t_ev[oc_global - 2]))

                def f_mm2(eng, ogi=ogi, oci=oci, ow=ow, mrows=mrows, bank0=bank0):
                    ins = None
                    for kc in range(DC):
                        base = kc * ow + oci * 128
                        lhsT = wb[ogi % 2][:, base:base + mrows]
                        for tt in range(4):
                            rhs = xlnm[:, kc * T + tt * 512: kc * T + tt * 512 + 512]
                            ins = eng.matmul(pss[bank0 + tt][0:mrows, :], lhsT, rhs,
                                             start=(kc == 0), stop=(kc == DC - 1))
                    return ins
                t_mm = ph.step("tensor", waits=mywaits, inc=self.sPE)(f_mm2)
                t_mm_last[ogi] = t_mm

                evwaits = [(self.sPE, t_mm)]
                if not (ogi == 16 and oci == 2):
                    if oc_global >= 2:
                        evwaits.append((self.sST, t_st[oc_global - 2]))

                    def f_ev(eng, oc_global=oc_global, bank0=bank0):
                        s = stg[oc_global % 2]
                        ins = None
                        for tt in range(4):
                            ins = eng.tensor_copy(s[:, tt * 512:(tt + 1) * 512],
                                                  pss[bank0 + tt][:, :])
                        return ins
                    t_ev[oc_global] = ph.step("vector", waits=evwaits, inc=self.sDV)(f_ev)
                    t_st[oc_global] = ph.step(
                        "scalar", waits=[(self.sDV, t_ev[oc_global])],
                        inc=self.sST, inck=16)(
                        lambda eng, oc=oc_global: eng.dma_start(
                            self.PROJD[oc, :, :], stg[oc % 2][:, :]))
                else:
                    def f_evdt(eng, bank0=bank0):
                        ins = None
                        for tt in range(4):
                            ins = eng.tensor_copy(dt_all[:, tt * 512:(tt + 1) * 512],
                                                  pss[bank0 + tt][0:64, :])
                        return ins
                    t_ev[oc_global] = ph.step("vector", waits=evwaits, inc=self.sDV)(f_evdt)
                oc_global += 1
        self.drain(ph)
        ph.flush()
        ctx.close()
        return {"done_dv": self.sDV.v, "done_st": self.sST.v}

    def p3_ssd(self, dt_all):
        nc = self.nc
        ctx = ExitStack()
        sb = lambda n, shp, dt=F32: ctx.enter_context(nc.sbuf_tensor(n, shp, dt))
        pst = lambda n, dt=F32, w=512: ctx.enter_context(nc.psum_tensor(n, [P, w], dt))

        xbc_in = [sb(f"p3_xin{i}", [P, 34 * 131], BF16) for i in range(2)]
        gate_in = [sb(f"p3_gin{i}", [P, 32 * 128], BF16) for i in range(2)]
        conv_raw = sb("p3_craw", [P, 34 * 128])
        xbc_act = sb("p3_act", [P, 34 * 128])
        sg = sb("p3_sg", [P, 32 * 128], BF16)
        xs_bf = sb("p3_xsbf", [P, 32 * 128], BF16)
        dtxT = sb("p3_dtxT", [P, 32 * 128], BF16)
        y2bf = sb("p3_y2", [P, 32 * 128], BF16)
        S_sb = sb("p3_S", [P, 32 * 128], BF16)
        tp_scr = sb("p3_tps", [P, 128])
        sq_full = sb("p3_sqf", [P, 32 * 128], BF16)
        G_sl = [sb(f"p3_G{i}", [P, 128], BF16) for i in range(4)]
        M_sl = [sb(f"p3_M{i}", [P, 128], BF16) for i in range(4)]
        Bd_sl = [sb(f"p3_Bd{i}", [P, 128], BF16) for i in range(4)]
        Ce_bf = sb("p3_Ce", [P, 512], BF16)
        B_bf = sb("p3_Bbf", [P, 128], BF16)
        C_bf = sb("p3_Cbf", [P, 128], BF16)
        BT_bf = sb("p3_BT", [P, 128], BF16)
        CB_bf = sb("p3_CB", [P, 128], BF16)
        C_dup = sb("p3_Cdup", [P, 512])
        dtw = sb("p3_dtw", [HEADS, 3 * 128])
        cum_hs = sb("p3_cumhs", [HEADS, 128])
        exp_hs = sb("p3_exphs", [HEADS, 128])
        dtT_sb = sb("p3_dtT", [P, 64])
        ldA_T = sb("p3_ldAT", [P, HEADS])
        cum_T = sb("p3_cumT", [P, HEADS])
        ncum_T = sb("p3_ncumT", [P, HEADS])
        dtmp64 = sb("p3_dtmp64", [P, HEADS])
        dcol = sb("p3_dcol", [P, 64])
        e_cols = sb("p3_ecols", [P, 64])
        cl_row = sb("p3_clrow", [1, 64])
        rows = sb("p3_rows", [1, 256])
        cum_rows = sb("p3_cumrows", [1, HEADS * 128])
        ec_rows = sb("p3_ecrows", [1, HEADS * 128], BF16)
        D_sl = [sb(f"p3_D{i}", [P, 128]) for i in range(4)]

        b01 = [pst("p3_b0"), pst("p3_b1")]
        b23 = [pst("p3_b2"), pst("p3_b3")]
        b45 = [pst("p3_b4"), pst("p3_b5")]
        b6 = pst("p3_b6", BF16, 1024)
        b7 = pst("p3_b7")

        z_sl = dtw[:, 0:128]
        sf_sl = dtw[:, 128:256]
        l1_sl = dtw[:, 256:384]
        rstd_row = rows[0:1, 0:128]
        lnv_row = rows[0:1, 128:256]

        ph = self.phase("p3")
        prev = {}

        t_ms = ph.step("vector", inc=self.sDV)(
            lambda eng: (eng.memset(xbc_in[0][:, :], 0.0),
                         eng.memset(xbc_in[1][:, :], 0.0))[-1])

        def emit_loads(ci, extra_waits):
            t0 = ci * 128
            self.sLD.v += 32
            tgt = self.sLD.v

            def f_ld(eng, ci=ci, t0=t0, w=tuple(extra_waits)):
                for s, v in w:
                    eng.wait_ge(s.h, v)
                bi = ci % 2
                if ci == 0:
                    dst = xbc_in[0][:, :].rearrange("p (c j) -> p c j", c=34)[:, :, 3:131]
                    srcx = self.PROJD[32:66, :, 0:128]
                else:
                    dst = xbc_in[bi][:, :].rearrange("p (c j) -> p c j", c=34)
                    srcx = self.PROJD[32:66, :, t0 - 3:t0 + 128]
                eng.dma_start(dst, srcx.rearrange("c p j -> p c j")).then_inc(self.sLD.h, 16)
                eng.dma_start(
                    gate_in[bi][:, :].rearrange("p (c j) -> p c j", c=32),
                    self.PROJD[0:32, :, t0:t0 + 128].rearrange("c p j -> p c j"),
                ).then_inc(self.sLD.h, 16)
            ph.steps["sync"].append(f_ld)
            return tgt

        t_ld = {0: emit_loads(0, [(self.sST, self.sST.v), (self.sDV, t_ms)])}

        for ci in range(P3_CHUNKS):
            t0 = ci * 128
            bi = ci % 2

            def f_conv(eng, bi=bi):
                ins = None
                for j in range(4):
                    for cc in range(34):
                        o = cc * 131
                        oo = cc * 128
                        if j == 0:
                            ins = eng.tensor_scalar(
                                conv_raw[:, oo:oo + 128], xbc_in[bi][:, o:o + 128],
                                self.c_conv_w[:, cc * 4:cc * 4 + 1], None, ALU.mult)
                        else:
                            ins = eng.scalar_tensor_tensor(
                                conv_raw[:, oo:oo + 128],
                                xbc_in[bi][:, o + j:o + j + 128],
                                self.c_conv_w[:, cc * 4 + j:cc * 4 + j + 1],
                                conv_raw[:, oo:oo + 128], ALU.mult, ALU.add)
                return ins
            cw = [(self.sLD, t_ld[ci])]
            if "act_done" in prev:
                cw.append((self.sAC, prev["act_done"]))
            t_conv = ph.step("vector", waits=cw, inc=self.sDV)(f_conv)

            def f_silu(eng, bi=bi):
                ins = None
                for cc in range(34):
                    oo = cc * 128
                    dst = xs_bf if cc < 32 else xbc_act
                    eng.activation(dst[:, oo:oo + 128], conv_raw[:, oo:oo + 128],
                                   AF.Silu, bias=self.c_conv_b[:, cc:cc + 1])
                for cc in range(32):
                    oo = cc * 128
                    ins = eng.activation(sg[:, oo:oo + 128], gate_in[bi][:, oo:oo + 128],
                                         AF.Silu)
                return ins
            sw = [(self.sDV, t_conv)]
            if "ynorm_dv" in prev:
                sw.append((self.sDV, prev["ynorm_dv"]))
            t_silu = ph.step("scalar", waits=sw, inc=self.sAC)(f_silu)
            prev["act_done"] = t_silu

            if ci + 1 < P3_CHUNKS:
                t_ld[ci + 1] = emit_loads(ci + 1, [(self.sDV, t_conv)])

            def f_dt1(eng, t0=t0):
                return eng.tensor_scalar(z_sl, dt_all[:, t0:t0 + 128],
                                         self.c_dt_bias[:, 0:1], None, ALU.add)
            t_dt1 = ph.step("vector", waits=[], inc=self.sDV)(f_dt1)

            def f_dt2(eng):
                eng.activation(sf_sl, z_sl, AF.Abs)
                eng.activation(l1_sl, sf_sl, AF.Exp, scale=-1.0)
                return eng.activation(l1_sl, l1_sl, AF.Ln,
                                      bias=self.c_one[0:HEADS, 0:1])
            t_dt2 = ph.step("scalar", waits=[(self.sDV, t_dt1)],
                            inc=self.sAC)(f_dt2)

            def f_casts(eng):
                o = 32 * 128
                eng.tensor_scalar(sf_sl, z_sl, 0.0, None, ALU.max)
                eng.tensor_copy(B_bf[:, :], xbc_act[:, o:o + 128])
                eng.tensor_copy(C_bf[:, :], xbc_act[:, o + 128:o + 256])
                eng.tensor_add(sf_sl, sf_sl, l1_sl)
                ins = None
                for r in range(4):
                    ins = eng.tensor_copy(C_dup[:, r * 128:(r + 1) * 128],
                                          xbc_act[:, o + 128:o + 256])
                return ins
            t_dt3 = ph.step("vector", waits=[(self.sAC, t_silu), (self.sAC, t_dt2)],
                            inc=self.sDV)(f_casts)
            t_scan = t_dt3

            bw = [(self.sDV, t_scan), (self.sDV, t_dt3),
                  (self.sAC, prev["act_done"])]
            if "rms_act" in prev:
                bw.append((self.sAC, prev["rms_act"]))
            if "ynorm_dv" in prev:
                bw.append((self.sDV, prev["ynorm_dv"]))

            def f_tp1(eng):
                eng.transpose(b7[0:128, 0:64], sf_sl, self.c_ident[0:HEADS, 0:HEADS])
                eng.matmul(b7[:, 256:384], B_bf[:, :], C_bf[:, :],
                           start=True, stop=True)
                ins = None
                for j in range(8):
                    ins = eng.transpose(b6[:, j * 128:(j + 1) * 128],
                                        xs_bf[:, j * 128:(j + 1) * 128],
                                        self.c_ident_bf[:, :])
                return ins
            t_tp1 = ph.step("tensor", waits=bw, inc=self.sPE)(f_tp1)

            def dtx_evacs(eng, r):
                ins = None
                for j in range(8 * (r - 1), 8 * r):
                    for hf in range(2):
                        h = 2 * j + hf
                        ins = eng.tensor_scalar(
                            dtxT[:, j * 128 + hf * 64:j * 128 + hf * 64 + 64],
                            b6[:, (j % 8) * 128 + hf * 64:(j % 8) * 128 + hf * 64 + 64],
                            dtT_sb[:, h:h + 1], None, ALU.mult)
                return ins

            def f_ev1(eng):
                eng.tensor_copy(dtT_sb[:, :], b7[0:128, 0:64])
                eng.tensor_mul(CB_bf[:, :], b7[:, 256:384], self.c_lt[:, :])
                eng.tensor_mul(ldA_T[:, :], dtT_sb[:, :], self.c_abc[:, :])
                return dtx_evacs(eng, 1)
            t_ev1 = ph.step("vector", waits=[(self.sPE, t_tp1)], inc=self.sDV)(f_ev1)

            def f_tp2(eng):
                eng.matmul(b7[:, 64:128], self.c_lt[:, :], ldA_T[:, :],
                           start=True, stop=True)
                ins = None
                for j in range(8, 16):
                    ins = eng.transpose(b6[:, (j % 8) * 128:(j % 8) * 128 + 128],
                                        xs_bf[:, j * 128:(j + 1) * 128],
                                        self.c_ident_bf[:, :])
                return ins
            t_tp2 = ph.step("tensor", waits=[(self.sDV, t_ev1)], inc=self.sPE)(f_tp2)

            def f_ev2(eng):
                eng.tensor_copy(cum_T[:, :], b7[:, 64:128])
                ins = dtx_evacs(eng, 2)
                eng.tensor_scalar_mul(ncum_T[:, :], cum_T[:, :], -1.0)
                return eng.tensor_copy(tp_scr[0:1, 0:1], ncum_T[0:1, 0:1])
            t_ev2 = ph.step("vector", waits=[(self.sPE, t_tp2)], inc=self.sDV)(f_ev2)

            def f_tp3(eng):
                eng.transpose(b7[0:64, 128:256], cum_T[:, :], self.c_ident[:, :])
                ins = None
                for j in range(16, 24):
                    ins = eng.transpose(b6[:, (j % 8) * 128:(j % 8) * 128 + 128],
                                        xs_bf[:, j * 128:(j + 1) * 128],
                                        self.c_ident_bf[:, :])
                return ins
            t_tp3 = ph.step("tensor", waits=[(self.sDV, t_ev2)], inc=self.sPE)(f_tp3)

            evw3 = [(self.sPE, t_tp3)]
            if "rows_gp" in prev:
                evw3.append((self.sGP, prev["rows_gp"]))
            def f_ev3(eng):
                eng.tensor_copy(cum_hs[:, :], b7[0:64, 128:256])
                return dtx_evacs(eng, 3)
            t_ev3 = ph.step("vector", waits=evw3, inc=self.sDV)(f_ev3)

            t_a1 = ph.step("scalar", waits=[(self.sDV, t_ev3)], inc=self.sAC)(
                lambda eng: eng.activation(exp_hs[:, :], cum_hs[:, :], AF.Exp))

            self.sGP.v += 32
            t_rows = self.sGP.v

            def f_rows_w(eng, t_a1=t_a1):
                eng.wait_ge(self.sAC.h, t_a1)
                eng.dma_start(cum_rows[0:1, :], cum_hs[:, :]).then_inc(self.sGP.h, 16)
                eng.dma_start(ec_rows[0:1, :], exp_hs[:, :]).then_inc(self.sGP.h, 16)
            ph.steps["gpsimd"].append(f_rows_w)
            prev["rows_gp"] = t_rows

            def f_tp4(eng):
                eng.transpose(b7[0:1, 64:128], cum_hs[:, 127:128],
                              self.c_ident[0:HEADS, 0:HEADS])
                ins = None
                for j in range(24, 32):
                    ins = eng.transpose(b6[:, (j % 8) * 128:(j % 8) * 128 + 128],
                                        xs_bf[:, j * 128:(j + 1) * 128],
                                        self.c_ident_bf[:, :])
                return ins
            t_tp4 = ph.step("tensor", waits=[(self.sDV, t_ev3)], inc=self.sPE)(f_tp4)

            def f_ev4(eng):
                eng.tensor_copy(cl_row[0:1, :], b7[0:1, 64:64 + HEADS])
                return dtx_evacs(eng, 4)
            t_ev4 = ph.step("vector", waits=[(self.sPE, t_tp4)], inc=self.sDV)(f_ev4)

            tpw = [(self.sDV, t_ev4)]
            if "last_d2" in prev:
                tpw.append((self.sDV, prev["last_d2"]))

            def f_tp5(eng):
                eng.transpose(b6[:, 0:128], B_bf[:, :], self.c_ident_bf[:, :])
                return eng.matmul(b23[0][:, 0:64], self.c_ones_row[:],
                                  cl_row[0:1, :], start=True, stop=True)
            t_tp5 = ph.step("tensor", waits=tpw, inc=self.sPE)(f_tp5)

            def f_ev5(eng):
                eng.tensor_copy(BT_bf[:, :], b6[:, 0:128])
                return eng.tensor_sub(dtmp64[:, :], b23[0][:, 0:64], cum_T[:, :])
            t_ev5 = ph.step("vector", waits=[(self.sPE, t_tp5)], inc=self.sDV)(f_ev5)

            def f_a2(eng):
                eng.activation(dcol[:, :], dtmp64[:, :], AF.Exp)
                return eng.activation(e_cols[:, :], b23[0][:, 0:64], AF.Exp)
            t_a2 = ph.step("scalar", waits=[(self.sDV, t_ev5)], inc=self.sAC)(f_a2)

            t_scan_q = {}
            t_yevac = {}
            t_smm = {}

            def emit_bc(k):
                wbc = [(self.sGP, t_rows), (self.sDV, t_ev5)]
                if k >= 2:
                    wbc.append((self.sDV, t_scan_q[k - 2]))
                elif "last_scan" in prev:
                    wbc.append((self.sDV, prev["last_scan"]))
                if k == 0:
                    wbc.append((self.sAC, t_a2))
                    if "ynorm_dv" in prev:
                        wbc.append((self.sDV, prev["ynorm_dv"]))

                def f_bc(eng, q=k):
                    eng.matmul(b01[q % 2][:, :], self.c_ones_row[:],
                               cum_rows[0:1, 512 * q:512 * q + 512],
                               start=True, stop=True)
                    return eng.matmul(b23[q % 2][:, :], self.c_ones_row_bf[:],
                                      ec_rows[0:1, 512 * q:512 * q + 512],
                                      start=True, stop=True)
                return ph.step("tensor", waits=wbc, inc=self.sPE)(f_bc)

            t_bcd = {0: emit_bc(0)}
            for q in range(16):
                if q + 1 < 16:
                    t_bcd[q + 1] = emit_bc(q + 1)
                t_bc = t_bcd[q]

                def f_d2a(eng, q=q):
                    ins = None
                    for k in range(4):
                        h = 4 * q + k
                        ins = eng.tensor_scalar(D_sl[k][:, :],
                                                b01[q % 2][:, k * 128:(k + 1) * 128],
                                                cum_T[:, h:h + 1], None, ALU.min)
                    return ins
                t_d2a = ph.step("vector", waits=[(self.sPE, t_bc)],
                                inc=self.sDV)(f_d2a)

                def f_d2b(eng, q=q):
                    for k in range(4):
                        h = 4 * q + k
                        eng.tensor_scalar(Bd_sl[k][:, :], BT_bf[:, :],
                                          dcol[:, h:h + 1], None, ALU.mult)
                    return eng.tensor_mul(Ce_bf[:, :], C_dup[:, :], b23[q % 2][:, :])
                t_d2b = ph.step("vector", waits=[], inc=self.sDV)(f_d2b)
                t_scan_q[q] = t_d2b

                t_ga, t_ma = {}, {}
                for ghf in range(2):
                    def f_g(eng, q=q, ghf=ghf):
                        ins = None
                        for k in (2 * ghf, 2 * ghf + 1):
                            h = 4 * q + k
                            ins = eng.activation(G_sl[k][:, :], D_sl[k][:, :],
                                                 AF.Exp, bias=ncum_T[:, h:h + 1])
                        return ins
                    t_ga[ghf] = ph.step("scalar", waits=[(self.sDV, t_d2a)],
                                        inc=self.sAC)(f_g)

                    def f_m(eng, q=q, ghf=ghf):
                        ins = None
                        for k in (2 * ghf, 2 * ghf + 1):
                            ins = eng.tensor_mul(M_sl[k][:, :], CB_bf[:, :],
                                                 G_sl[k][:, :])
                        return ins
                    t_ma[ghf] = ph.step("vector", waits=[(self.sAC, t_ga[ghf])],
                                        inc=self.sDV)(f_m)

                for r in range(2):
                    p = 2 * q + r
                    wmm = [(self.sDV, t_ma[r])]
                    if p >= 2:
                        wmm.append((self.sDV, t_smm[p - 2]))
                    elif "last_smm" in prev:
                        wmm.append((self.sDV, prev["last_smm"]))

                    def f_ymm(eng, q=q, r=r, p=p, ci=ci):
                        ins = None
                        if "duck" in P3_ABLATE:
                            eng.matmul(b45[p % 2][0:64, 384:385],
                                       dtxT[:, p * 128:p * 128 + 64],
                                       M_sl[2 * r][:, 0:1], start=True, stop=True)
                        for hf in range(2):
                            k = 2 * r + hf
                            dsl = slice(p * 128 + hf * 64, p * 128 + hf * 64 + 64)
                            ysl = b45[p % 2][0:64, hf * 128:hf * 128 + 128]
                            ins = eng.matmul(ysl, dtxT[:, dsl], M_sl[k][:, :],
                                             start=True, stop=(ci == 0))
                            if ci > 0:
                                ins = eng.matmul(ysl, S_sb[:, dsl],
                                                 Ce_bf[:, k * 128:(k + 1) * 128],
                                                 start=False, stop=True)
                            if "contrib" not in P3_ABLATE:
                                ins = eng.matmul(
                                    b45[p % 2][:, 256 + hf * 64:256 + hf * 64 + 64],
                                    Bd_sl[k][:, :], dtxT[:, dsl],
                                    start=True, stop=True)
                        return ins
                    t_ymmp = ph.step("tensor", waits=wmm, inc=self.sPE)(f_ymm)

                    wye = [(self.sPE, t_ymmp)]
                    if p == 0 and "last_rmm" in prev:
                        wye.append((self.sPE, prev["last_rmm"]))
                    if p == 0 and "yn_store" in prev:
                        wye.append((self.sST, prev["yn_store"]))
                    if "act_evac" in P3_ABLATE:
                        wae = list(wye)
                        if p >= 1:
                            wae.append((self.sDV, t_yevac[p - 1]))

                        def f_aev(eng, p=p):
                            ins = None
                            for hf in range(2):
                                rsl = slice(hf * 64, hf * 64 + 64)
                                ins = eng.copy(tp_scr[rsl, :],
                                               b45[p % 2][0:64, hf * 128:hf * 128 + 128])
                            return ins
                        t_aev = ph.step("scalar", waits=wae, inc=self.sAC)(f_aev)
                        wye = wye + [(self.sAC, t_aev)]

                    def f_yev(eng, p=p):
                        o = slice(p * 128, (p + 1) * 128)
                        if "act_evac" in P3_ABLATE:
                            for hf in range(2):
                                rsl = slice(hf * 64, hf * 64 + 64)
                                eng.scalar_tensor_tensor(
                                    tp_scr[rsl, :], xs_bf[rsl, o],
                                    self.c_dext[rsl, p:p + 1], tp_scr[rsl, :],
                                    ALU.mult, ALU.add)
                        elif "stt_evac" in P3_ABLATE:
                            for hf in range(2):
                                rsl = slice(hf * 64, hf * 64 + 64)
                                eng.tensor_copy(tp_scr[rsl, :],
                                                b45[p % 2][0:64, hf * 128:hf * 128 + 128])
                            for hf in range(2):
                                rsl = slice(hf * 64, hf * 64 + 64)
                                eng.scalar_tensor_tensor(
                                    tp_scr[rsl, :], xs_bf[rsl, o],
                                    self.c_dext[rsl, p:p + 1], tp_scr[rsl, :],
                                    ALU.mult, ALU.add)
                        else:
                            for hf in range(2):
                                rsl = slice(hf * 64, hf * 64 + 64)
                                eng.scalar_tensor_tensor(
                                    tp_scr[rsl, :],
                                    xs_bf[rsl, o],
                                    self.c_dext[rsl, p:p + 1],
                                    b45[p % 2][0:64, hf * 128:hf * 128 + 128],
                                    ALU.mult, ALU.add)
                        for hf in range(2):
                            rsl = slice(hf * 64, hf * 64 + 64)
                            eng.tensor_mul(y2bf[rsl, o], tp_scr[rsl, :], sg[rsl, o])
                        for hf in range(2):
                            rsl = slice(hf * 64, hf * 64 + 64)
                            ins = eng.tensor_mul(sq_full[rsl, o], y2bf[rsl, o],
                                                 y2bf[rsl, o])
                        return ins
                    t_yevac[p] = ph.step("vector", waits=wye, inc=self.sDV)(f_yev)

                    def f_sup(eng, q=q, r=r, p=p, ci=ci):
                        if "contrib" in P3_ABLATE:
                            return eng.engine_nop()
                        ins = None
                        for hf in range(2):
                            h = 4 * q + 2 * r + hf
                            ssl = slice(p * 128 + hf * 64, p * 128 + hf * 64 + 64)
                            csl = b45[p % 2][:, 256 + hf * 64:256 + hf * 64 + 64]
                            if ci == 0:
                                ins = eng.tensor_copy(S_sb[:, ssl], csl)
                            else:
                                ins = eng.scalar_tensor_tensor(
                                    S_sb[:, ssl], S_sb[:, ssl],
                                    e_cols[:, h:h + 1], csl, ALU.mult, ALU.add)
                        return ins
                    t_smm[p] = ph.step("vector", waits=[], inc=self.sDV)(f_sup)

            def f_rmm(eng):
                ins = None
                for p in range(32):
                    ins = eng.matmul(b7[0:1, 384:512], self.c_ones_col_bf[:, :],
                                     sq_full[:, p * 128:(p + 1) * 128],
                                     start=(p == 0), stop=(p == 31))
                return ins
            t_rmm = ph.step("tensor", waits=[(self.sDV, t_yevac[31])],
                            inc=self.sPE)(f_rmm)
            prev["last_rmm"] = t_rmm
            if "qdbg" in self.dbg and ci == 0:
                for nm, ap, shp, dt in [
                    ("dbg_G0", G_sl[0][:, :], [P, 128], BF16),
                    ("dbg_y2bf", y2bf[:, :], [P, 32 * 128], BF16),
                    ("dbg_dtxT", dtxT[:, :], [P, 32 * 128], BF16),
                ]:
                    out_t = self.nc.dram_tensor(nm, shp, dt, kind="ExternalOutput")
                    self.sGP.v += 16
                    tgtv = self.sGP.v

                    def f_dbg(eng, out_t=out_t, ap=ap, tgtv=tgtv, w=t_smm[31]):
                        eng.wait_ge(self.sDV.h, w)
                        eng.dma_start(out_t[:, :], ap).then_inc(self.sGP.h, 16)
                        eng.wait_ge(self.sGP.h, tgtv)
                    ph.steps["gpsimd"].append(f_dbg)

            prev["last_scan"] = t_scan_q[15]
            prev["last_d2"] = t_scan_q[14]
            prev["last_smm"] = t_smm[31]

            if "p3dbg" in self.dbg and ci == 0:
                for nm, ap, shp, dt in [
                    ("dbg_M0", M_sl[0][:, :], [P, 128], BF16),
                    ("dbg_M1", M_sl[1][:, :], [P, 128], BF16),
                    ("dbg_dtxT", dtxT[:, :], [P, 32 * 128], BF16),
                    ("dbg_dArows", dA_rows[0:1, :], [1, HEADS * 128], F32),
                    ("dbg_Yf", Yf[:, :], [HEADS, 128], F32),
                    ("dbg_CB", CB_bf[:, :], [P, 128], BF16),
                    ("dbg_y2bf", y2bf[:, :], [P, 32 * 128], BF16),
                    ("dbg_dcol", dcol[:, :], [P, 64], F32),
                ]:
                    out_t = self.nc.dram_tensor(nm, shp, dt, kind="ExternalOutput")
                    self.sGP.v += 16
                    tgtv = self.sGP.v

                    def f_dbg(eng, out_t=out_t, ap=ap, tgtv=tgtv, w=t_smm[31]):
                        eng.wait_ge(self.sDV.h, w)
                        eng.dma_start(out_t[:, :], ap).then_inc(self.sGP.h, 16)
                        eng.wait_ge(self.sGP.h, tgtv)
                    ph.steps["gpsimd"].append(f_dbg)

            def f_rms(eng):
                eng.activation(lnv_row, b7[0:1, 384:512], AF.Ln,
                               bias=self.c_eps[0:1, 0:1], scale=1.0 / 4096)
                return eng.activation(rstd_row, lnv_row, AF.Exp, scale=-0.5)
            t_rms = ph.step("scalar", waits=[(self.sPE, t_rmm)], inc=self.sAC)(f_rms)
            prev["rms_act"] = t_rms

            if "rmsdbg" in self.dbg and ci == 0:
                sumrow = ctx.enter_context(nc.sbuf_tensor("p3_sumdbg", [1, 128], F32))
                t_sd = ph.step("vector", waits=[(self.sPE, t_rmm), (self.sAC, t_rms)],
                               inc=self.sDV)(
                    lambda eng: eng.tensor_copy(sumrow[0:1, :], b7[0:1, 384:512]))
                out_t = self.nc.dram_tensor("dbg_sumrow", [1, 128], F32, kind="ExternalOutput")
                out_t2 = self.nc.dram_tensor("dbg_rstd", [1, 256], F32, kind="ExternalOutput")
                self.sGP.v += 32
                tgtv = self.sGP.v

                def f_sd(eng, tgtv=tgtv, t_sd=t_sd, t_rms=t_rms):
                    eng.wait_ge(self.sDV.h, t_sd)
                    eng.wait_ge(self.sAC.h, t_rms)
                    eng.dma_start(out_t[:, :], sumrow[0:1, :]).then_inc(self.sGP.h, 16)
                    eng.dma_start(out_t2[:, :], rows[0:1, :]).then_inc(self.sGP.h, 16)
                    eng.wait_ge(self.sGP.h, tgtv)
                ph.steps["gpsimd"].append(f_sd)

            t_rbc = ph.step("tensor", waits=[(self.sAC, t_rms),
                                             (self.sDV, t_scan_q[14])],
                            inc=self.sPE)(
                lambda eng: eng.matmul(b01[0][:, 0:128], self.c_ones_row[:],
                                       rstd_row, start=True, stop=True))

            wyn = [(self.sPE, t_rbc)]
            if "yn_store" in prev:
                wyn.append((self.sST, prev["yn_store"]))

            def f_yn(eng):
                if "skipnorm" in P3_ABLATE:
                    return eng.engine_nop()
                ins = None
                for j in range(32):
                    ins = eng.scalar_tensor_tensor(
                        y2bf[:, j * 128:(j + 1) * 128],
                        y2bf[:, j * 128:(j + 1) * 128],
                        self.c_gnw[:, j:j + 1], b01[0][:, 0:128],
                        ALU.mult, ALU.mult)
                return ins
            t_yn = ph.step("vector", waits=wyn, inc=self.sDV)(f_yn)
            prev["ynorm_dv"] = t_yn

            t_st = ph.step("scalar", waits=[(self.sDV, t_yn)],
                           inc=self.sST, inck=16)(
                lambda eng, t0=t0: eng.dma_start(
                    self.YNORM[:, :, t0:t0 + 128].rearrange("j p t -> p j t"),
                    y2bf[:, :].rearrange("p (j t) -> p j t", j=32)))
            prev["yn_store"] = t_st

        self.drain(ph)
        ph.flush()
        ctx.close()

    def p4_out_proj(self):
        nc = self.nc
        ctx = ExitStack()
        sb = lambda n, shp, dt=F32: ctx.enter_context(nc.sbuf_tensor(n, shp, dt))
        yn = sb("p4_ynf", [P, 32 * 2048], BF16)
        wb = [sb(f"p4_w{i}", [P, 32 * 128], BF16) for i in range(2)]
        xrb = [sb(f"p4_x{i}", [P, T]) for i in range(2)]
        stg = [sb(f"p4_s{i}", [P, T]) for i in range(2)]
        pss = [ctx.enter_context(nc.psum_tensor(f"p4_ps{i}", [P, 512], F32))
               for i in range(8)]
        ph = self.phase("p4")

        self.sLD.v += 16
        t_yn = self.sLD.v
        stv = self.sST.v

        def f_ldyn(eng):
            eng.wait_ge(self.sST.h, stv)
            eng.dma_start(yn[:, :].rearrange("p (c j) -> p c j", c=32),
                          self.YNORM[:, :, :].rearrange("c p j -> p c j")
                          ).then_inc(self.sLD.h, 16)
        ph.steps["sync"].append(f_ldyn)

        t_wl = {}
        t_mm = {}
        t_ev = {}
        t_st = {}
        t_xr = {}
        for oc in range(16):
            o0 = oc * 128
            bank0 = (oc % 2) * 4
            waits = []
            if oc >= 2:
                waits.append((self.sPE, t_mm[oc - 2]))
            t_wl[oc] = ph.step("sync", waits=waits, inc=self.sLD, inck=16)(
                lambda eng, o0=o0, oc=oc: eng.dma_start(
                    wb[oc % 2][:, :].rearrange("p (c j) -> p c j", c=32),
                    self.W_OUT[:, :, o0:o0 + 128].rearrange("c p j -> p c j")))
            if True:
                xw = []
                if oc >= 2:
                    xw.append((self.sDV, t_ev[oc - 2]))
                t_xr[oc] = ph.step("sync", waits=xw, inc=self.sLD, inck=16)(
                    lambda eng, oc=oc: eng.dma_start(xrb[oc % 2][:, :],
                                                     self.XR[oc, :, :]))

                mywaits = [(self.sLD, t_wl[oc]), (self.sLD, t_yn)]
                if oc >= 2:
                    mywaits.append((self.sDV, t_ev[oc - 2]))

                def f_mm(eng, oc=oc, bank0=bank0):
                    ins = None
                    for kc in range(32):
                        lhsT = wb[oc % 2][:, kc * 128:(kc + 1) * 128]
                        for tt in range(4):
                            ins = eng.matmul(pss[bank0 + tt][:, :], lhsT,
                                             yn[:, kc * 2048 + tt * 512:kc * 2048 + tt * 512 + 512],
                                             start=(kc == 0), stop=(kc == 31))
                    return ins
                t_mm[oc] = ph.step("tensor", waits=mywaits, inc=self.sPE)(f_mm)

                evw = [(self.sPE, t_mm[oc]), (self.sLD, t_xr[oc])]
                if oc >= 2:
                    evw.append((self.sST, t_st[oc - 2]))

                def f_ev(eng, oc=oc, bank0=bank0):
                    ins = None
                    for tt in range(4):
                        ins = eng.tensor_add(stg[oc % 2][:, tt * 512:(tt + 1) * 512],
                                             pss[bank0 + tt][:, :],
                                             xrb[oc % 2][:, tt * 512:(tt + 1) * 512])
                    return ins
                t_ev[oc] = ph.step("vector", waits=evw, inc=self.sDV)(f_ev)
                t_st[oc] = ph.step("scalar", waits=[(self.sDV, t_ev[oc])],
                                   inc=self.sST, inck=16)(
                    lambda eng, oc=oc: eng.dma_start(self.X1[oc, :, :],
                                                     stg[oc % 2][:, :]))
        self.drain(ph)
        ph.flush()
        ctx.close()

    def p_ffn(self, SRC, DST, tag):
        nc = self.nc
        ctx = ExitStack()
        sb = lambda n, shp, dt=F32: ctx.enter_context(
            nc.sbuf_tensor(f"{tag}{n}", shp, dt))
        TH = 1024
        acc = sb("acc", [P, DC * TH])
        xln = sb("xln", [P, DC * TH], BF16)
        mbc = sb("mbc", [P, 1024])
        sqs = sb("sqs", [P, 2 * 512], BF16)
        gh = sb("gh", [P, 11 * TH], BF16)
        ftmp = sb("ftmp", [P, 512])
        cbs = sb("cbs", [P, TH])
        wgu = [sb(f"wg{i}", [P, DC * 512], BF16) for i in range(2)]
        wdn = [sb(f"wd{i}", [P, 11 * 512], BF16) for i in range(2)]
        rt_sb = sb("rt", [P, DC * E], BF16)
        sgw_sb = sb("sgw", [P, DC], BF16)
        ctm = sb("ctm", [P, 8 * E])
        crow = sb("crow", [1, TH])
        sigr = sb("sigr", [1, TH])
        rows = sb("rows", [1, 3 * 512])
        rwork = sb("rwork", [P, 336])
        pb = [ctx.enter_context(nc.psum_tensor(f"{tag}b{i}", [P, 512], F32))
              for i in range(8)]
        ph = self.phase(f"p_{tag}")
        st = {"gu_ld": 0, "dn_ld": 0, "mm_i": 0,
              "gu_slot_pe": [0, 0], "dn_slot_pe": [0, 0],
              "bank": [None] * 8, "acc_st": 0, "xln_pe": 0, "ct_dv": 0,
              "mrow_pe": 0}
        m_row, v_row, l_row = (rows[0:1, 0:512], rows[0:1, 512:1024],
                               rows[0:1, 1024:1536])

        def bankw(b):
            g = st["bank"][b]
            return [g] if g is not None else []

        t_rt = ph.step("sync", waits=[], inc=self.sLD, inck=16)(
            lambda eng: eng.dma_start(
                rt_sb[:, :].rearrange("p (c j) -> p c j", c=DC),
                self.W_RT[:, :, :].rearrange("c p j -> p c j")))
        t_rt = ph.step("sync", waits=[], inc=self.sLD, inck=16)(
            lambda eng: eng.dma_start(sgw_sb[:, :], self.W_SG[:, :]))

        for half in range(FFN_HALVES):
            t0h = half * TH

            lw = [(self.sST, st["acc_st"])] if st["acc_st"] else []
            t_ldx = ph.step("sync", waits=lw, inc=self.sLD, inck=16)(
                lambda eng, t0h=t0h: eng.dma_start(
                    acc[:, :].rearrange("p (c j) -> p c j", c=DC),
                    SRC[:, :, t0h:t0h + TH].rearrange("c p j -> p c j")))

            cw = [(self.sLD, t_ldx)]
            if st["xln_pe"]:
                cw.append((self.sPE, st["xln_pe"]))

            def f_cast(eng):
                ins = None
                for c in range(DC):
                    ins = eng.tensor_copy(xln[:, c * TH:(c + 1) * TH],
                                          acc[:, c * TH:(c + 1) * TH])
                return ins
            t_cast = ph.step("vector", waits=cw, inc=self.sDV)(f_cast)

            for ti in range(2):
                ts0 = ti * 512

                def f_sx(eng, ts0=ts0):
                    ins = None
                    for c in range(DC):
                        ins = eng.matmul(pb[0][0:1, :], self.c_ones_col_bf[:, :],
                                         xln[:, c * TH + ts0:c * TH + ts0 + 512],
                                         start=(c == 0), stop=(c == DC - 1))
                    return ins
                t_sx = ph.step("tensor", waits=[(self.sDV, t_cast)] + bankw(0),
                               inc=self.sPE)(f_sx)

                t_sqc, t_mmq = {}, {}
                for c in range(DC):
                    qw = [(self.sDV, t_cast)]
                    if c >= 2:
                        qw.append((self.sPE, t_mmq[c - 2]))

                    def f_sq(eng, c=c, ts0=ts0):
                        sl = slice((c % 2) * 512, (c % 2) * 512 + 512)
                        return eng.tensor_mul(
                            sqs[:, sl], xln[:, c * TH + ts0:c * TH + ts0 + 512],
                            xln[:, c * TH + ts0:c * TH + ts0 + 512])
                    t_sqc[c] = ph.step("vector", waits=qw, inc=self.sDV)(f_sq)

                    def f_mq(eng, c=c):
                        sl = slice((c % 2) * 512, (c % 2) * 512 + 512)
                        return eng.matmul(pb[1][0:1, :], self.c_ones_col_bf[:, :],
                                          sqs[:, sl], start=(c == 0),
                                          stop=(c == DC - 1))
                    t_mmq[c] = ph.step(
                        "tensor",
                        waits=[(self.sDV, t_sqc[c])] + (bankw(1) if c == 0 else []),
                        inc=self.sPE)(f_mq)

                rww = [(self.sPE, t_mmq[DC - 1]), (self.sPE, t_sx)]
                if st["mrow_pe"]:
                    rww.append((self.sPE, st["mrow_pe"]))

                def f_rows(eng):
                    eng.tensor_scalar(m_row, pb[0][0:1, :], 1.0 / 2048, None,
                                      ALU.mult)
                    eng.tensor_mul(v_row, m_row, m_row)
                    return eng.scalar_tensor_tensor(
                        v_row, pb[1][0:1, :], 1.0 / 2048, v_row,
                        ALU.mult, ALU.subtract)
                t_rows = ph.step("vector", waits=rww, inc=self.sDV)(f_rows)
                st["bank"][0] = st["bank"][1] = (self.sDV, t_rows)

                def f_rstd(eng):
                    eng.activation(l_row, v_row, AF.Ln, bias=self.c_eps[0:1, 0:1])
                    return eng.activation(l_row, l_row, AF.Exp, scale=-0.5)
                t_rstd = ph.step("scalar", waits=[(self.sDV, t_rows)],
                                 inc=self.sAC)(f_rstd)

                def f_bc(eng):
                    eng.matmul(pb[4][:, :], self.c_ones_row[:], m_row,
                               start=True, stop=True)
                    return eng.matmul(pb[5][:, :], self.c_ones_row[:], l_row,
                                      start=True, stop=True)
                t_bc = ph.step("tensor",
                               waits=[(self.sAC, t_rstd)] + bankw(4) + bankw(5),
                               inc=self.sPE)(f_bc)
                st["mrow_pe"] = t_bc

                def f_mc(eng):
                    eng.tensor_copy(mbc[:, 0:512], pb[4][:, :])
                    return eng.tensor_copy(mbc[:, 512:1024], pb[5][:, :])
                t_mc = ph.step("vector", waits=[(self.sPE, t_bc)],
                               inc=self.sDV)(f_mc)
                st["bank"][4] = st["bank"][5] = (self.sDV, t_mc)

                def f_norm(eng, ts0=ts0):
                    for c in range(DC):
                        sl = slice(c * TH + ts0, c * TH + ts0 + 512)
                        eng.tensor_sub(xln[:, sl], xln[:, sl], mbc[:, 0:512])
                    ins = None
                    for c in range(DC):
                        sl = slice(c * TH + ts0, c * TH + ts0 + 512)
                        ins = eng.tensor_mul(xln[:, sl], xln[:, sl],
                                             mbc[:, 512:1024])
                    return ins
                t_norm = ph.step("vector", waits=[], inc=self.sDV)(f_norm)
            t_xln = t_norm

            t_lgdv = None
            for tt8 in range(8):
                q0 = tt8 * 128
                b = 6 + tt8 % 2

                def f_lg(eng, q0=q0, b=b):
                    ins = None
                    for c in range(DC):
                        ins = eng.matmul(pb[b][0:128, 0:E],
                                         xln[:, c * TH + q0:c * TH + q0 + 128],
                                         rt_sb[:, c * E:(c + 1) * E],
                                         start=(c == 0), stop=(c == DC - 1))
                    return ins
                t_lg = ph.step(
                    "tensor",
                    waits=[(self.sDV, t_xln), (self.sLD, t_rt)] + bankw(b),
                    inc=self.sPE)(f_lg)

                def f_ex(eng, b=b, tt8=tt8):
                    lg = pb[b][0:128, 0:E]
                    eng.reduce_max(rwork[:, 192 + tt8:193 + tt8], lg, AX.X,
                                   negate=True)
                    return eng.tensor_copy(rwork[:, tt8 * E:tt8 * E + E], lg)
                t_lgdv = ph.step("vector", waits=[(self.sPE, t_lg)],
                                 inc=self.sDV)(f_ex)
                st["bank"][b] = (self.sDV, t_lgdv)

            def f_exp(eng):
                ins = None
                for t8 in range(8):
                    ins = eng.activation(rwork[:, 64 + t8 * E:64 + t8 * E + E],
                                         rwork[:, t8 * E:t8 * E + E], AF.Exp,
                                         bias=rwork[:, 192 + t8:193 + t8])
                return ins
            t_exp = ph.step("scalar", waits=[(self.sDV, t_lgdv)],
                            inc=self.sAC)(f_exp)

            def f_t1(eng):
                ins = None
                for t8 in range(8):
                    ins = eng.reduce_max(rwork[:, 128 + t8:129 + t8],
                                         rwork[:, 64 + t8 * E:64 + t8 * E + E],
                                         AX.X)
                return ins
            t_t1 = ph.step("vector", waits=[(self.sAC, t_exp)],
                           inc=self.sDV)(f_t1)
            t_t1b = ph.step("scalar", waits=[(self.sDV, t_t1)], inc=self.sAC)(
                lambda eng: eng.copy(rwork[:, 136:144], rwork[:, 128:136]))

            def f_mask(eng):
                ins = None
                for t8 in range(8):
                    ins = eng.tensor_scalar(
                        rwork[:, 208 + t8 * E:208 + t8 * E + E],
                        rwork[:, 64 + t8 * E:64 + t8 * E + E],
                        rwork[:, 136 + t8:137 + t8], None, ALU.is_ge)
                return ins
            t_mask = ph.step("vector", waits=[(self.sAC, t_t1b)],
                             inc=self.sDV)(f_mask)
            t_maskb = ph.step("scalar", waits=[(self.sDV, t_mask)],
                              inc=self.sAC)(
                lambda eng: eng.copy(rwork[:, 272:336], rwork[:, 208:272]))

            def f_excl(eng):
                ins = None
                for t8 in range(8):
                    ins = eng.scalar_tensor_tensor(
                        rwork[:, 208 + t8 * E:208 + t8 * E + E],
                        rwork[:, 272 + t8 * E:272 + t8 * E + E], -1e30,
                        rwork[:, 64 + t8 * E:64 + t8 * E + E],
                        ALU.mult, ALU.add)
                return ins
            t_excl = ph.step("vector", waits=[(self.sAC, t_maskb)],
                             inc=self.sDV)(f_excl)
            t_exclb = ph.step("scalar", waits=[(self.sDV, t_excl)],
                              inc=self.sAC)(
                lambda eng: eng.copy(rwork[:, 272:336], rwork[:, 208:272]))

            def f_sec(eng):
                ins = None
                for t8 in range(8):
                    ins = eng.reduce_max(rwork[:, 144 + t8:145 + t8],
                                         rwork[:, 272 + t8 * E:272 + t8 * E + E],
                                         AX.X)
                return ins
            t_sec = ph.step("vector", waits=[(self.sAC, t_exclb)],
                            inc=self.sDV)(f_sec)
            t_secb = ph.step("scalar", waits=[(self.sDV, t_sec)], inc=self.sAC)(
                lambda eng: eng.copy(rwork[:, 152:160], rwork[:, 144:152]))

            def f_sum(eng):
                eng.tensor_add(rwork[:, 160:168], rwork[:, 152:160],
                               rwork[:, 128:136])
                ins = None
                for t8 in range(8):
                    ins = eng.tensor_scalar(
                        rwork[:, 208 + t8 * E:208 + t8 * E + E],
                        rwork[:, 64 + t8 * E:64 + t8 * E + E],
                        rwork[:, 152 + t8:153 + t8], None, ALU.is_ge)
                return ins
            t_sum = ph.step("vector", waits=[(self.sAC, t_secb)],
                            inc=self.sDV)(f_sum)

            def f_sumb(eng):
                eng.copy(rwork[:, 200:208], rwork[:, 160:168])
                return eng.copy(rwork[:, 272:336], rwork[:, 208:272])
            t_sumb = ph.step("scalar", waits=[(self.sDV, t_sum)],
                             inc=self.sAC)(f_sumb)

            t_rec = ph.step("vector", waits=[(self.sAC, t_sumb)],
                            inc=self.sDV)(
                lambda eng: eng.reciprocal(rwork[:, 168:176],
                                           rwork[:, 200:208]))
            t_recb = ph.step("scalar", waits=[(self.sDV, t_rec)], inc=self.sAC)(
                lambda eng: eng.copy(rwork[:, 176:184], rwork[:, 168:176]))

            def f_ct(eng):
                ins = None
                for t8 in range(8):
                    ins = eng.scalar_tensor_tensor(
                        ctm[:, t8 * E:(t8 + 1) * E],
                        rwork[:, 64 + t8 * E:64 + t8 * E + E],
                        rwork[:, 176 + t8:177 + t8],
                        rwork[:, 272 + t8 * E:272 + t8 * E + E],
                        ALU.mult, ALU.mult)
                return ins
            st["ct_dv"] = ph.step("vector", waits=[(self.sAC, t_recb)],
                                  inc=self.sDV)(f_ct)

            for ti in range(2):
                b = 6 + ti

                def f_sg(eng, ti=ti, b=b):
                    ins = None
                    for c in range(DC):
                        ins = eng.matmul(
                            pb[b][0:1, :], sgw_sb[:, c:c + 1],
                            xln[:, c * TH + ti * 512:c * TH + ti * 512 + 512],
                            start=(c == 0), stop=(c == DC - 1))
                    return ins
                t_s1 = ph.step(
                    "tensor",
                    waits=[(self.sDV, t_xln), (self.sLD, t_rt)] + bankw(b),
                    inc=self.sPE)(f_sg)
                t_s2 = ph.step("scalar", waits=[(self.sPE, t_s1)], inc=self.sAC)(
                    lambda eng, ti=ti, b=b: eng.activation(
                        sigr[0:1, ti * 512:ti * 512 + 512], pb[b][0:1, :],
                        AF.Tanh, scale=0.5))
                st["bank"][b] = (self.sAC, t_s2)
            t_sg3 = ph.step("vector", waits=[(self.sAC, t_s2)], inc=self.sDV)(
                lambda eng: eng.tensor_scalar(sigr[0:1, :], sigr[0:1, :],
                                              0.5, 0.5, ALU.mult, ALU.add))

            pe_list = ([("r", e) for e in range(E)] +
                       [("s", q) for q in range(4)])[:FFN_NEXP]
            for pidx, (kind, e) in enumerate(pe_list):
                self._ffn_expert(ph, st, kind, e, xln, gh, wgu, wdn, ctm, crow,
                                 sigr, cbs, ftmp, acc, pb, t_sg3)
            st["xln_pe"] = self.sPE.v

            dvv = self.sDV.v
            st["acc_st"] = ph.step("scalar", waits=[(self.sDV, dvv)],
                                   inc=self.sST, inck=16)(
                lambda eng, t0h=t0h: eng.dma_start(
                    DST[:, :, t0h:t0h + TH].rearrange("c p j -> p c j"),
                    acc[:, :].rearrange("p (c j) -> p c j", c=DC)))

        self.drain(ph)
        ph.flush()
        if "fdbg" in self.dbg and tag == "f1":
            self.dump_sbuf("dbg_facc", acc[:, :], [P, DC * TH], F32)
            self.dump_sbuf("dbg_fxln", xln[:, :], [P, DC * TH], BF16)
            self.dump_sbuf("dbg_fctm", ctm[:, :], [P, 8 * E], F32)
            self.dump_sbuf("dbg_fsig", sigr[0:1, :], [1, TH], F32)
            self.dump_sbuf("dbg_fgh", gh[:, :], [P, 11 * TH], BF16)
            self.dump_sbuf("dbg_fcbs", cbs[:, :], [P, TH], F32)
            self.dump_sbuf("dbg_fcrow", crow[0:1, :], [1, TH], F32)
        ctx.close()

    def _ffn_expert(self, ph, st, kind, e, xln, gh, wgu, wdn, ctm, crow, sigr,
                    cbs, ftmp, acc, pb, t_sg3):
        nc = self.nc
        TH = 1024
        GRPS = [(0, 512), (512, 512), (1024, 384)]

        if kind == "r":
            tps = {}
            for rnd in range(2):
                if rnd == 0:
                    tw = [(self.sDV, st["ct_dv"])] + bankw_list(st, 6)
                else:
                    tw = [(self.sDV, tps[0])]

                def f_tp(eng, rnd=rnd, e=e):
                    ins = None
                    for j in range(4):
                        tt8 = rnd * 4 + j
                        ins = eng.transpose(
                            pb[6][0:1, j * 128:j * 128 + 128],
                            ctm[:, tt8 * E + e:tt8 * E + e + 1],
                            self.c_ident[:, :])
                    return ins
                t_tp = ph.step("tensor", waits=tw, inc=self.sPE)(f_tp)

                def f_cv(eng, rnd=rnd):
                    return eng.tensor_copy(
                        crow[0:1, rnd * 512:rnd * 512 + 512], pb[6][0:1, :])
                tps[rnd] = ph.step("vector", waits=[(self.sPE, t_tp)],
                                   inc=self.sDV)(f_cv)
            st["bank"][6] = (self.sDV, tps[1])
            srow, row_dep = crow, (self.sDV, tps[1])
        else:
            srow, row_dep = sigr, (self.sDV, t_sg3)

        def f_k1(eng, srow=srow):
            eng.matmul(pb[4][:, :], self.c_ones_row[:], srow[0:1, 0:512],
                       start=True, stop=True)
            return eng.matmul(pb[5][:, :], self.c_ones_row[:],
                              srow[0:1, 512:1024], start=True, stop=True)
        t_k1 = ph.step("tensor",
                       waits=[row_dep] + bankw_list(st, 4) + bankw_list(st, 5),
                       inc=self.sPE)(f_k1)

        def f_cbs(eng):
            eng.tensor_copy(cbs[:, 0:512], pb[4][:, :])
            return eng.tensor_copy(cbs[:, 512:1024], pb[5][:, :])
        t_cbs = ph.step("vector", waits=[(self.sPE, t_k1)], inc=self.sDV)(f_cbs)
        st["bank"][4] = st["bank"][5] = (self.sDV, t_cbs)

        t_gate_last = None
        t_up_last = None
        for up_flag in range(2):
            for gi, (io0, gw) in enumerate(GRPS):
                slot = st["gu_ld"] % 2
                st["gu_ld"] += 1
                lw = ([(self.sPE, st["gu_slot_pe"][slot])]
                      if st["gu_slot_pe"][slot] else [])
                if kind == "r":
                    wsrc = self.W_GU[e, :, :, up_flag * I_MOE + io0:
                                     up_flag * I_MOE + io0 + gw]
                else:
                    W = self.W_SHU if up_flag else self.W_SHG
                    wsrc = W[:, :, e * I_MOE + io0:e * I_MOE + io0 + gw]
                t_wl = ph.step("sync", waits=lw, inc=self.sLD, inck=16)(
                    lambda eng, slot=slot, gw=gw, wsrc=wsrc: eng.dma_start(
                        wgu[slot][:, 0:DC * gw].rearrange("p (c j) -> p c j",
                                                          c=DC),
                        wsrc.rearrange("c p j -> p c j")))
                for icl in range(gw // 128):
                    ic = io0 // 128 + icl
                    for tt in range(2):
                        b = st["mm_i"] % 4
                        st["mm_i"] += 1
                        pw = [(self.sLD, t_wl)] + bankw_list(st, b)

                        def f_mm(eng, slot=slot, gw=gw, icl=icl, tt=tt, b=b):
                            ins = None
                            for kc in range(DC):
                                lhsT = wgu[slot][:, kc * gw + icl * 128:
                                                 kc * gw + icl * 128 + 128]
                                rhs = xln[:, kc * TH + tt * 512:
                                          kc * TH + tt * 512 + 512]
                                ins = eng.matmul(pb[b][:, :], lhsT, rhs,
                                                 start=(kc == 0),
                                                 stop=(kc == DC - 1))
                            return ins
                        t_mm = ph.step("tensor", waits=pw, inc=self.sPE)(f_mm)
                        st["gu_slot_pe"][slot] = t_mm
                        gsl = slice(ic * TH + tt * 512, ic * TH + tt * 512 + 512)
                        if not up_flag:
                            t_gate_last = ph.step(
                                "scalar", waits=[(self.sPE, t_mm)],
                                inc=self.sAC)(
                                lambda eng, gsl=gsl, b=b: eng.activation(
                                    gh[:, gsl], pb[b][:, :], AF.Silu))
                            st["bank"][b] = (self.sAC, t_gate_last)
                        else:
                            uw = [(self.sPE, t_mm), (self.sAC, t_gate_last)]

                            def f_up(eng, gsl=gsl, b=b, tt=tt):
                                eng.tensor_mul(ftmp[:, :], pb[b][:, :],
                                               cbs[:, tt * 512:tt * 512 + 512])
                                return eng.tensor_mul(gh[:, gsl], ftmp[:, :],
                                                      gh[:, gsl])
                            t_up_last = ph.step("vector", waits=uw,
                                                inc=self.sDV)(f_up)
                            st["bank"][b] = (self.sDV, t_up_last)

        for og in range(4):
            slot = st["dn_ld"] % 2
            st["dn_ld"] += 1
            lw = ([(self.sPE, st["dn_slot_pe"][slot])]
                  if st["dn_slot_pe"][slot] else [])
            if kind == "r":
                wsrc = self.W_DN[e, :, :, og * 512:og * 512 + 512]
            else:
                wsrc = self.W_SHD[11 * e:11 * e + 11, :, og * 512:og * 512 + 512]
            t_wl = ph.step("sync", waits=lw, inc=self.sLD, inck=16)(
                lambda eng, slot=slot, wsrc=wsrc: eng.dma_start(
                    wdn[slot][:, :].rearrange("p (c j) -> p c j", c=11),
                    wsrc.rearrange("c p j -> p c j")))
            for ocl in range(4):
                co = og * 4 + ocl
                for tt in range(2):
                    b = st["mm_i"] % 4
                    st["mm_i"] += 1
                    pw = [(self.sLD, t_wl), (self.sDV, t_up_last)] + \
                        bankw_list(st, b)

                    def f_dmm(eng, slot=slot, ocl=ocl, tt=tt, b=b):
                        ins = None
                        for ic in range(11):
                            lhsT = wdn[slot][:, ic * 512 + ocl * 128:
                                             ic * 512 + ocl * 128 + 128]
                            rhs = gh[:, ic * TH + tt * 512:
                                     ic * TH + tt * 512 + 512]
                            ins = eng.matmul(pb[b][:, :], lhsT, rhs,
                                             start=(ic == 0), stop=(ic == 10))
                        return ins
                    t_mm = ph.step("tensor", waits=pw, inc=self.sPE)(f_dmm)
                    st["dn_slot_pe"][slot] = t_mm

                    def f_dev(eng, co=co, tt=tt, b=b):
                        sl = slice(co * TH + tt * 512, co * TH + tt * 512 + 512)
                        return eng.tensor_add(acc[:, sl], acc[:, sl],
                                              pb[b][:, :])
                    t_dev = ph.step("vector", waits=[(self.sPE, t_mm)],
                                    inc=self.sDV)(f_dev)
                    st["bank"][b] = (self.sDV, t_dev)

    def p6_attn(self, SRC, DST):
        nc = self.nc
        ctx = ExitStack()
        sb = lambda n, shp, dt=F32: ctx.enter_context(nc.sbuf_tensor(n, shp, dt))
        TH2 = 1024
        union = sb("a_un", [P, AH * T], BF16)
        kfm = sb("a_kf", [P, 4 * T], BF16)
        vtm = sb("a_vt", [P, 16 * 512], BF16)
        tck = sb("a_ck", [P, T], BF16)
        tsk = sb("a_sk", [P, T], BF16)
        rsw = sb("a_rw", [P, P], BF16)
        wqb = [sb(f"a_wq{i}", [P, DC * 128], BF16) for i in range(2)]
        wkv = [sb(f"a_wk{i}", [P, DC * 256], BF16) for i in range(2)]
        t1 = sb("a_t1", [P, 512], BF16)
        rows = sb("a_ro", [1, 3 * 512])
        mbc = sb("a_mbc", [P, 1024])
        sqs = sb("a_sqs", [P, 2 * 512], BF16)
        pb = [ctx.enter_context(nc.psum_tensor(f"a_b{i}", [P, 512], F32))
              for i in range(8)]
        ctxB = ExitStack()
        xln = ctxB.enter_context(nc.sbuf_tensor("a_xln", [P, DC * TH2], BF16))
        xt = ctxB.enter_context(nc.sbuf_tensor("a_xt", [P, DC * 256], F32))
        ph = self.phase("p6")
        st = {"bank": [None] * 8, "mrow_pe": 0, "xln_pe": 0, "cast_dv": 0,
              "wq_pe": [0, 0], "wkv_pe": [0, 0], "xt_dv": 0}
        m_row, v_row, l_row = (rows[0:1, 0:512], rows[0:1, 512:1024],
                               rows[0:1, 1024:1536])

        def bankw(b):
            g = st["bank"][b]
            return [g] if g is not None else []

        for dst, dsrc in [(tck, self.COS_K), (tsk, self.SIN_K),
                          (rsw, self.RSW)]:
            ph.step("sync", waits=[], inc=self.sLD, inck=16)(
                lambda eng, dst=dst, dsrc=dsrc: eng.dma_start(dst[:, :],
                                                              dsrc[:, :]))
        t_tbl = self.sLD.v

        q_ld_i = 0
        kv_ld_i = 0
        pp_i = 0
        sw_i = 0
        for th in range(2):
            t0h = th * TH2

            for ti in range(2):
                ts_g = t0h + ti * 512
                for hf in range(2):
                    xw = [(self.sDV, st["xt_dv"])] if st["xt_dv"] else []
                    t_ldx = ph.step("sync", waits=xw, inc=self.sLD, inck=16)(
                        lambda eng, ts_g=ts_g, hf=hf: eng.dma_start(
                            xt[:, :].rearrange("p (c j) -> p c j", c=DC),
                            SRC[:, :, ts_g + hf * 256:ts_g + hf * 256 + 256]
                            .rearrange("c p j -> p c j")))
                    cw = [(self.sLD, t_ldx)]
                    if st["xln_pe"]:
                        cw.append((self.sPE, st["xln_pe"]))

                    def f_cast(eng, ti=ti, hf=hf):
                        ins = None
                        for c in range(DC):
                            o = c * TH2 + ti * 512 + hf * 256
                            ins = eng.tensor_copy(xln[:, o:o + 256],
                                                  xt[:, c * 256:(c + 1) * 256])
                        return ins
                    t_cast = ph.step("vector", waits=cw, inc=self.sDV)(f_cast)
                    st["xt_dv"] = t_cast

                def f_sx(eng, ti=ti):
                    ins = None
                    for c in range(DC):
                        ins = eng.matmul(
                            pb[0][0:1, :], self.c_ones_col_bf[:, :],
                            xln[:, c * TH2 + ti * 512:c * TH2 + ti * 512 + 512],
                            start=(c == 0), stop=(c == DC - 1))
                    return ins
                t_sx = ph.step("tensor", waits=[(self.sDV, t_cast)] + bankw(0),
                               inc=self.sPE)(f_sx)

                t_sqc, t_mmq = {}, {}
                for c in range(DC):
                    qw = [(self.sDV, t_cast)]
                    if c >= 2:
                        qw.append((self.sPE, t_mmq[c - 2]))

                    def f_sq(eng, c=c, ti=ti):
                        sl = slice((c % 2) * 512, (c % 2) * 512 + 512)
                        xs = xln[:, c * TH2 + ti * 512:c * TH2 + ti * 512 + 512]
                        return eng.tensor_mul(sqs[:, sl], xs, xs)
                    t_sqc[c] = ph.step("vector", waits=qw, inc=self.sDV)(f_sq)

                    def f_mq(eng, c=c):
                        sl = slice((c % 2) * 512, (c % 2) * 512 + 512)
                        return eng.matmul(pb[1][0:1, :], self.c_ones_col_bf[:, :],
                                          sqs[:, sl], start=(c == 0),
                                          stop=(c == DC - 1))
                    t_mmq[c] = ph.step(
                        "tensor",
                        waits=[(self.sDV, t_sqc[c])] + (bankw(1) if c == 0 else []),
                        inc=self.sPE)(f_mq)

                rww = [(self.sPE, t_mmq[DC - 1]), (self.sPE, t_sx)]
                if st["mrow_pe"]:
                    rww.append((self.sPE, st["mrow_pe"]))

                def f_rows(eng):
                    eng.tensor_scalar(m_row, pb[0][0:1, :], 1.0 / 2048, None,
                                      ALU.mult)
                    eng.tensor_mul(v_row, m_row, m_row)
                    return eng.scalar_tensor_tensor(
                        v_row, pb[1][0:1, :], 1.0 / 2048, v_row,
                        ALU.mult, ALU.subtract)
                t_rows = ph.step("vector", waits=rww, inc=self.sDV)(f_rows)
                st["bank"][0] = st["bank"][1] = (self.sDV, t_rows)

                def f_rstd(eng):
                    eng.activation(l_row, v_row, AF.Ln, bias=self.c_eps[0:1, 0:1])
                    return eng.activation(l_row, l_row, AF.Exp, scale=-0.5)
                t_rstd = ph.step("scalar", waits=[(self.sDV, t_rows)],
                                 inc=self.sAC)(f_rstd)

                def f_bc(eng):
                    eng.matmul(pb[2][:, :], self.c_ones_row[:], m_row,
                               start=True, stop=True)
                    return eng.matmul(pb[3][:, :], self.c_ones_row[:], l_row,
                                      start=True, stop=True)
                t_bc = ph.step("tensor",
                               waits=[(self.sAC, t_rstd)] + bankw(2) + bankw(3),
                               inc=self.sPE)(f_bc)
                st["mrow_pe"] = t_bc

                def f_mc(eng):
                    eng.tensor_copy(mbc[:, 0:512], pb[2][:, :])
                    return eng.tensor_copy(mbc[:, 512:1024], pb[3][:, :])
                t_mc = ph.step("vector", waits=[(self.sPE, t_bc)],
                               inc=self.sDV)(f_mc)
                st["bank"][2] = st["bank"][3] = (self.sDV, t_mc)

                def f_norm(eng, ti=ti):
                    for c in range(DC):
                        sl = slice(c * TH2 + ti * 512, c * TH2 + ti * 512 + 512)
                        eng.tensor_sub(xln[:, sl], xln[:, sl], mbc[:, 0:512])
                    ins = None
                    for c in range(DC):
                        sl = slice(c * TH2 + ti * 512, c * TH2 + ti * 512 + 512)
                        ins = eng.tensor_mul(xln[:, sl], xln[:, sl],
                                             mbc[:, 512:1024])
                    return ins
                t_norm = ph.step("vector", waits=[], inc=self.sDV)(f_norm)
            t_xln = t_norm

            def rope_tile(dst_sl, tc, ts, ts_g, t_evac, swb):
                t_sw = ph.step("tensor",
                               waits=[(self.sDV, t_evac), (self.sLD, t_tbl)] +
                               bankw(swb), inc=self.sPE)(
                    lambda eng: eng.matmul(pb[swb][:, :], rsw[:, :],
                                           dst_sl, start=True, stop=True))

                def f_rope(eng):
                    eng.tensor_mul(t1[:, :], pb[swb][:, :],
                                   ts[:, ts_g:ts_g + 512])
                    eng.tensor_mul(dst_sl, dst_sl, tc[:, ts_g:ts_g + 512])
                    return eng.tensor_add(dst_sl, dst_sl, t1[:, :])
                t_r = ph.step("vector", waits=[(self.sPE, t_sw)],
                              inc=self.sDV)(f_rope)
                st["bank"][swb] = (self.sDV, t_r)
                return t_r

            for grp in range(2):
                slot = kv_ld_i % 2
                kv_ld_i += 1
                lw = ([(self.sPE, st["wkv_pe"][slot])]
                      if st["wkv_pe"][slot] else [])
                t_wl = ph.step("sync", waits=lw, inc=self.sLD, inck=16)(
                    lambda eng, slot=slot, grp=grp: eng.dma_start(
                        wkv[slot][:, :].rearrange("p (c j) -> p c j", c=DC),
                        self.WK[:, :, grp * 256:grp * 256 + 256]
                        .rearrange("c p j -> p c j")))
                for kvl in range(2):
                    kv = grp * 2 + kvl
                    for tt in range(2):
                        ts_g = t0h + tt * 512
                        ppx = pp_i % 3
                        pp_i += 1

                        def f_kmm(eng, slot=slot, kvl=kvl, tt=tt, ppx=ppx):
                            ins = None
                            for kc in range(DC):
                                ins = eng.matmul(
                                    pb[ppx][:, :],
                                    wkv[slot][:, kc * 256 + kvl * 128:
                                              kc * 256 + kvl * 128 + 128],
                                    xln[:, kc * TH2 + tt * 512:
                                        kc * TH2 + tt * 512 + 512],
                                    start=(kc == 0), stop=(kc == DC - 1))
                            return ins
                        t_mm = ph.step(
                            "tensor",
                            waits=[(self.sLD, t_wl), (self.sDV, t_xln)] +
                            bankw(ppx), inc=self.sPE)(f_kmm)
                        st["wkv_pe"][slot] = t_mm
                        dst_sl = kfm[:, kv * T + ts_g:kv * T + ts_g + 512]
                        t_ev = ph.step("vector", waits=[(self.sPE, t_mm)],
                                       inc=self.sDV)(
                            lambda eng, dst_sl=dst_sl, ppx=ppx:
                            eng.tensor_copy(dst_sl, pb[ppx][:, :]))
                        st["bank"][ppx] = (self.sDV, t_ev)
                        rope_tile(dst_sl, tck, tsk, ts_g, t_ev, 4 + sw_i % 2)
                        sw_i += 1

            for grp in range(2):
                slot = kv_ld_i % 2
                kv_ld_i += 1
                lw = ([(self.sPE, st["wkv_pe"][slot])]
                      if st["wkv_pe"][slot] else [])
                t_wl = ph.step("sync", waits=lw, inc=self.sLD, inck=16)(
                    lambda eng, slot=slot, grp=grp: eng.dma_start(
                        wkv[slot][:, :].rearrange("p (c j) -> p c j", c=DC),
                        self.WV[:, :, grp * 256:grp * 256 + 256]
                        .rearrange("c p j -> p c j")))
                for tl in range(8):
                    ppx = pp_i % 3
                    pp_i += 1

                    def f_vmm(eng, slot=slot, tl=tl, ppx=ppx):
                        ins = None
                        for kc in range(DC):
                            ins = eng.matmul(
                                pb[ppx][0:128, 0:256],
                                xln[:, kc * TH2 + tl * 128:
                                    kc * TH2 + tl * 128 + 128],
                                wkv[slot][:, kc * 256:kc * 256 + 256],
                                start=(kc == 0), stop=(kc == DC - 1))
                        return ins
                    t_mm = ph.step(
                        "tensor",
                        waits=[(self.sLD, t_wl), (self.sDV, t_xln)] + bankw(ppx),
                        inc=self.sPE)(f_vmm)
                    st["wkv_pe"][slot] = t_mm
                    vsl = vtm[:, (8 * th + tl) * 512 + grp * 256:
                              (8 * th + tl) * 512 + grp * 256 + 256]
                    t_ev = ph.step("vector", waits=[(self.sPE, t_mm)],
                                   inc=self.sDV)(
                        lambda eng, vsl=vsl, ppx=ppx:
                        eng.tensor_copy(vsl, pb[ppx][0:128, 0:256]))
                    st["bank"][ppx] = (self.sDV, t_ev)

            for h in range(AH):
                slot = q_ld_i % 2
                q_ld_i += 1
                lw = ([(self.sPE, st["wq_pe"][slot])]
                      if st["wq_pe"][slot] else [])
                t_wl = ph.step("sync", waits=lw, inc=self.sLD, inck=16)(
                    lambda eng, slot=slot, h=h: eng.dma_start(
                        wqb[slot][:, :].rearrange("p (c j) -> p c j", c=DC),
                        self.WQ[:, :, h * 128:h * 128 + 128]
                        .rearrange("c p j -> p c j")))
                for tt in range(2):
                    ts_g = t0h + tt * 512
                    ppx = pp_i % 3
                    pp_i += 1

                    def f_qmm(eng, slot=slot, tt=tt, ppx=ppx):
                        ins = None
                        for kc in range(DC):
                            ins = eng.matmul(
                                pb[ppx][:, :],
                                wqb[slot][:, kc * 128:kc * 128 + 128],
                                xln[:, kc * TH2 + tt * 512:
                                    kc * TH2 + tt * 512 + 512],
                                start=(kc == 0), stop=(kc == DC - 1))
                        return ins
                    t_mm = ph.step(
                        "tensor",
                        waits=[(self.sLD, t_wl), (self.sDV, t_xln)] + bankw(ppx),
                        inc=self.sPE)(f_qmm)
                    st["wq_pe"][slot] = t_mm
                    dst_sl = union[:, h * T + ts_g:h * T + ts_g + 512]
                    t_ev = ph.step("vector", waits=[(self.sPE, t_mm)],
                                   inc=self.sDV)(
                        lambda eng, dst_sl=dst_sl, ppx=ppx:
                        eng.tensor_scalar_mul(dst_sl, pb[ppx][:, :],
                                              1.0 / float(np.sqrt(AHD))))
                    st["bank"][ppx] = (self.sDV, t_ev)
                    rope_tile(dst_sl, tck, tsk, ts_g, t_ev, 4 + sw_i % 2)
                    sw_i += 1
            st["xln_pe"] = self.sPE.v

        ctxB.close()
        xrb = [sb(f"a_xr{i}", [P, T]) for i in range(2)]
        stg = [sb(f"a_sg{i}", [P, T]) for i in range(2)]
        probs = sb("a_pb", [P, 4 * 512], BF16)
        rbs = sb("a_rb", [P, 512])
        qrow = [sb(f"a_qr{i}", [1, 512], BF16) for i in range(2)]
        t_bdone = self.sDV.v

        qn_pe = [0, 0]
        for h in range(AH):
            kv = h // 4
            qw_ = [(self.sPE, qn_pe[h % 2])] if qn_pe[h % 2] else []
            t_qn = ph.step("sync", waits=qw_, inc=self.sLD, inck=16)(
                lambda eng, h=h: eng.dma_start(qrow[h % 2][:, :],
                                               self.QNEG[h:h + 1, :]))
            for qt in range(4):
                q0 = qt * 512
                nkt = 4 * qt + 4
                t_exp, t_spv = {}, {}
                for kt in range(nkt):
                    jc = kt - 4 * qt + 12
                    A = pb[kt % 4]
                    scw = [(self.sDV, t_bdone)] if (h == 0 and qt == 0) else []
                    scw += ([(self.sAC, t_exp[kt - 4])] if kt >= 4
                            else bankw(kt % 4))

                    def f_sc(eng, h=h, kv=kv, kt=kt, q0=q0, A=A):
                        eng.matmul(A[:, :],
                                   kfm[:, kv * T + kt * 128:kv * T + kt * 128 + 128],
                                   union[:, h * T + q0:h * T + q0 + 512],
                                   start=True, stop=False)
                        return eng.matmul(A[:, :], self.c_ones_row_bf[:, :],
                                          qrow[h % 2][0:1, :], start=False,
                                          stop=True)
                    if kt == 0 and qt == 0:
                        scw.append((self.sLD, t_qn))
                    t_sc = ph.step("tensor", waits=scw, inc=self.sPE)(f_sc)

                    t_msk = None
                    if kt >= 4 * qt:
                        j0 = kt - 4 * qt

                        def f_msk(eng, j0=j0, A=A):
                            ins = None
                            if j0 > 0:
                                ins = eng.memset(A[:, 0:128 * j0], NEG_BIG)
                            sl = slice(j0 * 128, j0 * 128 + 128)
                            return eng.tensor_add(A[:, sl], A[:, sl],
                                                  self.c_tri[:, :])
                        t_msk = ph.step("vector", waits=[(self.sPE, t_sc)],
                                        inc=self.sDV)(f_msk)

                    ew = [(self.sPE, t_sc)]
                    if t_msk is not None:
                        ew.append((self.sDV, t_msk))
                    if kt >= 4:
                        ew.append((self.sPE, t_spv[kt - 4]))
                    elif st.get("spv_prev"):
                        ew.append((self.sPE, st["spv_prev"]))

                    def f_exp(eng, kt=kt, h=h, jc=jc, A=A):
                        return eng.activation(
                            probs[:, (kt % 4) * 512:(kt % 4) * 512 + 512],
                            A[:, :], AF.Exp,
                            bias=self.c_alibi[:, h * 16 + jc:h * 16 + jc + 1])
                    t_exp[kt] = ph.step("scalar", waits=ew, inc=self.sAC)(f_exp)

                    def emit_spv(kt, kv=kv, nkt=nkt):
                        sw = [(self.sAC, t_exp[kt])]
                        if kt == 0:
                            sw += bankw(6) + bankw(7)

                        def f_spv(eng, kt=kt, kv=kv, nkt=nkt):
                            psl = probs[:, (kt % 4) * 512:(kt % 4) * 512 + 512]
                            eng.matmul(pb[6][0:1, :], self.c_ones_col_bf[:, :],
                                       psl, start=(kt == 0), stop=(kt == nkt - 1))
                            return eng.matmul(
                                pb[7][:, :],
                                vtm[:, kt * 512 + kv * 128:kt * 512 + kv * 128 + 128],
                                psl, start=(kt == 0), stop=(kt == nkt - 1))
                        t_spv[kt] = ph.step("tensor", waits=sw, inc=self.sPE)(f_spv)
                    if kt >= 1:
                        emit_spv(kt - 1)
                    if kt == nkt - 1:
                        emit_spv(kt)

                def f_r(eng):
                    eng.activation(l_row, pb[6][0:1, :], AF.Ln)
                    return eng.activation(l_row, l_row, AF.Exp, scale=-1.0)
                t_r = ph.step("scalar", waits=[(self.sPE, t_spv[nkt - 1])],
                              inc=self.sAC)(f_r)
                st["bank"][6] = (self.sAC, t_r)
                for kb in range(min(4, nkt)):
                    st["bank"][(nkt - 1 - kb) % 4] = (
                        self.sAC, t_exp[nkt - 1 - kb])
                st["spv_prev"] = t_spv[nkt - 1]

                t_rbc = ph.step("tensor", waits=[(self.sAC, t_r)] + bankw(5),
                                inc=self.sPE)(
                    lambda eng: eng.matmul(pb[5][:, :], self.c_ones_row[:],
                                           l_row, start=True, stop=True))

                def f_fin(eng, h=h, q0=q0):
                    eng.tensor_copy(rbs[:, :], pb[5][:, :])
                    return eng.tensor_mul(union[:, h * T + q0:h * T + q0 + 512],
                                          rbs[:, :], pb[7][:, :])
                t_fin = ph.step("vector", waits=[(self.sPE, t_rbc)],
                                inc=self.sDV)(f_fin)
                st["bank"][5] = st["bank"][7] = (self.sDV, t_fin)
            qn_pe[h % 2] = t_spv[nkt - 1]
        t_cdone = self.sDV.v

        t_wl, t_mm, t_ev, t_st, t_xr = {}, {}, {}, {}, {}
        for oc in range(16):
            bank0 = (oc % 2) * 4
            slot = q_ld_i % 2
            q_ld_i += 1
            lw = [(self.sPE, st["wq_pe"][slot])] if st["wq_pe"][slot] else []
            t_wl[oc] = ph.step("sync", waits=lw, inc=self.sLD, inck=16)(
                lambda eng, oc=oc, slot=slot: eng.dma_start(
                    wqb[slot][:, :].rearrange("p (c j) -> p c j", c=DC),
                    self.WO[:, :, oc * 128:oc * 128 + 128]
                    .rearrange("c p j -> p c j")))
            xw = [(self.sDV, t_ev[oc - 2])] if oc >= 2 else []
            t_xr[oc] = ph.step("sync", waits=xw, inc=self.sLD, inck=16)(
                lambda eng, oc=oc: eng.dma_start(xrb[oc % 2][:, :],
                                                 SRC[oc, :, :]))
            mw = [(self.sLD, t_wl[oc])]
            if oc == 0:
                mw.append((self.sDV, t_cdone))
            if oc >= 2:
                mw.append((self.sDV, t_ev[oc - 2]))
            else:
                for bx in range(bank0, bank0 + 4):
                    mw += bankw(bx)

            def f_mm(eng, oc=oc, bank0=bank0, slot=slot):
                ins = None
                for kc in range(DC):
                    lhsT = wqb[slot][:, kc * 128:(kc + 1) * 128]
                    for tt in range(4):
                        ins = eng.matmul(
                            pb[bank0 + tt][:, :], lhsT,
                            union[:, kc * T + tt * 512:kc * T + tt * 512 + 512],
                            start=(kc == 0), stop=(kc == DC - 1))
                return ins
            t_mm[oc] = ph.step("tensor", waits=mw, inc=self.sPE)(f_mm)
            st["wq_pe"][slot] = t_mm[oc]

            evw = [(self.sPE, t_mm[oc]), (self.sLD, t_xr[oc])]
            if oc >= 2:
                evw.append((self.sST, t_st[oc - 2]))

            def f_ev(eng, oc=oc, bank0=bank0):
                ins = None
                for tt in range(4):
                    ins = eng.tensor_add(stg[oc % 2][:, tt * 512:(tt + 1) * 512],
                                         pb[bank0 + tt][:, :],
                                         xrb[oc % 2][:, tt * 512:(tt + 1) * 512])
                return ins
            t_ev[oc] = ph.step("vector", waits=evw, inc=self.sDV)(f_ev)
            t_st[oc] = ph.step("scalar", waits=[(self.sDV, t_ev[oc])],
                               inc=self.sST, inck=16)(
                lambda eng, oc=oc: eng.dma_start(DST[oc, :, :],
                                                 stg[oc % 2][:, :]))

        self.drain(ph)
        ph.flush()
        if "adbg" in self.dbg:
            self.dump_sbuf("dbg_aun", union[:, :], [P, AH * T], BF16)
            self.dump_sbuf("dbg_akf", kfm[:, :], [P, 4 * T], BF16)
            self.dump_sbuf("dbg_avt", vtm[:, :], [P, 16 * 512], BF16)
        ctx.close()

    def dump_sbuf(self, name, ap, shp, dt=F32):
        out = self.nc.dram_tensor(name, list(shp), dt, kind="ExternalOutput")
        ph = self.phase(f"dbg_{name}")
        self.sGP.v += 16
        tgt = self.sGP.v

        def f(eng):
            eng.dma_start(out[:, :], ap).then_inc(self.sGP.h, 16)
            eng.wait_ge(self.sGP.h, tgt)
        ph.steps["gpsimd"].append(f)
        ph.flush()

    def dump_dram(self, name, src, shp, dt=F32):
        nc = self.nc
        out = nc.dram_tensor(name, list(shp), dt, kind="ExternalOutput")
        n0 = int(np.prod(shp[:-1]))
        with nc.sbuf_tensor(f"dbg_{name}", [P, shp[-1]], dt) as tile:
            ph = self.phase(f"dbgd_{name}")

            def f(eng):
                v0 = self.sGP.v
                k = 0
                if len(shp) == 3:
                    src_f = src[:, :, :].rearrange("a p t -> (a p) t")
                    out_f = out[:, :, :].rearrange("a p t -> (a p) t")
                else:
                    src_f = src[:, :]
                    out_f = out[:, :]
                for r0 in range(0, n0, P):
                    rr = min(P, n0 - r0)
                    eng.dma_start(tile[0:rr, :], src_f[r0:r0 + rr, :]).then_inc(self.sGP.h, 16)
                    eng.wait_ge(self.sGP.h, v0 + 32 * k + 16)
                    eng.dma_start(out_f[r0:r0 + rr, :], tile[0:rr, :]).then_inc(self.sGP.h, 16)
                    eng.wait_ge(self.sGP.h, v0 + 32 * k + 32)
                    k += 1
                self.sGP.v = v0 + 32 * k
            ph.steps["gpsimd"].append(f)
            ph.flush()

    def finalize_noop_out(self):
        nc = self.nc
        with nc.sbuf_tensor("fin_tile", [P, T], F32) as tile:
            ph = self.phase("fin")

            def f(eng):
                v0 = self.sGP.v
                eng.memset(tile[:, :], 0.0)
                for c in range(DC):
                    eng.dma_start(self.OUT[c, :, :], tile[:, :]).then_inc(self.sGP.h, 16)
                eng.wait_ge(self.sGP.h, v0 + 16 * DC)
                self.sGP.v = v0 + 16 * DC
            ph.steps["gpsimd"].append(f)
            ph.flush()


def build_p12(dbg=(), nphase=2):
    b = Builder(dbg)
    b.p0_consts()
    with ExitStack() as mctx:
        dt_all = mctx.enter_context(b.nc.sbuf_tensor("dt_all", [HEADS, T], F32))
        with ExitStack() as ictx:
            xlnm = ictx.enter_context(b.nc.sbuf_tensor("xlnm", [P, DC * T], BF16))
            b.p1_mamba_ln(xlnm, ictx)
            if "xlnm" in dbg:
                b.dump_sbuf("dbg_xlnm", xlnm[:, :], [P, DC * T], BF16)
            b.p2_in_proj(xlnm, dt_all, ictx)
            if "dt" in dbg:
                b.dump_sbuf("dbg_dt", dt_all[:, :], [HEADS, T], F32)
        if nphase >= 3:
            b.p3_ssd(dt_all)
        if nphase >= 4:
            b.p4_out_proj()
    if nphase >= 5:
        b.p_ffn(b.X1, b.X2, "f1")
    if nphase >= 6:
        b.p6_attn(b.X2, b.X3)
    if nphase >= 7:
        b.p_ffn(b.X3, b.OUT, "f2")
    if "xr" in dbg:
        b.dump_dram("dbg_xr", b.XR, [DC, P, T])
    if "proj" in dbg:
        b.dump_dram("dbg_proj", b.PROJD, [66, P, T], BF16)
    if "ynorm" in dbg:
        b.dump_dram("dbg_ynorm", b.YNORM, [32, P, T], BF16)
    if "x1" in dbg:
        b.dump_dram("dbg_x1", b.X1, [DC, P, T])
    if "x2" in dbg:
        b.dump_dram("dbg_x2", b.X2, [DC, P, T])
    if "x3" in dbg:
        b.dump_dram("dbg_x3", b.X3, [DC, P, T])
    if nphase >= 7:
        b.g.close()
        return b
    b.finalize_noop_out()
    b.g.close()
    return b


def kernel(**inputs):
    consts = host_constants()
    b = build_p12(nphase=7)
    in_maps = [prep_core_inputs(c, inputs, consts) for c in range(NCORES)]
    res = run_bass_kernel_spmd(b.nc, in_maps, core_ids=list(range(NCORES)))
    outs = []
    for c in range(NCORES):
        o = np.asarray(res.results[c]["out"], np.float32).reshape(T, T)
        outs.append(np.ascontiguousarray(o.T))
    return np.stack(outs)
